# version 07
# speedup vs baseline: 1.3369x; 1.0355x over previous
import os as _os
import numpy as np
import ml_dtypes
import contextlib
import concourse.bass as bass
import concourse.mybir as mybir
from concourse.bass_utils import run_bass_kernel_spmd
from concourse.aot_env import allow_ici_all_to_all

F32 = mybir.dt.float32
BF16 = mybir.dt.bfloat16
AF = mybir.ActivationFunctionType
OP = mybir.AluOpType
AX = mybir.AxisListType
bf16 = ml_dtypes.bfloat16

NCORE = 8
T = 8192
H = 4096
D = 8192
NST = 16
KCV = 4
RDT = 256
TLOC = 1024
G = 256
NGRP = 4
RG = [list(range(NCORE))]

def groups_of(c):
    return [c, 15 - c, 16 + c, 31 - c]

def owner(k):
    if k < 8: return k
    if k < 16: return 15 - k
    if k < 24: return k - 16
    return 31 - k

def gidx(k):
    return 0 if k < 8 else (1 if k < 16 else (2 if k < 24 else 3))

def alibi_slopes_py(n=32):
    import math
    cp2 = 2 ** int(math.floor(math.log2(n)))
    base = 2.0 ** (-(2.0 ** (-(math.log2(cp2) - 3))))
    slopes = [base ** i for i in range(1, cp2 + 1)]
    if cp2 != n:
        extra_base = 2.0 ** (-(2.0 ** (-(math.log2(2 * cp2) - 3))))
        rem = min(cp2, n - cp2)
        slopes += [extra_base ** i for i in range(1, 1 + 2 * rem, 2)]
    return slopes


class Ctx:
    def __init__(self, nc, es):
        self.nc = nc
        self.es = es
        self.esem = {}
        self.ecnt = {}
        self.dsem = {}
        self.dcnt = {}
        self.waited = {}

    def bump(self, eng_name, instr, n=1):
        if eng_name not in self.esem:
            self.esem[eng_name] = self.es.enter_context(self.nc.semaphore(f"se_{eng_name}"))
            self.ecnt[eng_name] = 0
        self.ecnt[eng_name] += n
        instr.then_inc(self.esem[eng_name], n)
        return ("E", eng_name, self.ecnt[eng_name])

    def dma(self, stream, instr):
        if stream not in self.dsem:
            self.dsem[stream] = self.es.enter_context(self.nc.semaphore(f"sd_{stream}"))
            self.dcnt[stream] = 0
        self.dcnt[stream] += 16
        instr.then_inc(self.dsem[stream], 16)
        return ("D", stream, self.dcnt[stream])

    def wait(self, eng_obj, waiter_name, milestone):
        kind, key, val = milestone
        if kind == "E" and key == waiter_name:
            return
        wkey = (waiter_name, kind, key)
        if self.waited.get(wkey, -1) >= val:
            return
        sem = self.esem[key] if kind == "E" else self.dsem[key]
        eng_obj.wait_ge(sem, val)
        self.waited[wkey] = val

    def dma_total(self, stream):
        return ("D", stream, self.dcnt.get(stream, 0))

    def self_edge(self, eng_name, eng_obj, instr):
        m = self.bump(eng_name, instr)
        eng_obj.wait_ge(self.esem[eng_name], m[2])
        return m

    def join(self, block):
        snap_e = dict(self.ecnt)
        snap_d = dict(self.dcnt)
        for engname, attr in (("pe", "tensor"), ("act", "scalar"), ("dve", "vector"),
                              ("pool", "gpsimd"), ("sp", "sync")):
            def body(eng, engname=engname):
                for k2, v in snap_e.items():
                    if v and k2 != engname:
                        self.wait(eng, engname, ("E", k2, v))
                for k2, v in snap_d.items():
                    if v:
                        self.wait(eng, engname, ("D", k2, v))
            getattr(block, attr)(body)


def build_fused():
    nc = bass.Bass("TRN2", target_bir_lowering=False, debug=False, detect_race_conditions=not _os.environ.get("BASS_NO_RD"))
    x0T = nc.dram_tensor("x0T", [H, TLOC], F32, kind="ExternalInput").ap()
    x0Tpre = nc.dram_tensor("x0Tpre", [H, 16], F32, kind="ExternalInput").ap()
    W_inT = nc.dram_tensor("W_inT", [H, 2 * D], F32, kind="ExternalInput").ap()
    WxT = nc.dram_tensor("WxT", [D, 288], F32, kind="ExternalInput").ap()
    conv_w = nc.dram_tensor("conv_w", [D, KCV], F32, kind="ExternalInput").ap()
    conv_b2 = nc.dram_tensor("conv_b2", [128, 64], F32, kind="ExternalInput").ap()
    dtlnw2 = nc.dram_tensor("dtlnw2", [128, 2], F32, kind="ExternalInput").ap()
    b_ln_w = nc.dram_tensor("b_ln_w", [NST], F32, kind="ExternalInput").ap()
    c_ln_w = nc.dram_tensor("c_ln_w", [NST], F32, kind="ExternalInput").ap()
    wdtT_c = nc.dram_tensor("wdtT_c", [RDT, 1024], F32, kind="ExternalInput").ap()
    A_c = nc.dram_tensor("A_c", [128, 8 * NST], F32, kind="ExternalInput").ap()
    dtb_c = nc.dram_tensor("dtb_c", [128, 8], F32, kind="ExternalInput").ap()
    Dp_c = nc.dram_tensor("Dp_c", [128, 8], F32, kind="ExternalInput").ap()
    esel = nc.dram_tensor("esel", [NST, NST * 128], BF16, kind="ExternalInput").ap()
    W_outT = nc.dram_tensor("W_outT", [D, H], F32, kind="ExternalInput").ap()
    base_WT = nc.dram_tensor("base_WT", [H, H], F32, kind="ExternalInput").ap()
    router_WT = nc.dram_tensor("router_WT", [H, 8], F32, kind="ExternalInput").ap()
    loraA2 = nc.dram_tensor("loraA2", [H, 128], F32, kind="ExternalInput").ap()
    loraB2 = nc.dram_tensor("loraB2", [128, H], F32, kind="ExternalInput").ap()
    rep16 = nc.dram_tensor("rep16", [8, 128], F32, kind="ExternalInput").ap()
    w_qkv = nc.dram_tensor("w_qkv", [H, 4352], F32, kind="ExternalInput").ap()
    w_c = nc.dram_tensor("w_c", [H, H], F32, kind="ExternalInput").ap()
    rwT1 = nc.dram_tensor("rwT1", [H, 64], F32, kind="ExternalInput").ap()
    rb1 = nc.dram_tensor("rb1", [64, 1], F32, kind="ExternalInput").ap()
    ebias = nc.dram_tensor("ebias", [64, H], F32, kind="ExternalInput").ap()
    ident = nc.dram_tensor("ident", [128, 128], F32, kind="ExternalInput").ap()
    identb = nc.dram_tensor("identb", [128, 128], BF16, kind="ExternalInput").ap()
    ones_col = nc.dram_tensor("ones_col", [128, 1], F32, kind="ExternalInput").ap()
    ones_row = nc.dram_tensor("ones_row", [1, 128], F32, kind="ExternalInput").ap()
    onesb = nc.dram_tensor("onesb", [128, 1], BF16, kind="ExternalInput").ap()
    onesrb = nc.dram_tensor("onesrb", [1, 128], BF16, kind="ExternalInput").ap()
    kcol = nc.dram_tensor("kcol", [128, 64], F32, kind="ExternalInput").ap()
    maskc = nc.dram_tensor("maskc", [128, 4, 512], F32, kind="ExternalInput").ap()
    slopes_c = nc.dram_tensor("slopes_c", [128, 4], F32, kind="ExternalInput").ap()
    qrefc = nc.dram_tensor("qrefc", [128, 4, 512], F32, kind="ExternalInput").ap()
    ones128b = nc.dram_tensor("ones128b", [128, 128], BF16, kind="ExternalInput").ap()
    x4_out = nc.dram_tensor("x4_out", [H, TLOC], F32, kind="ExternalOutput").ap()
    v_scr = nc.dram_tensor("v_scr", [D, TLOC], BF16, kind="Internal").ap()
    gate_scr = nc.dram_tensor("gate_scr", [D, TLOC], BF16, kind="Internal").ap()
    dtn_scr = nc.dram_tensor("dtn_scr", [288, TLOC], BF16, kind="Internal").ap()
    v_a2a = nc.dram_tensor("v_a2a", [D, TLOC], BF16, kind="Internal").ap()
    dtn_ag = nc.dram_tensor("dtn_ag", [NCORE * 288, TLOC], BF16, kind="Internal").ap()
    y_in = nc.dram_tensor("y_in", [D, TLOC], BF16, kind="Internal").ap()
    y_out = nc.dram_tensor("y_out2", [D, TLOC], BF16, kind="Internal").ap()
    x1_scr = nc.dram_tensor("x1_scr", [H, TLOC], F32, kind="Internal").ap()
    x2_scr = nc.dram_tensor("x2_scr", [H, TLOC], F32, kind="Internal").ap()
    x3_scr = nc.dram_tensor("x3_scr", [H, TLOC], F32, kind="Internal").ap()
    r_scr = nc.dram_tensor("r_scr", [H, TLOC], F32, kind="Internal").ap()
    q_in = nc.dram_tensor("q_in", [H, TLOC], BF16, kind="Internal").ap()
    q_out = nc.dram_tensor("q_out", [H, TLOC], BF16, kind="Internal").ap()
    kv_in = nc.dram_tensor("kv_in", [256, TLOC], BF16, kind="Internal").ap()
    kv_out = nc.dram_tensor("kv_out", [NCORE * 256, TLOC], BF16, kind="Internal").ap()
    at_in = nc.dram_tensor("at_in", [H, TLOC], BF16, kind="Internal").ap()
    at_out = nc.dram_tensor("at_out", [H, TLOC], BF16, kind="Internal").ap()

    es = contextlib.ExitStack()
    with es:
        onc = es.enter_context(nc.sbuf_tensor("onc", [128, 1], F32))
        onr = es.enter_context(nc.sbuf_tensor("onr", [1, 128], F32))
        onb = es.enter_context(nc.sbuf_tensor("onb", [128, 1], BF16))
        onrb = es.enter_context(nc.sbuf_tensor("onrb", [1, 128], BF16))
        idt = es.enter_context(nc.sbuf_tensor("idt", [128, 128], F32))
        idb = es.enter_context(nc.sbuf_tensor("idb", [128, 128], BF16))
        block = es.enter_context(nc.Block())
        C = Ctx(nc, es)
        M = {}

        @block.sync
        def _(sp):
            C.dma("cst", sp.dma_start(out=onc[:], in_=ones_col[:]))
            C.dma("cst", sp.dma_start(out=onr[:], in_=ones_row[:]))
            C.dma("cst", sp.dma_start(out=onb[:], in_=onesb[:]))
            C.dma("cst", sp.dma_start(out=onrb[:], in_=onesrb[:]))
            C.dma("cst", sp.dma_start(out=idt[:], in_=ident[:]))
            C.dma("cst", sp.dma_start(out=idb[:], in_=identb[:]))
        CST0 = C.dma_total("cst")

        with contextlib.ExitStack() as ps:
            x0bf = ps.enter_context(nc.sbuf_tensor("x0bf", [128, 32, TLOC], BF16))
            x0pre_bf = ps.enter_context(nc.sbuf_tensor("x0pre_bf", [128, 32, 16], BF16))
            wbuf = ps.enter_context(nc.sbuf_tensor("wbuf", [128, 2, 32, 128], BF16))
            wxbuf = ps.enter_context(nc.sbuf_tensor("wxbuf", [128, 64, 288], BF16))
            cwb = ps.enter_context(nc.sbuf_tensor("cwb", [128, 64, KCV], F32))
            cbb = ps.enter_context(nc.sbuf_tensor("cbb", [128, 64], F32))
            lnw = ps.enter_context(nc.sbuf_tensor("lnw", [128, 4], F32))
            u_pad = ps.enter_context(nc.sbuf_tensor("u_pad", [128, NGRP, G + 3], F32))
            vt = ps.enter_context(nc.sbuf_tensor("vt", [128, NGRP, G], F32))
            v_sb = ps.enter_context(nc.sbuf_tensor("v_sb", [128, 2, TLOC], BF16))
            g_sb = ps.enter_context(nc.sbuf_tensor("g_sb", [128, 2, TLOC], BF16))
            vwx = ps.enter_context(nc.sbuf_tensor("vwx", [128, 32, 512], BF16))
            ssm_raw = ps.enter_context(nc.sbuf_tensor("ssm_raw", [128, 2, 512], F32))
            bc_raw = ps.enter_context(nc.sbuf_tensor("bc_raw", [16, 2, 512], F32))
            sq1 = ps.enter_context(nc.sbuf_tensor("sq1", [128, 2, 512], F32))
            sqbc = ps.enter_context(nc.sbuf_tensor("sqbc", [16, 2, 512], F32))
            rrow = ps.enter_context(nc.sbuf_tensor("rrow1", [1, 10, 512], F32))
            dtn_sb = ps.enter_context(nc.sbuf_tensor("dtn_sb", [128, 3, 512], BF16))
            pproj = ps.enter_context(nc.psum_tensor("pproj", [128, 1040], F32))
            pwx1 = ps.enter_context(nc.psum_tensor("pwx1", [128, 512], F32))
            pwx2 = ps.enter_context(nc.psum_tensor("pwx2", [128, 512], F32))
            pwxbc = ps.enter_context(nc.psum_tensor("pwxbc", [64, 512], F32))
            pstat = ps.enter_context(nc.psum_tensor("pstat", [1, 512], F32))
            pbcast = ps.enter_context(nc.psum_tensor("pbcast", [128, 512], F32))

            @block.gpsimd
            def _(gp):
                C.dma("gcst", gp.dma_start(out=x0bf[:], in_=x0T.rearrange("(e p) t -> p e t", p=128)))
                C.dma("gcst", gp.dma_start(out=x0pre_bf[:, :, 0:12], in_=x0Tpre[:, 0:12].rearrange("(e p) t -> p e t", p=128)))
                C.dma("gcst", gp.dma_start(out=wxbuf[:], in_=WxT.rearrange("(f p) r -> p f r", p=128)))
            @block.sync
            def _(sp):
                C.dma("cst", sp.dma_start(out=cwb[:], in_=conv_w.rearrange("(f p) k -> p f k", p=128)))
                C.dma("cst", sp.dma_start(out=cbb[:], in_=conv_b2[:]))
                C.dma("cst", sp.dma_start(out=lnw[:, 0:2], in_=dtlnw2[:]))
                C.dma("cst", sp.dma_start(out=lnw[0:16, 2:3], in_=b_ln_w.rearrange("(n o) -> n o", o=1)))
                C.dma("cst", sp.dma_start(out=lnw[0:16, 3:4], in_=c_ln_w.rearrange("(n o) -> n o", o=1)))
            CST = C.dma_total("cst")
            GCST = C.dma_total("gcst")

            ev_mark = {}
            for f in range(128):
                uhalf = f < 64
                @block.gpsimd
                def _(gp, f=f):
                    if f >= 2:
                        C.wait(gp, "pool", M[f"pe_{f-2}"])
                    M[f"w_{f}"] = C.dma(f"wb{f % 2}", gp.dma_start(
                        out=wbuf[:, f % 2],
                        in_=W_inT[:, 128 * f:128 * (f + 1)].rearrange("(e p) j -> p e j", p=128)))
                @block.tensor
                def _(te, f=f, uhalf=uhalf):
                    C.wait(te, "pe", M[f"w_{f}"])
                    if f == 0:
                        C.wait(te, "pe", CST)
                        C.wait(te, "pe", GCST)
                    if f >= 1:
                        C.wait(te, "pe", ev_mark[f - 1])
                    last = None
                    for e in range(32):
                        lhsT = wbuf[:, f % 2, e]
                        te.ldweights(lhsT)
                        te.matmul(pproj[:, 0:512], lhsT, x0bf[:, e, 0:512], start=(e == 0), stop=(e == 31))
                        last = te.matmul(pproj[:, 512:1024], lhsT, x0bf[:, e, 512:1024], start=(e == 0), stop=(e == 31))
                        if uhalf:
                            last = te.matmul(pproj[:, 1024:1036], lhsT, x0pre_bf[:, e, 0:12], start=(e == 0), stop=(e == 31))
                    M[f"pe_{f}"] = C.bump("pe", last)
                @block.scalar
                def _(sc, f=f, uhalf=uhalf):
                    C.wait(sc, "act", M[f"pe_{f}"])
                    if uhalf:
                        if f >= 1:
                            C.wait(sc, "act", M[f"conv_{f-1}"])
                        last = None
                        for g in range(NGRP):
                            sc.copy(u_pad[:, g, 3:3 + G], pproj[:, G * g:G * (g + 1)])
                            last = sc.copy(u_pad[:, g, 0:3], pproj[:, 1024 + 3 * g:1024 + 3 * g + 3])
                        ev_mark[f] = M[f"ev_{f}"] = C.bump("act", last)
                    else:
                        fb_ = f - 64
                        if f >= 66:
                            C.wait(sc, "act", M[f"gst_{f-2}"])
                        half = fb_ % 2
                        sc.copy(g_sb[:, half, 0:512], pproj[:, 0:512])
                        ev_mark[f] = M[f"gev_{f}"] = C.bump("act", sc.copy(g_sb[:, half, 512:1024], pproj[:, 512:1024]))
                if uhalf:
                    @block.vector
                    def _(dv, f=f):
                        C.wait(dv, "dve", M[f"ev_{f}"])
                        C.wait(dv, "dve", CST)
                        if f >= 1:
                            C.wait(dv, "dve", M[f"vs_{f-1}"])
                        dv.tensor_scalar(vt[:], u_pad[:, :, 0:G], cwb[:, f, 0:1], None, OP.mult)
                        last = None
                        for k in range(1, KCV):
                            last = dv.scalar_tensor_tensor(vt[:], u_pad[:, :, k:k + G], cwb[:, f, k:k + 1], vt[:], OP.mult, OP.add)
                        M[f"conv_{f}"] = C.bump("dve", last)
                    @block.scalar
                    def _(sc, f=f):
                        C.wait(sc, "act", M[f"conv_{f}"])
                        if f >= 2:
                            C.wait(sc, "act", M[f"vst_{f-2}"])
                        M[f"vs_{f}"] = C.bump("act", sc.activation(
                            v_sb[:, f % 2].rearrange("p (g t) -> p g t", g=NGRP), vt[:],
                            AF.Silu, bias=cbb[:, f:f + 1], scale=1.0))
                @block.sync
                def _(sp, f=f, uhalf=uhalf):
                    if uhalf:
                        C.wait(sp, "sp", M[f"vs_{f}"])
                        M[f"vst_{f}"] = C.dma(f"vst{f % 2}", sp.dma_start(out=v_scr[128 * f:128 * (f + 1), :], in_=v_sb[:, f % 2]))
                    else:
                        C.wait(sp, "sp", M[f"gev_{f}"])
                        M[f"gst_{f}"] = C.dma(f"gst{(f - 64) % 2}", sp.dma_start(out=gate_scr[128 * (f - 64):128 * (f - 63), :], in_=g_sb[:, (f - 64) % 2]))

            for t in range(2):
                for fh in range(2):
                    @block.sync
                    def _(sp, t=t, fh=fh):
                        C.wait(sp, "sp", C.dma_total("vst0"))
                        C.wait(sp, "sp", C.dma_total("vst1"))
                        if t > 0 or fh > 0:
                            C.wait(sp, "sp", M[f"wx_mm_{t}_{fh-1}"] if fh > 0 else M[f"wx_mm_{t-1}_1"])
                        M[f"vwx_{t}_{fh}"] = C.dma("vwx", sp.dma_start(
                            out=vwx[:], in_=v_scr[4096 * fh:4096 * (fh + 1), 512 * t:512 * (t + 1)].rearrange("(f p) x -> p f x", p=128)))
                    @block.tensor
                    def _(te, t=t, fh=fh):
                        C.wait(te, "pe", M[f"vwx_{t}_{fh}"])
                        if t == 1 and fh == 0:
                            C.wait(te, "pe", M["nrm_done_0"])
                        last = None
                        for fi in range(32):
                            f = 32 * fh + fi
                            st = (f == 0)
                            sto = (f == 63)
                            te.ldweights(wxbuf[:, f, 0:128])
                            te.matmul(pwx1[:], wxbuf[:, f, 0:128], vwx[:, fi], start=st, stop=sto)
                            te.ldweights(wxbuf[:, f, 128:256])
                            te.matmul(pwx2[:], wxbuf[:, f, 128:256], vwx[:, fi], start=st, stop=sto)
                            te.ldweights(wxbuf[:, f, 256:272])
                            te.matmul(pwxbc[0:16, :], wxbuf[:, f, 256:272], vwx[:, fi], start=st, stop=sto)
                            te.ldweights(wxbuf[:, f, 272:288])
                            last = te.matmul(pwxbc[32:48, :], wxbuf[:, f, 272:288], vwx[:, fi], start=st, stop=sto)
                        M[f"wx_mm_{t}_{fh}"] = C.bump("pe", last)
                M[f"wx_mm_{t}"] = M[f"wx_mm_{t}_1"]

                @block.scalar
                def _(sc, t=t):
                    C.wait(sc, "act", M[f"wx_mm_{t}"])
                    if t == 1:
                        C.wait(sc, "act", M["dtnst_dt_0"])
                    sc.copy(ssm_raw[:, 0], pwx1[:])
                    sc.copy(ssm_raw[:, 1], pwx2[:])
                    sc.copy(bc_raw[:, 0], pwxbc[0:16])
                    sc.copy(bc_raw[:, 1], pwxbc[32:48])
                    sc.activation(sq1[:, 0], pwx1[:], AF.Square)
                    sc.activation(sq1[:, 1], pwx2[:], AF.Square)
                    sc.activation(sqbc[:, 0], pwxbc[0:16], AF.Square)
                    M[f"sqd_{t}"] = C.bump("act", sc.activation(sqbc[:, 1], pwxbc[32:48], AF.Square))
                @block.tensor
                def _(te, t=t):
                    C.wait(te, "pe", M[f"sqd_{t}"])
                    te.matmul(pstat[:], onc[:, 0:1], sq1[:, 0], start=True, stop=False)
                    M[f"st_dt_{t}"] = C.bump("pe", te.matmul(pstat[:], onc[:, 0:1], sq1[:, 1], start=False, stop=True))
                @block.vector
                def _(dv, t=t):
                    C.wait(dv, "dve", M[f"st_dt_{t}"])
                    M[f"ms_dt_{t}"] = C.self_edge("dve", dv, dv.tensor_scalar(rrow[:, 0], pstat[:], 1.0 / 256, 1e-6, OP.mult, OP.add))
                @block.scalar
                def _(sc, t=t):
                    C.wait(sc, "act", M[f"ms_dt_{t}"])
                    M[f"sqrt_dt_{t}"] = C.bump("act", sc.activation(rrow[:, 1], rrow[:, 0], AF.Sqrt))
                @block.vector
                def _(dv, t=t):
                    C.wait(dv, "dve", M[f"sqrt_dt_{t}"])
                    M[f"rstd_dt_{t}"] = C.self_edge("dve", dv, dv.reciprocal(rrow[:, 2], rrow[:, 1]))
                @block.tensor
                def _(te, t=t):
                    C.wait(te, "pe", M[f"rstd_dt_{t}"])
                    M[f"bc_dt_{t}"] = C.bump("pe", te.matmul(pbcast[:], onr[:], rrow[:, 2], start=True, stop=True))
                @block.vector
                def _(dv, t=t):
                    C.wait(dv, "dve", M[f"bc_dt_{t}"])
                    if t == 1:
                        C.wait(dv, "dve", M["dtnst_dt_0"])
                    dv.scalar_tensor_tensor(dtn_sb[:, 0], ssm_raw[:, 0], lnw[:, 0:1], pbcast[:], OP.mult, OP.mult)
                    M[f"dtn_dt_{t}"] = C.bump("dve", dv.scalar_tensor_tensor(
                        dtn_sb[:, 1], ssm_raw[:, 1], lnw[:, 1:2], pbcast[:], OP.mult, OP.mult))
                @block.sync
                def _(sp, t=t):
                    C.wait(sp, "sp", M[f"dtn_dt_{t}"])
                    M[f"dtnst_dt_{t}"] = C.dma("dtst", sp.dma_start(
                        out=dtn_scr[0:256, 512 * t:512 * (t + 1)].rearrange("(a p) x -> p a x", p=128), in_=dtn_sb[:, 0:2]))
                for bi, lncol in enumerate((2, 3)):
                    @block.tensor
                    def _(te, t=t, bi=bi):
                        C.wait(te, "pe", M[f"sqd_{t}"])
                        C.wait(te, "pe", M[f"ms_dt_{t}"] if bi == 0 else M[f"ms_b0_{t}"])
                        M[f"st_b{bi}_{t}"] = C.bump("pe", te.matmul(pstat[:], onc[0:16, 0:1], sqbc[:, bi], start=True, stop=True))
                    @block.vector
                    def _(dv, t=t, bi=bi):
                        C.wait(dv, "dve", M[f"st_b{bi}_{t}"])
                        M[f"ms_b{bi}_{t}"] = C.self_edge("dve", dv, dv.tensor_scalar(rrow[:, 3 + 3 * bi], pstat[:], 1.0 / 16, 1e-6, OP.mult, OP.add))
                    @block.scalar
                    def _(sc, t=t, bi=bi):
                        C.wait(sc, "act", M[f"ms_b{bi}_{t}"])
                        M[f"sqrt_b{bi}_{t}"] = C.bump("act", sc.activation(rrow[:, 4 + 3 * bi], rrow[:, 3 + 3 * bi], AF.Sqrt))
                    @block.vector
                    def _(dv, t=t, bi=bi):
                        C.wait(dv, "dve", M[f"sqrt_b{bi}_{t}"])
                        M[f"rstd_b{bi}_{t}"] = C.self_edge("dve", dv, dv.reciprocal(rrow[:, 5 + 3 * bi], rrow[:, 4 + 3 * bi]))
                    @block.tensor
                    def _(te, t=t, bi=bi):
                        C.wait(te, "pe", M[f"rstd_b{bi}_{t}"])
                        C.wait(te, "pe", M[f"dtn_dt_{t}"] if bi == 0 else M[f"dtn_b0_{t}"])
                        M[f"bc_b{bi}_{t}"] = C.bump("pe", te.matmul(pbcast[:], onr[:], rrow[:, 5 + 3 * bi], start=True, stop=True))
                    @block.vector
                    def _(dv, t=t, bi=bi, lncol=lncol):
                        C.wait(dv, "dve", M[f"bc_b{bi}_{t}"])
                        if bi == 1:
                            C.wait(dv, "dve", M[f"dtnst_b0_{t}"])
                        elif t == 1:
                            C.wait(dv, "dve", M["dtnst_b1_0"])
                        M[f"dtn_b{bi}_{t}"] = C.bump("dve", dv.scalar_tensor_tensor(
                            dtn_sb[0:16, 2], bc_raw[:, bi], lnw[0:16, lncol:lncol + 1], pbcast[0:16, :], OP.mult, OP.mult))
                    @block.sync
                    def _(sp, t=t, bi=bi):
                        C.wait(sp, "sp", M[f"dtn_b{bi}_{t}"])
                        M[f"dtnst_b{bi}_{t}"] = C.dma(f"bcst{bi}", sp.dma_start(
                            out=dtn_scr[256 + 16 * bi:256 + 16 * (bi + 1), 512 * t:512 * (t + 1)], in_=dtn_sb[0:16, 2]))
                M[f"nrm_done_{t}"] = M[f"dtn_b1_{t}"]

            @block.gpsimd
            def _(gp):
                for s in ("vst0", "vst1", "dtst", "bcst0", "bcst1"):
                    C.wait(gp, "pool", C.dma_total(s))
                M["cc_v"] = C.bump("cc", gp.collective_compute(
                    "AllToAll", OP.bypass, replica_groups=RG,
                    ins=[v_scr.tensor.ap().opt()], outs=[v_a2a.tensor.ap().opt()]))
                M["cc_dtn"] = C.bump("cc", gp.collective_compute(
                    "AllGather", OP.bypass, replica_groups=RG,
                    ins=[dtn_scr.tensor.ap().opt()], outs=[dtn_ag.tensor.ap().opt()]))
            C.join(block)

        with contextlib.ExitStack() as ps:
            SP = 2048
            vh = ps.enter_context(nc.sbuf_tensor("vh", [128, 4, SP], BF16))
            dtn_sb2 = ps.enter_context(nc.sbuf_tensor("dtn_sb2", [128, 2, SP], BF16))
            bt = ps.enter_context(nc.sbuf_tensor("bt", [NST, SP], BF16))
            ct = ps.enter_context(nc.sbuf_tensor("ct", [NST, SP], BF16))
            wdt = ps.enter_context(nc.sbuf_tensor("wdt", [128, 2, 8, 128], BF16))
            ab = ps.enter_context(nc.sbuf_tensor("ab", [128, 8 * NST], F32))
            dtb = ps.enter_context(nc.sbuf_tensor("dtb", [128, 8], F32))
            dpp = ps.enter_context(nc.sbuf_tensor("dpp", [128, 8], F32))
            esl = ps.enter_context(nc.sbuf_tensor("esl", [NST, NST * 128], BF16))
            delta = ps.enter_context(nc.sbuf_tensor("delta", [128, 4, SP], BF16))
            du = ps.enter_context(nc.sbuf_tensor("du", [128, 4, SP], BF16))
            sp_t = ps.enter_context(nc.sbuf_tensor("sp_t", [128, SP], F32))
            sp_t2 = ps.enter_context(nc.sbuf_tensor("sp_t2", [128, SP], F32))
            bbc = ps.enter_context(nc.sbuf_tensor("bbc", [128, SP], BF16))
            cbc = ps.enter_context(nc.sbuf_tensor("cbc", [128, SP], BF16))
            a_sb = ps.enter_context(nc.sbuf_tensor("a_sb", [128, 4, SP], F32))
            b_sb = ps.enter_context(nc.sbuf_tensor("b_sb", [128, 2, SP], BF16))
            s_sb = ps.enter_context(nc.sbuf_tensor("s_sb", [128, 2, SP], BF16))
            z_sb = ps.enter_context(nc.sbuf_tensor("z_sb", [128, 2, SP], BF16))
            y_acc = ps.enter_context(nc.sbuf_tensor("y_acc", [128, 4, SP], F32))
            y2_sb = ps.enter_context(nc.sbuf_tensor("y2_sb", [128, 2, SP], BF16))
            carrys = [ps.enter_context(nc.sbuf_tensor(f"cr{i}", [128, 1], F32)) for i in range(8 * NST)]
            pdelta = ps.enter_context(nc.psum_tensor("pdelta", [128, 2, 512], F32))
            pbc2 = ps.enter_context(nc.psum_tensor("pbc2", [128, SP], F32))

            @block.gpsimd
            def _(gp):
                C.dma("wdtl", gp.dma_start(out=wdt[:], in_=wdtT_c.rearrange("(rc p) (dt j) -> p rc dt j", p=128, j=128)))
            @block.sync
            def _(sp):
                C.dma("cst2", sp.dma_start(out=ab[:], in_=A_c[:]))
                C.dma("cst2", sp.dma_start(out=dtb[:], in_=dtb_c[:]))
                C.dma("cst2", sp.dma_start(out=dpp[:], in_=Dp_c[:]))
                C.dma("cst2", sp.dma_start(out=esl[:], in_=esel[:]))
            CST = C.dma_total("cst2")
            WDTL = C.dma_total("wdtl")

            pd_free = {}
            a_free = {}
            a_last = {}
            z_free = {}
            last_bbc_read = None
            last_pbc_ev = None
            prev_delta_add = None
            prev_y2 = {}
            y2st = {}

            for s in range(4):
                tsl = slice(SP * s, SP * (s + 1))
                @block.sync
                def _(sp, s=s, tsl=tsl):
                    if s >= 1:
                        C.wait(sp, "sp", M[f"dmm_last_{s-1}"])
                        C.wait(sp, "sp", M[f"bc_mm_last_{s-1}"])
                    for r in range(8):
                        k = 8 * s + r
                        ow, gg = owner(k), gidx(k)
                        C.dma("dtnl", sp.dma_start(
                            out=dtn_sb2[:, :, 256 * r:256 * (r + 1)],
                            in_=dtn_ag[288 * ow:288 * ow + 256, 256 * gg:256 * (gg + 1)].rearrange("(a p) t -> p a t", p=128)))
                        C.dma("dtnl", sp.dma_start(out=bt[:, 256 * r:256 * (r + 1)],
                                                   in_=dtn_ag[288 * ow + 256:288 * ow + 272, 256 * gg:256 * (gg + 1)]))
                        C.dma("dtnl", sp.dma_start(out=ct[:, 256 * r:256 * (r + 1)],
                                                   in_=dtn_ag[288 * ow + 272:288 * ow + 288, 256 * gg:256 * (gg + 1)]))
                    M[f"dtnl_{s}"] = C.dma_total("dtnl")

                for h in range(2):
                    @block.sync
                    def _(sp, s=s, h=h, tsl=tsl):
                        prev = y2st.get((s, h - 1, 3) if h else (s - 1, 1, 3))
                        if prev is not None:
                            C.wait(sp, "sp", prev)
                        for r in range(8):
                            k = 8 * s + r
                            ow, gg = owner(k), gidx(k)
                            C.dma("vhl", sp.dma_start(
                                out=vh[:, :, 256 * r:256 * (r + 1)],
                                in_=v_a2a[1024 * ow + 512 * h:1024 * ow + 512 * (h + 1), 256 * gg:256 * (gg + 1)].rearrange("(dt p) t -> p dt t", p=128)))
                        M[f"vl_{s}_{h}"] = C.dma_total("vhl")

                    for di in range(4):
                        dt_g = 4 * h + di
                        for q in range(4):
                            par = q % 2
                            @block.tensor
                            def _(te, s=s, di=di, dt_g=dt_g, q=q, par=par):
                                C.wait(te, "pe", M[f"dtnl_{s}"])
                                C.wait(te, "pe", CST)
                                C.wait(te, "pe", WDTL)
                                if par in pd_free:
                                    C.wait(te, "pe", pd_free[par])
                                te.matmul(pdelta[:, par], wdt[:, 0, dt_g], dtn_sb2[:, 0, 512 * q:512 * (q + 1)], start=True, stop=False)
                                M["dmm_cur"] = C.bump("pe", te.matmul(
                                    pdelta[:, par], wdt[:, 1, dt_g], dtn_sb2[:, 1, 512 * q:512 * (q + 1)], start=False, stop=True))
                            @block.scalar
                            def _(sc, s=s, h=h, di=di, dt_g=dt_g, q=q, par=par):
                                C.wait(sc, "act", CST)
                                C.wait(sc, "act", M["dmm_cur"])
                                if q == 0 and prev_delta_add is not None:
                                    C.wait(sc, "act", prev_delta_add)
                                qq = slice(512 * q, 512 * (q + 1))
                                sc.activation(sp_t[:, qq], pdelta[:, par], AF.Abs, bias=dtb[:, dt_g:dt_g + 1])
                                sc.activation(sp_t[:, qq], sp_t[:, qq], AF.Exp, scale=-1.0)
                                sc.activation(sp_t[:, qq], sp_t[:, qq], AF.Ln, bias=onc[:, 0:1])
                                pd_free[par] = M["spq_cur"] = C.bump("act", sc.activation(
                                    sp_t2[:, qq], pdelta[:, par], AF.Relu, bias=dtb[:, dt_g:dt_g + 1]))
                        M[f"spl_{s}_{h}_{di}"] = M["spq_cur"]
                        if di == 3 and h == 1:
                            M[f"dmm_last_{s}"] = M["dmm_cur"]
                        @block.vector
                        def _(dv, s=s, h=h, di=di):
                            C.wait(dv, "dve", M[f"spl_{s}_{h}_{di}"])
                            if di in a_last:
                                C.wait(dv, "dve", a_last[di])
                            M[f"delta_{s}_{h}_{di}"] = C.bump("dve", dv.tensor_tensor(delta[:, di], sp_t2[:], sp_t[:], OP.add))
                            C.wait(dv, "dve", M[f"vl_{s}_{h}"])
                            M[f"du_{s}_{h}_{di}"] = C.bump("dve", dv.tensor_tensor(du[:, di], delta[:, di], vh[:, di], OP.mult))
                        prev_delta_add = M[f"delta_{s}_{h}_{di}"]

                    for n in range(NST):
                        act_base_n = C.ecnt["act"]
                        @block.tensor
                        def _(te, s=s, n=n, act_base_n=act_base_n):
                            C.wait(te, "pe", M[f"dtnl_{s}"])
                            if last_pbc_ev is not None:
                                C.wait(te, "pe", last_pbc_ev)
                            last = None
                            for q in range(4):
                                last = te.matmul(pbc2[:, 512 * q:512 * (q + 1)], esl[:, 128 * n:128 * (n + 1)],
                                                 bt[:, 512 * q:512 * (q + 1)], start=True, stop=True)
                            M["b_mm"] = C.bump("pe", last)
                            C.wait(te, "pe", ("E", "act", act_base_n + 1))
                            for q in range(4):
                                last = te.matmul(pbc2[:, 512 * q:512 * (q + 1)], esl[:, 128 * n:128 * (n + 1)],
                                                 ct[:, 512 * q:512 * (q + 1)], start=True, stop=True)
                            M["bc_mm_cur"] = C.bump("pe", last)
                        M[f"bc_mm_last_{s}"] = M["bc_mm_cur"]

                        @block.scalar
                        def _(sc, s=s, h=h, n=n, lbr=last_bbc_read):
                            C.wait(sc, "act", M["b_mm"])
                            if lbr is not None:
                                C.wait(sc, "act", lbr)
                            M["bbc_ev"] = C.bump("act", sc.copy(bbc[:], pbc2[:]))
                            C.wait(sc, "act", M["bc_mm_cur"])
                            M["cbc_ev"] = C.bump("act", sc.copy(cbc[:], pbc2[:]))
                            for di in range(4):
                                dt_g = 4 * h + di
                                apar = di
                                C.wait(sc, "act", M[f"delta_{s}_{h}_{di}"])
                                if apar in a_free:
                                    C.wait(sc, "act", a_free[apar])
                                M[f"a_{apar}"] = C.bump("act", sc.activation(
                                    a_sb[:, apar], delta[:, di], AF.Exp, scale=ab[:, NST * dt_g + n:NST * dt_g + n + 1]))
                                if n == NST - 1:
                                    a_last[di] = M[f"a_{apar}"]
                        last_pbc_ev = M["cbc_ev"]

                        for di in range(4):
                            dt_g = 4 * h + di
                            par = di % 2
                            apar = di
                            @block.vector
                            def _(dv, s=s, h=h, n=n, di=di, dt_g=dt_g, par=par, apar=apar):
                                C.wait(dv, "dve", M[f"du_{s}_{h}_{di}"])
                                C.wait(dv, "dve", M["bbc_ev"])
                                dv.tensor_tensor(b_sb[:, par], du[:, di], bbc[:], OP.mult)
                                C.wait(dv, "dve", M[f"a_{apar}"])
                                cr = carrys[dt_g * NST + n]
                                if s == 0:
                                    sc_i = dv.tensor_tensor_scan(s_sb[:, par], a_sb[:, apar], b_sb[:, par], 0.0, OP.mult, OP.add)
                                else:
                                    sc_i = dv.tensor_tensor_scan(s_sb[:, par], a_sb[:, apar], b_sb[:, par], cr[:, 0:1], OP.mult, OP.add)
                                C.self_edge("dve", dv, sc_i)
                                a_free[apar] = C.bump("dve", dv.tensor_copy(cr[:], s_sb[:, par, SP - 1:SP]))
                                C.wait(dv, "dve", M["cbc_ev"])
                                if par in z_free:
                                    C.wait(dv, "dve", z_free[par])
                                M[f"z_{par}"] = C.bump("dve", dv.tensor_tensor(z_sb[:, par], s_sb[:, par], cbc[:], OP.mult))
                            @block.gpsimd
                            def _(gp, s=s, h=h, n=n, di=di, par=par):
                                C.wait(gp, "pool", M[f"z_{par}"])
                                if n == 0:
                                    prev = prev_y2.get(di)
                                    if prev is not None:
                                        C.wait(gp, "pool", prev)
                                    mm = gp.tensor_copy(y_acc[:, di], z_sb[:, par])
                                else:
                                    mm = gp.tensor_tensor(y_acc[:, di], y_acc[:, di], z_sb[:, par], OP.add)
                                z_free[par] = C.bump("pool", mm)
                                if n == NST - 1:
                                    M[f"y_done_{di}"] = z_free[par]
                        last_bbc_read = M["z_1"]

                    for di in range(4):
                        dt_g = 4 * h + di
                        @block.vector
                        def _(dv, s=s, h=h, di=di, dt_g=dt_g):
                            C.wait(dv, "dve", M[f"y_done_{di}"])
                            prevst = y2st.get((s, h, di - 2) if di >= 2 else ((s, h - 1, di + 2) if h else (s - 1, 1, di + 2)))
                            if prevst is not None:
                                C.wait(dv, "dve", prevst)
                            M["y2_cur"] = C.bump("dve", dv.scalar_tensor_tensor(
                                y2_sb[:, di % 2], vh[:, di], dpp[:, dt_g:dt_g + 1], y_acc[:, di], OP.mult, OP.add))
                            prev_y2[di] = M["y2_cur"]
                        @block.sync
                        def _(sp, s=s, h=h, di=di, tsl=tsl):
                            C.wait(sp, "sp", M["y2_cur"])
                            for r in range(8):
                                k = 8 * s + r
                                ow, gg = owner(k), gidx(k)
                                C.dma(f"y2st{di % 2}", sp.dma_start(
                                    out=y_in[1024 * ow + 512 * h + 128 * di:1024 * ow + 512 * h + 128 * (di + 1),
                                             256 * gg:256 * (gg + 1)],
                                    in_=y2_sb[:, di % 2, 256 * r:256 * (r + 1)]))
                            y2st[(s, h, di)] = C.dma_total(f"y2st{di % 2}")

            @block.gpsimd
            def _(gp):
                C.wait(gp, "pool", C.dma_total("y2st0"))
                C.wait(gp, "pool", C.dma_total("y2st1"))
                M["cc_y"] = C.bump("cc", gp.collective_compute(
                    "AllToAll", OP.bypass, replica_groups=RG,
                    ins=[y_in.tensor.ap().opt()], outs=[y_out.tensor.ap().opt()]))
            C.join(block)

        with contextlib.ExitStack() as ps:
            y3 = ps.enter_context(nc.sbuf_tensor("y3", [128, 64, TLOC], BF16))
            t_y2 = ps.enter_context(nc.sbuf_tensor("t_y2", [128, 2, TLOC], BF16))
            t_g = ps.enter_context(nc.sbuf_tensor("t_g", [128, 2, TLOC], BF16))
            t_sg = ps.enter_context(nc.sbuf_tensor("t_sg", [128, TLOC], BF16))
            wbuf3 = ps.enter_context(nc.sbuf_tensor("wbuf3", [128, 2, 32, 128], BF16))
            x0t = ps.enter_context(nc.sbuf_tensor("x0t", [128, 2, TLOC], F32))
            r_sb = ps.enter_context(nc.sbuf_tensor("r_sb3", [128, 2, TLOC], F32))
            sq3 = ps.enter_context(nc.sbuf_tensor("sq3", [128, TLOC], F32))
            rrow3 = ps.enter_context(nc.sbuf_tensor("rrow3", [1, 6, TLOC], F32))
            xn = ps.enter_context(nc.sbuf_tensor("xn3", [128, 2, TLOC], F32))
            pW = ps.enter_context(nc.psum_tensor("pW3", [128, TLOC], F32))
            psum_s = ps.enter_context(nc.psum_tensor("pS3", [1, TLOC], F32))
            psum_q = ps.enter_context(nc.psum_tensor("pQ3", [1, TLOC], F32))
            pbc3 = ps.enter_context(nc.psum_tensor("pbc3", [128, TLOC], F32))

            for dc in range(64):
                par = dc % 2
                @block.sync
                def _(sp, dc=dc, par=par):
                    if dc >= 2:
                        C.wait(sp, "sp", M[f"y3_{dc-2}"])
                    C.dma(f"y2l{par}", sp.dma_start(out=t_y2[:, par], in_=y_out[128 * dc:128 * (dc + 1), :]))
                    M[f"ld3_{dc}"] = C.dma(f"gl{par}", sp.dma_start(out=t_g[:, par], in_=gate_scr[128 * dc:128 * (dc + 1), :]))
                @block.scalar
                def _(sc, dc=dc, par=par):
                    C.wait(sc, "act", C.dma_total(f"y2l{par}"))
                    C.wait(sc, "act", C.dma_total(f"gl{par}"))
                    if dc >= 1:
                        C.wait(sc, "act", M[f"y3_{dc-1}"])
                    M[f"sg_{dc}"] = C.bump("act", sc.activation(t_sg[:], t_g[:, par], AF.Silu))
                @block.vector
                def _(dv, dc=dc, par=par):
                    C.wait(dv, "dve", M[f"sg_{dc}"])
                    M[f"y3_{dc}"] = C.bump("dve", dv.tensor_tensor(y3[:, dc], t_y2[:, par], t_sg[:], OP.mult))

            for hc in range(32):
                par = hc % 2
                @block.gpsimd
                def _(gp, hc=hc):
                    for half in range(2):
                        g = 2 * hc + half
                        if g >= 2:
                            C.wait(gp, "pool", M[f"mmh3_{g-2}"])
                        M[f"wl3_{g}"] = C.dma(f"wl3{g % 2}", gp.dma_start(
                            out=wbuf3[:, g % 2],
                            in_=W_outT[4096 * half:4096 * (half + 1), 128 * hc:128 * (hc + 1)].rearrange(
                                "(dc p) j -> p dc j", p=128)))
                @block.sync
                def _(sp, hc=hc, par=par):
                    if hc >= 2:
                        C.wait(sp, "sp", M[f"resid3_{hc-2}"])
                    M[f"x0l_{hc}"] = C.dma(f"x0l{par}", sp.dma_start(out=x0t[:, par], in_=x0T[128 * hc:128 * (hc + 1), :]))
                @block.tensor
                def _(te, hc=hc):
                    if hc == 0:
                        for dc in range(64):
                            C.wait(te, "pe", M[f"y3_{dc}"])
                    if hc >= 1:
                        C.wait(te, "pe", M[f"ev3_{hc-1}"])
                    last = None
                    for half in range(2):
                        g = 2 * hc + half
                        C.wait(te, "pe", M[f"wl3_{g}"])
                        for dj in range(32):
                            dc = 32 * half + dj
                            te.ldweights(wbuf3[:, g % 2, dj])
                            te.matmul(pW[:, 0:512], wbuf3[:, g % 2, dj], y3[:, dc, 0:512], start=(dc == 0), stop=(dc == 63))
                            last = te.matmul(pW[:, 512:1024], wbuf3[:, g % 2, dj], y3[:, dc, 512:1024], start=(dc == 0), stop=(dc == 63))
                        M[f"mmh3_{g}"] = C.bump("pe", last)
                    M[f"mm3_{hc}"] = M[f"mmh3_{2 * hc + 1}"]
                @block.vector
                def _(dv, hc=hc, par=par):
                    C.wait(dv, "dve", M[f"mm3_{hc}"])
                    C.wait(dv, "dve", M[f"x0l_{hc}"])
                    if hc >= 2:
                        C.wait(dv, "dve", M[f"rst3_{hc-2}"])
                    M[f"resid3_{hc}"] = C.bump("dve", dv.tensor_tensor(r_sb[:, par], pW[:], x0t[:, par], OP.add))
                @block.scalar
                def _(sc, hc=hc, par=par):
                    C.wait(sc, "act", M[f"resid3_{hc}"])
                    if hc >= 1:
                        C.wait(sc, "act", M[f"stat3_{hc-1}"])
                    M[f"ev3_{hc}"] = M[f"sq3_{hc}"] = C.bump("act", sc.activation(sq3[:], r_sb[:, par], AF.Square))
                @block.tensor
                def _(te, hc=hc, par=par):
                    C.wait(te, "pe", M[f"sq3_{hc}"])
                    te.matmul(psum_s[:, 0:512], onc[:, 0:1], r_sb[:, par, 0:512], start=(hc == 0), stop=(hc == 31))
                    te.matmul(psum_s[:, 512:1024], onc[:, 0:1], r_sb[:, par, 512:1024], start=(hc == 0), stop=(hc == 31))
                    te.matmul(psum_q[:, 0:512], onc[:, 0:1], sq3[:, 0:512], start=(hc == 0), stop=(hc == 31))
                    M[f"stat3_{hc}"] = C.bump("pe", te.matmul(psum_q[:, 512:1024], onc[:, 0:1], sq3[:, 512:1024], start=(hc == 0), stop=(hc == 31)))
                @block.sync
                def _(sp, hc=hc, par=par):
                    C.wait(sp, "sp", M[f"stat3_{hc}"])
                    M[f"rst3_{hc}"] = C.dma(f"rst{par}", sp.dma_start(out=r_scr[128 * hc:128 * (hc + 1), :], in_=r_sb[:, par]))

            @block.vector
            def _(dv):
                C.wait(dv, "dve", M["stat3_31"])
                C.self_edge("dve", dv, dv.tensor_scalar(rrow3[:, 0], psum_s[:], 1.0 / H, None, OP.mult))
                C.self_edge("dve", dv, dv.tensor_scalar(rrow3[:, 1], psum_q[:], 1.0 / H, None, OP.mult))
                C.self_edge("dve", dv, dv.tensor_tensor(rrow3[:, 2], rrow3[:, 0], rrow3[:, 0], OP.mult))
                C.self_edge("dve", dv, dv.tensor_tensor(rrow3[:, 3], rrow3[:, 1], rrow3[:, 2], OP.subtract))
                M["var3"] = C.bump("dve", dv.tensor_scalar(rrow3[:, 3], rrow3[:, 3], 1.0, 1e-5, OP.mult, OP.add))
            @block.scalar
            def _(sc):
                C.wait(sc, "act", M["var3"])
                M["sqrt3"] = C.bump("act", sc.activation(rrow3[:, 4], rrow3[:, 3], AF.Sqrt))
            @block.vector
            def _(dv):
                C.wait(dv, "dve", M["sqrt3"])
                M["rstd3"] = C.self_edge("dve", dv, dv.reciprocal(rrow3[:, 5], rrow3[:, 4]))
            @block.tensor
            def _(te):
                C.wait(te, "pe", M["rstd3"])
                te.matmul(pbc3[:, 0:512], onr[:], rrow3[:, 5, 0:512], start=True, stop=True)
                te.matmul(pbc3[:, 512:1024], onr[:], rrow3[:, 5, 512:1024], start=True, stop=True)
                te.matmul(pW[:, 0:512], onr[:], rrow3[:, 0, 0:512], start=True, stop=True)
                M["bc3"] = C.bump("pe", te.matmul(pW[:, 512:1024], onr[:], rrow3[:, 0, 512:1024], start=True, stop=True))
            for hc in range(32):
                par = hc % 2
                @block.sync
                def _(sp, hc=hc, par=par):
                    C.wait(sp, "sp", C.dma_total("rst0"))
                    C.wait(sp, "sp", C.dma_total("rst1"))
                    if hc >= 2:
                        C.wait(sp, "sp", M[f"norm3_{hc-2}"])
                    M[f"rrd3_{hc}"] = C.dma(f"rrd{par}", sp.dma_start(out=r_sb[:, par], in_=r_scr[128 * hc:128 * (hc + 1), :]))
                @block.vector
                def _(dv, hc=hc, par=par):
                    C.wait(dv, "dve", M[f"rrd3_{hc}"])
                    C.wait(dv, "dve", M["bc3"])
                    if hc >= 2:
                        C.wait(dv, "dve", M[f"xst3_{hc-2}"])
                    dv.tensor_tensor(xn[:, par], r_sb[:, par], pW[:], OP.subtract)
                    M[f"norm3_{hc}"] = C.bump("dve", dv.tensor_tensor(xn[:, par], xn[:, par], pbc3[:], OP.mult))
                @block.sync
                def _(sp, hc=hc, par=par):
                    C.wait(sp, "sp", M[f"norm3_{hc}"])
                    M[f"xst3_{hc}"] = C.dma(f"xst{par}", sp.dma_start(out=x1_scr[128 * hc:128 * (hc + 1), :], in_=xn[:, par]))
            C.join(block)

        with contextlib.ExitStack() as ps:
            x1bf = ps.enter_context(nc.sbuf_tensor("x1bf", [128, 32, TLOC], BF16))
            bwbuf = ps.enter_context(nc.sbuf_tensor("bwbuf", [128, 2, 32, 128], BF16))
            a2bf = ps.enter_context(nc.sbuf_tensor("a2bf", [128, 32, 128], BF16))
            b2bf = ps.enter_context(nc.sbuf_tensor("b2bf", [128, 32, 128], BF16))
            repb = ps.enter_context(nc.sbuf_tensor("repb", [8, 128], BF16))
            rwb = ps.enter_context(nc.sbuf_tensor("rwb", [128, 32, 8], BF16))
            lgT = ps.enter_context(nc.sbuf_tensor("lgT", [8, TLOC], F32))
            lg = ps.enter_context(nc.sbuf_tensor("lg", [128, 8, 8], F32))
            esb = ps.enter_context(nc.sbuf_tensor("esb", [128, 8, 8], F32))
            wd = ps.enter_context(nc.sbuf_tensor("wd", [128, 8, 8], F32))
            mrow = ps.enter_context(nc.sbuf_tensor("mrow", [128, 8, 4], F32))
            top8 = ps.enter_context(nc.sbuf_tensor("top8", [128, 8, 8], F32))
            wdT = ps.enter_context(nc.sbuf_tensor("wdT", [8, TLOC], BF16))
            w16 = ps.enter_context(nc.sbuf_tensor("w16", [128, TLOC], BF16))
            midg = ps.enter_context(nc.sbuf_tensor("midg", [128, TLOC], BF16))
            fb = ps.enter_context(nc.sbuf_tensor("fb", [128, TLOC], F32))
            x1f = ps.enter_context(nc.sbuf_tensor("x1f", [128, 2, TLOC], F32))
            r_sb = ps.enter_context(nc.sbuf_tensor("r_sb0", [128, 2, TLOC], F32))
            sq0 = ps.enter_context(nc.sbuf_tensor("sq0", [128, TLOC], F32))
            rrow0 = ps.enter_context(nc.sbuf_tensor("rrow0", [1, 6, TLOC], F32))
            xn0 = ps.enter_context(nc.sbuf_tensor("xn0", [128, 2, TLOC], F32))
            pA = ps.enter_context(nc.psum_tensor("pA", [128, TLOC], F32))
            pB = ps.enter_context(nc.psum_tensor("pB2", [128, TLOC], F32))
            pS = ps.enter_context(nc.psum_tensor("pS", [1, TLOC], F32))
            pQ = ps.enter_context(nc.psum_tensor("pQ", [1, TLOC], F32))

            @block.gpsimd
            def _(gp):
                C.dma("gc0", gp.dma_start(out=x1bf[:], in_=x1_scr.rearrange("(e p) t -> p e t", p=128)))
                C.dma("gc0", gp.dma_start(out=a2bf[:], in_=loraA2.rearrange("(e p) j -> p e j", p=128)))
                C.dma("gc0", gp.dma_start(out=b2bf[:], in_=loraB2.rearrange("p (o j) -> p o j", j=128)))
                C.dma("gc0", gp.dma_start(out=repb[:], in_=rep16[:]))
                C.dma("gc0", gp.dma_start(out=rwb[:], in_=router_WT.rearrange("(e p) j -> p e j", p=128)))
            GC0 = C.dma_total("gc0")

            @block.tensor
            def _(te):
                C.wait(te, "pe", GC0)
                last = None
                for ec in range(32):
                    te.ldweights(rwb[:, ec])
                    te.matmul(pA[0:8, 0:512], rwb[:, ec], x1bf[:, ec, 0:512], start=(ec == 0), stop=(ec == 31))
                    last = te.matmul(pA[0:8, 512:1024], rwb[:, ec], x1bf[:, ec, 512:1024], start=(ec == 0), stop=(ec == 31))
                M["lg_mm"] = C.bump("pe", last)
            @block.scalar
            def _(sc):
                C.wait(sc, "act", M["lg_mm"])
                M["lg_ev"] = C.bump("act", sc.copy(lgT[:], pA[0:8, :]))
            @block.tensor
            def _(te):
                C.wait(te, "pe", M["lg_ev"])
                last = None
                for ti in range(8):
                    last = te.transpose(pB[0:128, 8 * ti:8 * (ti + 1)], lgT[:, 128 * ti:128 * (ti + 1)], idt[0:8, 0:8])
                M["lg_tr"] = C.bump("pe", last)
            @block.scalar
            def _(sc):
                C.wait(sc, "act", M["lg_tr"])
                M["lg2"] = C.bump("act", sc.copy(lg[:].rearrange("p a b -> p (a b)"), pB[0:128, 0:64]))

            for ti in range(8):
                @block.vector
                def _(dv, ti=ti):
                    C.wait(dv, "dve", M["lg2"])
                    C.self_edge("dve", dv, dv.reduce_max(mrow[:, ti, 0:1], lg[:, ti], axis=AX.X))
                    M[f"mx_{ti}"] = C.bump("dve", dv.tensor_scalar(mrow[:, ti, 1:2], mrow[:, ti, 0:1], -1.0, None, OP.mult))
                @block.scalar
                def _(sc, ti=ti):
                    C.wait(sc, "act", M[f"mx_{ti}"])
                    M[f"e_{ti}"] = C.bump("act", sc.activation(esb[:, ti], lg[:, ti], AF.Exp, bias=mrow[:, ti, 1:2]))
                @block.vector
                def _(dv, ti=ti):
                    C.wait(dv, "dve", M[f"e_{ti}"])
                    C.self_edge("dve", dv, dv.max(top8[:, ti], esb[:, ti]))
                    C.self_edge("dve", dv, dv.tensor_tensor(mrow[:, ti, 2:3], top8[:, ti, 0:1], top8[:, ti, 1:2], OP.add))
                    C.self_edge("dve", dv, dv.reciprocal(mrow[:, ti, 3:4], mrow[:, ti, 2:3]))
                    C.self_edge("dve", dv, dv.tensor_scalar(wd[:, ti], esb[:, ti], top8[:, ti, 1:2], None, OP.is_ge))
                    C.self_edge("dve", dv, dv.tensor_tensor(wd[:, ti], wd[:, ti], esb[:, ti], OP.mult))
                    M[f"wd_{ti}"] = C.bump("dve", dv.tensor_scalar(wd[:, ti], wd[:, ti], mrow[:, ti, 3:4], None, OP.mult))

            @block.tensor
            def _(te):
                for ti in range(8):
                    C.wait(te, "pe", M[f"wd_{ti}"])
                last = None
                for ti in range(8):
                    last = te.transpose(pA[0:8, 128 * ti:128 * (ti + 1)], wd[:, ti], idt[:])
                M["wd_tr"] = C.bump("pe", last)
            @block.scalar
            def _(sc):
                C.wait(sc, "act", M["wd_tr"])
                M["wdT_ev"] = C.bump("act", sc.copy(wdT[:], pA[0:8, :]))
            @block.tensor
            def _(te):
                C.wait(te, "pe", M["wdT_ev"])
                te.ldweights(repb[:])
                te.matmul(pA[0:128, 0:512], repb[:], wdT[:, 0:512], start=True, stop=True)
                M["w16_mm"] = C.bump("pe", te.matmul(pA[0:128, 512:1024], repb[:], wdT[:, 512:1024], start=True, stop=True))
            @block.scalar
            def _(sc):
                C.wait(sc, "act", M["w16_mm"])
                M["w16_ev"] = C.bump("act", sc.copy(w16[:], pA[:]))
            @block.tensor
            def _(te):
                C.wait(te, "pe", M["w16_ev"])
                last = None
                for ec in range(32):
                    te.ldweights(a2bf[:, ec])
                    te.matmul(pB[:, 0:512], a2bf[:, ec], x1bf[:, ec, 0:512], start=(ec == 0), stop=(ec == 31))
                    last = te.matmul(pB[:, 512:1024], a2bf[:, ec], x1bf[:, ec, 512:1024], start=(ec == 0), stop=(ec == 31))
                M["mid_mm"] = C.bump("pe", last)
            @block.vector
            def _(dv):
                C.wait(dv, "dve", M["mid_mm"])
                M["midg"] = C.bump("dve", dv.tensor_tensor(midg[:], pB[:], w16[:], OP.mult))

            for oc in range(32):
                par = oc % 2
                @block.gpsimd
                def _(gp, oc=oc, par=par):
                    if oc >= 2:
                        C.wait(gp, "pool", M[f"bmm_{oc-2}"])
                    M[f"bwl_{oc}"] = C.dma(f"bwl{par}", gp.dma_start(
                        out=bwbuf[:, par],
                        in_=base_WT[:, 128 * oc:128 * (oc + 1)].rearrange("(e p) j -> p e j", p=128)))
                @block.sync
                def _(sp, oc=oc, par=par):
                    if oc >= 2:
                        C.wait(sp, "sp", M[f"resid0_{oc-2}"])
                    M[f"x1l_{oc}"] = C.dma(f"x1l{par}", sp.dma_start(out=x1f[:, par], in_=x1_scr[128 * oc:128 * (oc + 1), :]))
                @block.tensor
                def _(te, oc=oc, par=par):
                    C.wait(te, "pe", M[f"bwl_{oc}"])
                    C.wait(te, "pe", M["midg"])
                    if oc >= 1:
                        C.wait(te, "pe", M[f"bev_{oc-1}"])
                        C.wait(te, "pe", M[f"resid0_{oc-1}"])
                    last = None
                    for ec in range(32):
                        te.ldweights(bwbuf[:, par, ec])
                        te.matmul(pA[:, 0:512], bwbuf[:, par, ec], x1bf[:, ec, 0:512], start=(ec == 0), stop=(ec == 31))
                        last = te.matmul(pA[:, 512:1024], bwbuf[:, par, ec], x1bf[:, ec, 512:1024], start=(ec == 0), stop=(ec == 31))
                    M[f"bmm_{oc}"] = C.bump("pe", last)
                    te.ldweights(b2bf[:, oc])
                    te.matmul(pB[:, 0:512], b2bf[:, oc], midg[:, 0:512], start=True, stop=True)
                    M[f"imm_{oc}"] = C.bump("pe", te.matmul(pB[:, 512:1024], b2bf[:, oc], midg[:, 512:1024], start=True, stop=True))
                @block.scalar
                def _(sc, oc=oc):
                    C.wait(sc, "act", M[f"bmm_{oc}"])
                    if oc >= 1:
                        C.wait(sc, "act", M[f"resid0_{oc-1}"])
                    M[f"bev_{oc}"] = C.bump("act", sc.copy(fb[:], pA[:]))
                @block.vector
                def _(dv, oc=oc, par=par):
                    C.wait(dv, "dve", M[f"bev_{oc}"])
                    C.wait(dv, "dve", M[f"imm_{oc}"])
                    C.wait(dv, "dve", M[f"x1l_{oc}"])
                    if oc >= 2:
                        C.wait(dv, "dve", M[f"rst0_{oc-2}"])
                    dv.tensor_tensor(r_sb[:, par], fb[:], pB[:], OP.add)
                    M[f"resid0_{oc}"] = C.bump("dve", dv.tensor_tensor(r_sb[:, par], r_sb[:, par], x1f[:, par], OP.add))
                @block.scalar
                def _(sc, oc=oc, par=par):
                    C.wait(sc, "act", M[f"resid0_{oc}"])
                    if oc >= 1:
                        C.wait(sc, "act", M[f"stat0_{oc-1}"])
                    M[f"sq0_{oc}"] = C.bump("act", sc.activation(sq0[:], r_sb[:, par], AF.Square))
                @block.tensor
                def _(te, oc=oc, par=par):
                    C.wait(te, "pe", M[f"sq0_{oc}"])
                    te.matmul(pS[:, 0:512], onc[:, 0:1], r_sb[:, par, 0:512], start=(oc == 0), stop=(oc == 31))
                    te.matmul(pS[:, 512:1024], onc[:, 0:1], r_sb[:, par, 512:1024], start=(oc == 0), stop=(oc == 31))
                    te.matmul(pQ[:, 0:512], onc[:, 0:1], sq0[:, 0:512], start=(oc == 0), stop=(oc == 31))
                    M[f"stat0_{oc}"] = C.bump("pe", te.matmul(pQ[:, 512:1024], onc[:, 0:1], sq0[:, 512:1024], start=(oc == 0), stop=(oc == 31)))
                @block.sync
                def _(sp, oc=oc, par=par):
                    C.wait(sp, "sp", M[f"stat0_{oc}"])
                    M[f"rst0_{oc}"] = C.dma(f"rsb{par}", sp.dma_start(out=r_scr[128 * oc:128 * (oc + 1), :], in_=r_sb[:, par]))

            @block.vector
            def _(dv):
                C.wait(dv, "dve", M["stat0_31"])
                C.self_edge("dve", dv, dv.tensor_scalar(rrow0[:, 0], pS[:], 1.0 / H, None, OP.mult))
                C.self_edge("dve", dv, dv.tensor_scalar(rrow0[:, 1], pQ[:], 1.0 / H, None, OP.mult))
                C.self_edge("dve", dv, dv.tensor_tensor(rrow0[:, 2], rrow0[:, 0], rrow0[:, 0], OP.mult))
                C.self_edge("dve", dv, dv.tensor_tensor(rrow0[:, 3], rrow0[:, 1], rrow0[:, 2], OP.subtract))
                M["var0"] = C.bump("dve", dv.tensor_scalar(rrow0[:, 3], rrow0[:, 3], 1.0, 1e-5, OP.mult, OP.add))
            @block.scalar
            def _(sc):
                C.wait(sc, "act", M["var0"])
                M["sqrt0"] = C.bump("act", sc.activation(rrow0[:, 4], rrow0[:, 3], AF.Sqrt))
            @block.vector
            def _(dv):
                C.wait(dv, "dve", M["sqrt0"])
                M["rstd0"] = C.self_edge("dve", dv, dv.reciprocal(rrow0[:, 5], rrow0[:, 4]))
            @block.tensor
            def _(te):
                C.wait(te, "pe", M["rstd0"])
                te.matmul(pB[:, 0:512], onr[:], rrow0[:, 5, 0:512], start=True, stop=True)
                te.matmul(pB[:, 512:1024], onr[:], rrow0[:, 5, 512:1024], start=True, stop=True)
                te.matmul(pA[:, 0:512], onr[:], rrow0[:, 0, 0:512], start=True, stop=True)
                M["bc0"] = C.bump("pe", te.matmul(pA[:, 512:1024], onr[:], rrow0[:, 0, 512:1024], start=True, stop=True))
            for oc in range(32):
                par = oc % 2
                @block.sync
                def _(sp, oc=oc, par=par):
                    C.wait(sp, "sp", C.dma_total("rsb0"))
                    C.wait(sp, "sp", C.dma_total("rsb1"))
                    if oc >= 2:
                        C.wait(sp, "sp", M[f"norm0_{oc-2}"])
                    M[f"rrd0_{oc}"] = C.dma(f"rrd{par}", sp.dma_start(out=r_sb[:, par], in_=r_scr[128 * oc:128 * (oc + 1), :]))
                @block.vector
                def _(dv, oc=oc, par=par):
                    C.wait(dv, "dve", M[f"rrd0_{oc}"])
                    C.wait(dv, "dve", M["bc0"])
                    if oc >= 2:
                        C.wait(dv, "dve", M[f"xst0_{oc-2}"])
                    dv.tensor_tensor(xn0[:, par], r_sb[:, par], pA[:], OP.subtract)
                    M[f"norm0_{oc}"] = C.bump("dve", dv.tensor_tensor(xn0[:, par], xn0[:, par], pB[:], OP.mult))
                @block.sync
                def _(sp, oc=oc, par=par):
                    C.wait(sp, "sp", M[f"norm0_{oc}"])
                    M[f"xst0_{oc}"] = C.dma(f"xsc{par}", sp.dma_start(out=x2_scr[128 * oc:128 * (oc + 1), :], in_=xn0[:, par]))
            C.join(block)

        with contextlib.ExitStack() as ps:
            x2bf = ps.enter_context(nc.sbuf_tensor("x2bf", [128, 32, TLOC], BF16))
            wqb = ps.enter_context(nc.sbuf_tensor("wqb", [128, 2, 32, 128], BF16))
            evq = ps.enter_context(nc.sbuf_tensor("evq", [128, 2, TLOC], BF16))
            pq = ps.enter_context(nc.psum_tensor("pq", [128, TLOC], F32))

            @block.gpsimd
            def _(gp):
                C.dma("gcq", gp.dma_start(out=x2bf[:], in_=x2_scr.rearrange("(e p) t -> p e t", p=128)))
            GCQ = C.dma_total("gcq")
            for fc in range(34):
                par = fc % 2
                @block.gpsimd
                def _(gp, fc=fc, par=par):
                    if fc >= 2:
                        C.wait(gp, "pool", M[f"qmm_{fc-2}"])
                    M[f"qwl_{fc}"] = C.dma(f"qwl{par}", gp.dma_start(
                        out=wqb[:, par],
                        in_=w_qkv[:, 128 * fc:128 * (fc + 1)].rearrange("(e p) j -> p e j", p=128)))
                @block.tensor
                def _(te, fc=fc, par=par):
                    C.wait(te, "pe", M[f"qwl_{fc}"])
                    if fc == 0:
                        C.wait(te, "pe", GCQ)
                    if fc >= 1:
                        C.wait(te, "pe", M[f"qev_{fc-1}"])
                    last = None
                    for ec in range(32):
                        te.ldweights(wqb[:, par, ec])
                        te.matmul(pq[:, 0:512], wqb[:, par, ec], x2bf[:, ec, 0:512], start=(ec == 0), stop=(ec == 31))
                        last = te.matmul(pq[:, 512:1024], wqb[:, par, ec], x2bf[:, ec, 512:1024], start=(ec == 0), stop=(ec == 31))
                    M[f"qmm_{fc}"] = C.bump("pe", last)
                @block.scalar
                def _(sc, fc=fc, par=par):
                    C.wait(sc, "act", M[f"qmm_{fc}"])
                    if fc >= 2:
                        C.wait(sc, "act", M[f"qst_{fc-2}"])
                    M[f"qev_{fc}"] = C.bump("act", sc.copy(evq[:, par], pq[:]))
                @block.sync
                def _(sp, fc=fc, par=par):
                    C.wait(sp, "sp", M[f"qev_{fc}"])
                    if fc < 32:
                        row = 512 * (fc % 8) + 128 * (fc // 8)
                        M[f"qst_{fc}"] = C.dma(f"qst{par}", sp.dma_start(out=q_in[row:row + 128, :], in_=evq[:, par]))
                    else:
                        M[f"qst_{fc}"] = C.dma(f"qst{par}", sp.dma_start(
                            out=kv_in[128 * (fc - 32):128 * (fc - 31), :], in_=evq[:, par]))

            @block.gpsimd
            def _(gp):
                C.wait(gp, "pool", C.dma_total("qst0"))
                C.wait(gp, "pool", C.dma_total("qst1"))
                M["cc_q"] = C.bump("cc", gp.collective_compute(
                    "AllToAll", OP.bypass, replica_groups=RG,
                    ins=[q_in.tensor.ap().opt()], outs=[q_out.tensor.ap().opt()]))
                M["cc_kv"] = C.bump("cc", gp.collective_compute(
                    "AllGather", OP.bypass, replica_groups=RG,
                    ins=[kv_in.tensor.ap().opt()], outs=[kv_out.tensor.ap().opt()]))
            C.join(block)

        with contextlib.ExitStack() as ps:
            NQT = 16
            SCL = 128 ** -0.5
            qsb = ps.enter_context(nc.sbuf_tensor("qsb", [128, 4, T], BF16))
            ksb = ps.enter_context(nc.sbuf_tensor("ksb", [128, 64, 128], BF16))
            vtb = ps.enter_context(nc.sbuf_tensor("vtb", [128, 64, 128], BF16))
            vsb = ps.enter_context(nc.sbuf_tensor("vsb", [128, 64, 128], BF16))
            kcb = ps.enter_context(nc.sbuf_tensor("kcb", [128, 64], F32))
            mkb = ps.enter_context(nc.sbuf_tensor("mkb", [128, 4, 512], F32))
            slps = ps.enter_context(nc.sbuf_tensor("slps", [128, 4], F32))
            qrc = ps.enter_context(nc.sbuf_tensor("qrc", [128, 4, 512], F32))
            o128 = ps.enter_context(nc.sbuf_tensor("o128", [128, 128], BF16))
            slpk2 = ps.enter_context(nc.sbuf_tensor("slpk2", [128, 2, 64], F32))
            qtm = ps.enter_context(nc.sbuf_tensor("qtm", [128, 1], F32))
            qbq = ps.enter_context(nc.sbuf_tensor("qbq", [128, 2, 512], F32))
            probs = ps.enter_context(nc.sbuf_tensor("probs", [128, 4, 512], BF16))
            asb = ps.enter_context(nc.sbuf_tensor("asb", [128, 2, 512], F32))
            ssb = ps.enter_context(nc.sbuf_tensor("ssb", [128, 2, 512], F32))
            rrec = ps.enter_context(nc.sbuf_tensor("rrec", [128, 2, 512], F32))
            ob = ps.enter_context(nc.sbuf_tensor("ob", [128, 2, 512], BF16))
            pqk = ps.enter_context(nc.psum_tensor("pqk", [128, 4, 512], F32))
            pattn = ps.enter_context(nc.psum_tensor("pattn", [128, 512], F32))
            pbcS = ps.enter_context(nc.psum_tensor("pbcS", [128, 512], F32))
            pv = ps.enter_context(nc.psum_tensor("pv", [128, 2, 1024], BF16))

            @block.sync
            def _(sp):
                for k in range(32):
                    ow, gg = owner(k), gidx(k)
                    for hh in range(4):
                        C.dma("aql", sp.dma_start(
                            out=qsb[:, hh, 256 * k:256 * (k + 1)],
                            in_=q_out[512 * ow + 128 * hh:512 * ow + 128 * (hh + 1), 256 * gg:256 * (gg + 1)]))
                    C.dma("aql", sp.dma_start(
                        out=ksb[:].rearrange("p kt j -> p (kt j)")[:, 256 * k:256 * (k + 1)],
                        in_=kv_out[256 * ow:256 * ow + 128, 256 * gg:256 * (gg + 1)]))
                    C.dma("aql", sp.dma_start(
                        out=vtb[:].rearrange("p kt j -> p (kt j)")[:, 256 * k:256 * (k + 1)],
                        in_=kv_out[256 * ow + 128:256 * ow + 256, 256 * gg:256 * (gg + 1)]))
                C.dma("cst3", sp.dma_start(out=kcb[:], in_=kcol[:]))
                C.dma("cst3", sp.dma_start(out=mkb[:], in_=maskc[:]))
                C.dma("cst3", sp.dma_start(out=slps[:], in_=slopes_c[:]))
                C.dma("cst3", sp.dma_start(out=qrc[:].rearrange("p a b -> p (a b)"), in_=qrefc[:].rearrange("p a b -> p (a b)")))
                C.dma("cst3", sp.dma_start(out=o128[:], in_=ones128b[:]))
            AQL = C.dma_total("aql")
            CST3 = C.dma_total("cst3")

            actV_base = C.ecnt.get("act", 0)
            @block.tensor
            def _(te):
                C.wait(te, "pe", AQL)
                C.wait(te, "pe", CST0)
                for kt in range(64):
                    par = kt % 2
                    if kt >= 2:
                        C.wait(te, "pe", ("E", "act", actV_base + kt - 1))
                    M[f"vtr_{kt}"] = C.bump("pe", te.transpose(pv[:, par, 0:128], vtb[:, kt], idb[:]))
            @block.scalar
            def _(sc):
                for kt in range(64):
                    par = kt % 2
                    C.wait(sc, "act", M[f"vtr_{kt}"])
                    M[f"vev_{kt}"] = C.bump("act", sc.copy(vsb[:, kt], pv[:, par, 0:128]))
            assert C.ecnt["act"] == actV_base + 64

            pqk_free = {}
            for hh in range(4):
                sph = hh % 2
                @block.vector
                def _(dv, hh=hh, sph=sph):
                    C.wait(dv, "dve", CST3)
                    C.self_edge("dve", dv, dv.tensor_scalar(
                        slpk2[:, sph], kcb[:], slps[:, hh:hh + 1], None, OP.mult))

                for qt in range(NQT):
                    q0 = 512 * qt
                    nk = q0 // 128 + 4
                    qp = (hh * NQT + qt) % 2
                    act_base = C.ecnt["act"]
                    pe_base = C.ecnt["pe"]
                    ops = []
                    for kti in range(nk + 1):
                        if kti < nk:
                            ops.append(("qk", kti))
                        if kti >= 1:
                            ops.append(("pv", kti - 1))
                    aqk_val = {kt: pe_base + ops.index(("qk", kt)) + 1 for kt in range(nk)}
                    apr_val = {kt: act_base + kt + 1 for kt in range(nk)}
                    aaev_val = act_base + nk + 1
                    stt_ms = {}
                    apv_ms = {}
                    prev_afin = M.get(f"afin_{qp}")

                    @block.vector
                    def _(dv, hh=hh, q0=q0, qp=qp, nk=nk, sph=sph, aqk_val=aqk_val,
                          aaev_val=aaev_val, stt_ms=stt_ms):
                        C.self_edge("dve", dv, dv.tensor_scalar(
                            qtm[:], slps[:, hh:hh + 1], float(1 - q0), None, OP.mult))
                        dv.tensor_scalar(qbq[:, qp], qrc[:, hh], qtm[:, 0:1], None, OP.add)
                        for kt in range(nk):
                            par = kt % 4
                            diag = kt - (nk - 4)
                            C.wait(dv, "dve", ("E", "pe", aqk_val[kt]))
                            last = dv.scalar_tensor_tensor(
                                pqk[:, par], pqk[:, par], slpk2[:, sph, kt:kt + 1], qbq[:, qp], OP.add, OP.add)
                            if diag >= 0:
                                last = dv.tensor_tensor(pqk[:, par], pqk[:, par], mkb[:, diag], OP.add)
                            stt_ms[kt] = C.bump("dve", last)
                        C.wait(dv, "dve", ("E", "pe", pe_base + 2 * nk))
                        M[f"sclamp_{qp}"] = C.bump("dve", dv.tensor_scalar(
                            ssb[:, qp], pbcS[:], 1e-35, None, OP.max))
                        C.self_edge("dve", dv, dv.reciprocal(rrec[:, qp], ssb[:, qp]))
                        C.wait(dv, "dve", ("E", "act", aaev_val))
                        if f"aost_{qp}" in M:
                            C.wait(dv, "dve", M[f"aost_{qp}"])
                        M[f"afin_{qp}"] = C.bump("dve", dv.tensor_tensor(
                            ob[:, qp], asb[:, qp], rrec[:, qp], OP.mult))

                    @block.tensor
                    def _(te, hh=hh, q0=q0, qp=qp, nk=nk, ops=ops, apr_val=apr_val, apv_ms=apv_ms):
                        for what, kt in ops:
                            if what == "qk":
                                par = kt % 4
                                if kt >= 4:
                                    C.wait(te, "pe", ("E", "act", apr_val[kt - 4]))
                                elif par in pqk_free:
                                    C.wait(te, "pe", pqk_free[par])
                                te.ldweights(ksb[:, kt])
                                C.bump("pe", te.matmul(
                                    pqk[:, par], ksb[:, kt], qsb[:, hh, q0:q0 + 512], start=True, stop=True))
                            else:
                                par = kt % 4
                                C.wait(te, "pe", ("E", "act", apr_val[kt]))
                                st = (kt == 0)
                                sto = (kt == nk - 1)
                                if st:
                                    if "aaev_last" in M:
                                        C.wait(te, "pe", M["aaev_last"])
                                    if "sclamp_last" in M:
                                        C.wait(te, "pe", M["sclamp_last"])
                                te.ldweights(vsb[:, kt])
                                te.matmul(pattn[:], vsb[:, kt], probs[:, par], start=st, stop=sto)
                                te.ldweights(o128[:])
                                apv_ms[kt] = C.bump("pe", te.matmul(
                                    pbcS[:], o128[:], probs[:, par], start=st, stop=sto))
                    assert C.ecnt["pe"] == pe_base + 2 * nk, (C.ecnt["pe"], pe_base, nk)
                    for i in range(4):
                        pqk_free[(nk - 1 - i) % 4] = ("E", "act", apr_val[nk - 1 - i])
                    M["sclamp_last"] = M[f"sclamp_{qp}"]

                    @block.scalar
                    def _(sc, qp=qp, nk=nk, stt_ms=stt_ms, apv_ms=apv_ms, prev_afin=prev_afin):
                        for kt in range(nk):
                            par = kt % 4
                            C.wait(sc, "act", stt_ms[kt])
                            if kt >= 4:
                                C.wait(sc, "act", apv_ms[kt - 4])
                            elif f"probs_free_{par}" in M:
                                C.wait(sc, "act", M[f"probs_free_{par}"])
                            C.bump("act", sc.activation(
                                probs[:, par], pqk[:, par], AF.Exp, scale=SCL))
                        C.wait(sc, "act", apv_ms[nk - 1])
                        if prev_afin is not None:
                            C.wait(sc, "act", prev_afin)
                        M["aaev_last"] = M[f"aaev_{qp}"] = C.bump("act", sc.copy(asb[:, qp], pattn[:]))
                    assert C.ecnt["act"] == aaev_val, (C.ecnt["act"], aaev_val)
                    for i in range(4):
                        M[f"probs_free_{(nk - 1 - i) % 4}"] = apv_ms[nk - 1 - i]

                    @block.sync
                    def _(sp, hh=hh, qt=qt, qp=qp):
                        C.wait(sp, "sp", M[f"afin_{qp}"])
                        for ci in range(2):
                            k = 2 * qt + ci
                            ow, gg = owner(k), gidx(k)
                            C.dma(f"aost{qp}", sp.dma_start(
                                out=at_in[512 * ow + 128 * hh:512 * ow + 128 * (hh + 1), 256 * gg:256 * (gg + 1)],
                                in_=ob[:, qp, 256 * ci:256 * (ci + 1)]))
                        M[f"aost_{qp}"] = C.dma_total(f"aost{qp}")

            @block.gpsimd
            def _(gp):
                C.wait(gp, "pool", C.dma_total("aost0"))
                C.wait(gp, "pool", C.dma_total("aost1"))
                M["cc_at"] = C.bump("cc", gp.collective_compute(
                    "AllToAll", OP.bypass, replica_groups=RG,
                    ins=[at_in.tensor.ap().opt()], outs=[at_out.tensor.ap().opt()]))
            C.join(block)

        with contextlib.ExitStack() as ps:
            abf = ps.enter_context(nc.sbuf_tensor("abf", [128, 32, TLOC], BF16))
            wbufW = ps.enter_context(nc.sbuf_tensor("wbufW", [128, 2, 32, 128], BF16))
            x2t = ps.enter_context(nc.sbuf_tensor("x2tW", [128, 2, TLOC], F32))
            r_sb = ps.enter_context(nc.sbuf_tensor("r_sbW", [128, 2, TLOC], F32))
            sqW = ps.enter_context(nc.sbuf_tensor("sqW", [128, TLOC], F32))
            rrowW = ps.enter_context(nc.sbuf_tensor("rrowW", [1, 6, TLOC], F32))
            xnW = ps.enter_context(nc.sbuf_tensor("xnW", [128, 2, TLOC], F32))
            pW = ps.enter_context(nc.psum_tensor("pWW", [128, TLOC], F32))
            psum_s = ps.enter_context(nc.psum_tensor("pSW", [1, TLOC], F32))
            psum_q = ps.enter_context(nc.psum_tensor("pQW", [1, TLOC], F32))
            pbcW = ps.enter_context(nc.psum_tensor("pbcW", [128, TLOC], F32))

            @block.sync
            def _(sp):
                for g in range(32):
                    C.dma("awl", sp.dma_start(
                        out=abf[:, g],
                        in_=at_out[512 * (g % 8) + 128 * (g // 8):512 * (g % 8) + 128 * (g // 8) + 128, :]))
            AWL = C.dma_total("awl")

            for hc in range(32):
                par = hc % 2
                @block.gpsimd
                def _(gp, hc=hc, par=par):
                    if hc >= 2:
                        C.wait(gp, "pool", M[f"wmm_{hc-2}"])
                    M[f"wwl_{hc}"] = C.dma(f"wwl{par}", gp.dma_start(
                        out=wbufW[:, par],
                        in_=w_c[:, 128 * hc:128 * (hc + 1)].rearrange("(e p) j -> p e j", p=128)))
                @block.sync
                def _(sp, hc=hc, par=par):
                    if hc >= 2:
                        C.wait(sp, "sp", M[f"residW_{hc-2}"])
                    M[f"x2l_{hc}"] = C.dma(f"x2l{par}", sp.dma_start(out=x2t[:, par], in_=x2_scr[128 * hc:128 * (hc + 1), :]))
                @block.tensor
                def _(te, hc=hc, par=par):
                    C.wait(te, "pe", M[f"wwl_{hc}"])
                    if hc == 0:
                        C.wait(te, "pe", AWL)
                    if hc >= 1:
                        C.wait(te, "pe", M[f"residW_{hc-1}"])
                    last = None
                    for ec in range(32):
                        te.ldweights(wbufW[:, par, ec])
                        te.matmul(pW[:, 0:512], wbufW[:, par, ec], abf[:, ec, 0:512], start=(ec == 0), stop=(ec == 31))
                        last = te.matmul(pW[:, 512:1024], wbufW[:, par, ec], abf[:, ec, 512:1024], start=(ec == 0), stop=(ec == 31))
                    M[f"wmm_{hc}"] = C.bump("pe", last)
                @block.vector
                def _(dv, hc=hc, par=par):
                    C.wait(dv, "dve", M[f"wmm_{hc}"])
                    C.wait(dv, "dve", M[f"x2l_{hc}"])
                    if hc >= 2:
                        C.wait(dv, "dve", M[f"rstW_{hc-2}"])
                    M[f"residW_{hc}"] = C.bump("dve", dv.tensor_tensor(r_sb[:, par], pW[:], x2t[:, par], OP.add))
                @block.scalar
                def _(sc, hc=hc, par=par):
                    C.wait(sc, "act", M[f"residW_{hc}"])
                    if hc >= 1:
                        C.wait(sc, "act", M[f"statW_{hc-1}"])
                    M[f"sqW_{hc}"] = C.bump("act", sc.activation(sqW[:], r_sb[:, par], AF.Square))
                @block.tensor
                def _(te, hc=hc, par=par):
                    C.wait(te, "pe", M[f"sqW_{hc}"])
                    te.matmul(psum_s[:, 0:512], onc[:, 0:1], r_sb[:, par, 0:512], start=(hc == 0), stop=(hc == 31))
                    te.matmul(psum_s[:, 512:1024], onc[:, 0:1], r_sb[:, par, 512:1024], start=(hc == 0), stop=(hc == 31))
                    te.matmul(psum_q[:, 0:512], onc[:, 0:1], sqW[:, 0:512], start=(hc == 0), stop=(hc == 31))
                    M[f"statW_{hc}"] = C.bump("pe", te.matmul(psum_q[:, 512:1024], onc[:, 0:1], sqW[:, 512:1024], start=(hc == 0), stop=(hc == 31)))
                @block.sync
                def _(sp, hc=hc, par=par):
                    C.wait(sp, "sp", M[f"statW_{hc}"])
                    M[f"rstW_{hc}"] = C.dma(f"rsw{par}", sp.dma_start(out=r_scr[128 * hc:128 * (hc + 1), :], in_=r_sb[:, par]))

            @block.vector
            def _(dv):
                C.wait(dv, "dve", M["statW_31"])
                C.self_edge("dve", dv, dv.tensor_scalar(rrowW[:, 0], psum_s[:], 1.0 / H, None, OP.mult))
                C.self_edge("dve", dv, dv.tensor_scalar(rrowW[:, 1], psum_q[:], 1.0 / H, None, OP.mult))
                C.self_edge("dve", dv, dv.tensor_tensor(rrowW[:, 2], rrowW[:, 0], rrowW[:, 0], OP.mult))
                C.self_edge("dve", dv, dv.tensor_tensor(rrowW[:, 3], rrowW[:, 1], rrowW[:, 2], OP.subtract))
                M["varW"] = C.bump("dve", dv.tensor_scalar(rrowW[:, 3], rrowW[:, 3], 1.0, 1e-5, OP.mult, OP.add))
            @block.scalar
            def _(sc):
                C.wait(sc, "act", M["varW"])
                M["sqrtW"] = C.bump("act", sc.activation(rrowW[:, 4], rrowW[:, 3], AF.Sqrt))
            @block.vector
            def _(dv):
                C.wait(dv, "dve", M["sqrtW"])
                M["rstdW"] = C.self_edge("dve", dv, dv.reciprocal(rrowW[:, 5], rrowW[:, 4]))
            @block.tensor
            def _(te):
                C.wait(te, "pe", M["rstdW"])
                te.matmul(pbcW[:, 0:512], onr[:], rrowW[:, 5, 0:512], start=True, stop=True)
                te.matmul(pbcW[:, 512:1024], onr[:], rrowW[:, 5, 512:1024], start=True, stop=True)
                te.matmul(pW[:, 0:512], onr[:], rrowW[:, 0, 0:512], start=True, stop=True)
                M["bcW"] = C.bump("pe", te.matmul(pW[:, 512:1024], onr[:], rrowW[:, 0, 512:1024], start=True, stop=True))
            for hc in range(32):
                par = hc % 2
                @block.sync
                def _(sp, hc=hc, par=par):
                    C.wait(sp, "sp", C.dma_total("rsw0"))
                    C.wait(sp, "sp", C.dma_total("rsw1"))
                    if hc >= 2:
                        C.wait(sp, "sp", M[f"normW_{hc-2}"])
                    M[f"rrdW_{hc}"] = C.dma(f"rdw{par}", sp.dma_start(out=r_sb[:, par], in_=r_scr[128 * hc:128 * (hc + 1), :]))
                @block.vector
                def _(dv, hc=hc, par=par):
                    C.wait(dv, "dve", M[f"rrdW_{hc}"])
                    C.wait(dv, "dve", M["bcW"])
                    if hc >= 2:
                        C.wait(dv, "dve", M[f"xstW_{hc-2}"])
                    dv.tensor_tensor(xnW[:, par], r_sb[:, par], pW[:], OP.subtract)
                    M[f"normW_{hc}"] = C.bump("dve", dv.tensor_tensor(xnW[:, par], xnW[:, par], pbcW[:], OP.mult))
                @block.sync
                def _(sp, hc=hc, par=par):
                    C.wait(sp, "sp", M[f"normW_{hc}"])
                    M[f"xstW_{hc}"] = C.dma(f"xsw{par}", sp.dma_start(out=x3_scr[128 * hc:128 * (hc + 1), :], in_=xnW[:, par]))
            C.join(block)

        with contextlib.ExitStack() as ps:
            x3bf = ps.enter_context(nc.sbuf_tensor("x3bf", [128, 32, TLOC], BF16))
            rwb1 = ps.enter_context(nc.sbuf_tensor("rwb1", [128, 32, 64], BF16))
            rbb = ps.enter_context(nc.sbuf_tensor("rbb", [64, 1], F32))
            ebb = ps.enter_context(nc.sbuf_tensor("ebb", [64, 32, 128], BF16))
            lgTF = ps.enter_context(nc.sbuf_tensor("lgTF", [64, TLOC], F32))
            lgF = ps.enter_context(nc.sbuf_tensor("lgF", [128, 8, 64], F32))
            esbF = ps.enter_context(nc.sbuf_tensor("esbF", [128, 8, 64], F32))
            wdd = ps.enter_context(nc.sbuf_tensor("wdd", [128, 8, 65], F32))
            top8F = ps.enter_context(nc.sbuf_tensor("top8F", [128, 8, 8], F32))
            mrowF = ps.enter_context(nc.sbuf_tensor("mrowF", [128, 8, 4], F32))
            wdTF = ps.enter_context(nc.sbuf_tensor("wdTF", [65, TLOC], BF16))
            wsrow = ps.enter_context(nc.sbuf_tensor("wsrow", [1, TLOC], BF16))
            msbF = ps.enter_context(nc.sbuf_tensor("msbF", [128, TLOC], F32))
            x3f = ps.enter_context(nc.sbuf_tensor("x3fF", [128, 2, TLOC], F32))
            r_sb = ps.enter_context(nc.sbuf_tensor("r_sbF", [128, 2, TLOC], F32))
            sqF = ps.enter_context(nc.sbuf_tensor("sqF", [128, TLOC], F32))
            rrowF = ps.enter_context(nc.sbuf_tensor("rrowF", [1, 6, TLOC], F32))
            xnF = ps.enter_context(nc.sbuf_tensor("xnF", [128, 2, TLOC], F32))
            pA = ps.enter_context(nc.psum_tensor("pAF", [128, TLOC], F32))
            pB = ps.enter_context(nc.psum_tensor("pBF", [128, TLOC], F32))
            pS = ps.enter_context(nc.psum_tensor("pSF", [1, TLOC], F32))
            pQ = ps.enter_context(nc.psum_tensor("pQF", [1, TLOC], F32))

            @block.gpsimd
            def _(gp):
                C.dma("gcf", gp.dma_start(out=x3bf[:], in_=x3_scr.rearrange("(e p) t -> p e t", p=128)))
                C.dma("gcf", gp.dma_start(out=rwb1[:], in_=rwT1.rearrange("(e p) j -> p e j", p=128)))
                C.dma("gcf", gp.dma_start(out=ebb[:], in_=ebias.rearrange("p (o j) -> p o j", j=128)))
            @block.sync
            def _(sp):
                C.dma("cst4", sp.dma_start(out=rbb[:], in_=rb1[:]))
            GCF = C.dma_total("gcf")
            CST4 = C.dma_total("cst4")

            @block.tensor
            def _(te):
                C.wait(te, "pe", GCF)
                last = None
                for ec in range(32):
                    te.ldweights(rwb1[:, ec])
                    te.matmul(pA[0:64, 0:512], rwb1[:, ec], x3bf[:, ec, 0:512], start=(ec == 0), stop=(ec == 31))
                    last = te.matmul(pA[0:64, 512:1024], rwb1[:, ec], x3bf[:, ec, 512:1024], start=(ec == 0), stop=(ec == 31))
                M["flg_mm"] = C.bump("pe", last)
            @block.vector
            def _(dv):
                C.wait(dv, "dve", M["flg_mm"])
                C.wait(dv, "dve", CST4)
                M["flg_ev"] = C.bump("dve", dv.tensor_scalar(lgTF[:], pA[0:64, :], rbb[:, 0:1], None, OP.add))
            @block.tensor
            def _(te):
                C.wait(te, "pe", M["flg_ev"])
                last = None
                for ti in range(8):
                    last = te.transpose(pB[0:128, 64 * ti:64 * (ti + 1)], lgTF[:, 128 * ti:128 * (ti + 1)], idt[0:64, 0:64])
                M["flg_tr"] = C.bump("pe", last)
            @block.scalar
            def _(sc):
                C.wait(sc, "act", M["flg_tr"])
                M["flg2"] = C.bump("act", sc.copy(lgF[:].rearrange("p a b -> p (a b)"), pB[0:128, 0:512]))

            for ti in range(8):
                @block.vector
                def _(dv, ti=ti):
                    C.wait(dv, "dve", M["flg2"])
                    C.self_edge("dve", dv, dv.reduce_max(mrowF[:, ti, 0:1], lgF[:, ti], axis=AX.X))
                    M[f"fmx_{ti}"] = C.bump("dve", dv.tensor_scalar(mrowF[:, ti, 1:2], mrowF[:, ti, 0:1], -1.0, None, OP.mult))
                @block.scalar
                def _(sc, ti=ti):
                    C.wait(sc, "act", M[f"fmx_{ti}"])
                    M[f"fe_{ti}"] = C.bump("act", sc.activation(esbF[:, ti], lgF[:, ti], AF.Exp, bias=mrowF[:, ti, 1:2]))
                @block.vector
                def _(dv, ti=ti):
                    C.wait(dv, "dve", M[f"fe_{ti}"])
                    C.self_edge("dve", dv, dv.reduce_sum(mrowF[:, ti, 2:3], esbF[:, ti], axis=AX.X))
                    C.self_edge("dve", dv, dv.reciprocal(mrowF[:, ti, 3:4], mrowF[:, ti, 2:3]))
                    C.self_edge("dve", dv, dv.tensor_scalar(esbF[:, ti], esbF[:, ti], mrowF[:, ti, 3:4], None, OP.mult))
                    C.self_edge("dve", dv, dv.max(top8F[:, ti], esbF[:, ti]))
                    C.self_edge("dve", dv, dv.reduce_sum(wdd[:, ti, 64:65], top8F[:, ti], axis=AX.X))
                    C.self_edge("dve", dv, dv.tensor_scalar(wdd[:, ti, 0:64], esbF[:, ti], top8F[:, ti, 7:8], None, OP.is_ge))
                    M[f"fwd_{ti}"] = C.bump("dve", dv.tensor_tensor(wdd[:, ti, 0:64], wdd[:, ti, 0:64], esbF[:, ti], OP.mult))

            @block.tensor
            def _(te):
                for ti in range(8):
                    C.wait(te, "pe", M[f"fwd_{ti}"])
                last = None
                for ti in range(8):
                    last = te.transpose(pA[0:65, 128 * ti:128 * (ti + 1)], wdd[:, ti], idt[:])
                M["fwd_tr"] = C.bump("pe", last)
            @block.scalar
            def _(sc):
                C.wait(sc, "act", M["fwd_tr"])
                sc.copy(wdTF[:], pA[0:65, :])
                M["fwdT_ev"] = C.bump("act", sc.copy(wsrow[:], pA[64:65, :]))

            for hc in range(32):
                par = hc % 2
                @block.sync
                def _(sp, hc=hc, par=par):
                    if hc >= 2:
                        C.wait(sp, "sp", M[f"residF_{hc-2}"])
                    M[f"x3l_{hc}"] = C.dma(f"x3l{par}", sp.dma_start(out=x3f[:, par], in_=x3_scr[128 * hc:128 * (hc + 1), :]))
                @block.tensor
                def _(te, hc=hc, par=par):
                    C.wait(te, "pe", M["fwdT_ev"])
                    if hc >= 1:
                        C.wait(te, "pe", M[f"fmev_{hc-1}"])
                        C.wait(te, "pe", M[f"residF_{hc-1}"])
                    te.ldweights(ebb[:, hc])
                    te.matmul(pB[:, 0:512], ebb[:, hc], wdTF[0:64, 0:512], start=True, stop=True)
                    M[f"fmix_{hc}"] = C.bump("pe", te.matmul(pB[:, 512:1024], ebb[:, hc], wdTF[0:64, 512:1024], start=True, stop=True))
                    te.ldweights(onrb[:])
                    te.matmul(pA[:, 0:512], onrb[:], wsrow[:, 0:512], start=True, stop=True)
                    M[f"fws_{hc}"] = C.bump("pe", te.matmul(pA[:, 512:1024], onrb[:], wsrow[:, 512:1024], start=True, stop=True))
                @block.scalar
                def _(sc, hc=hc):
                    C.wait(sc, "act", M[f"fmix_{hc}"])
                    if hc >= 1:
                        C.wait(sc, "act", M[f"residF_{hc-1}"])
                    M[f"fmev_{hc}"] = C.bump("act", sc.copy(msbF[:], pB[:]))
                @block.vector
                def _(dv, hc=hc, par=par):
                    C.wait(dv, "dve", M[f"fws_{hc}"])
                    C.wait(dv, "dve", M[f"fmev_{hc}"])
                    C.wait(dv, "dve", M[f"x3l_{hc}"])
                    if hc >= 2:
                        C.wait(dv, "dve", M[f"rstF_{hc-2}"])
                    dv.tensor_tensor(r_sb[:, par], x3f[:, par], pA[:], OP.mult)
                    dv.tensor_tensor(r_sb[:, par], r_sb[:, par], msbF[:], OP.add)
                    M[f"residF_{hc}"] = C.bump("dve", dv.tensor_tensor(r_sb[:, par], r_sb[:, par], x3f[:, par], OP.add))
                @block.scalar
                def _(sc, hc=hc, par=par):
                    C.wait(sc, "act", M[f"residF_{hc}"])
                    if hc >= 1:
                        C.wait(sc, "act", M[f"statF_{hc-1}"])
                    M[f"sqF_{hc}"] = C.bump("act", sc.activation(sqF[:], r_sb[:, par], AF.Square))
                @block.tensor
                def _(te, hc=hc, par=par):
                    C.wait(te, "pe", M[f"sqF_{hc}"])
                    te.matmul(pS[:, 0:512], onc[:, 0:1], r_sb[:, par, 0:512], start=(hc == 0), stop=(hc == 31))
                    te.matmul(pS[:, 512:1024], onc[:, 0:1], r_sb[:, par, 512:1024], start=(hc == 0), stop=(hc == 31))
                    te.matmul(pQ[:, 0:512], onc[:, 0:1], sqF[:, 0:512], start=(hc == 0), stop=(hc == 31))
                    M[f"statF_{hc}"] = C.bump("pe", te.matmul(pQ[:, 512:1024], onc[:, 0:1], sqF[:, 512:1024], start=(hc == 0), stop=(hc == 31)))
                @block.sync
                def _(sp, hc=hc, par=par):
                    C.wait(sp, "sp", M[f"statF_{hc}"])
                    M[f"rstF_{hc}"] = C.dma(f"rsf{par}", sp.dma_start(out=r_scr[128 * hc:128 * (hc + 1), :], in_=r_sb[:, par]))

            @block.vector
            def _(dv):
                C.wait(dv, "dve", M["statF_31"])
                C.self_edge("dve", dv, dv.tensor_scalar(rrowF[:, 0], pS[:], 1.0 / H, None, OP.mult))
                C.self_edge("dve", dv, dv.tensor_scalar(rrowF[:, 1], pQ[:], 1.0 / H, None, OP.mult))
                C.self_edge("dve", dv, dv.tensor_tensor(rrowF[:, 2], rrowF[:, 0], rrowF[:, 0], OP.mult))
                C.self_edge("dve", dv, dv.tensor_tensor(rrowF[:, 3], rrowF[:, 1], rrowF[:, 2], OP.subtract))
                M["varF"] = C.bump("dve", dv.tensor_scalar(rrowF[:, 3], rrowF[:, 3], 1.0, 1e-5, OP.mult, OP.add))
            @block.scalar
            def _(sc):
                C.wait(sc, "act", M["varF"])
                M["sqrtF"] = C.bump("act", sc.activation(rrowF[:, 4], rrowF[:, 3], AF.Sqrt))
            @block.vector
            def _(dv):
                C.wait(dv, "dve", M["sqrtF"])
                M["rstdF"] = C.self_edge("dve", dv, dv.reciprocal(rrowF[:, 5], rrowF[:, 4]))
            @block.tensor
            def _(te):
                C.wait(te, "pe", M["rstdF"])
                te.matmul(pB[:, 0:512], onr[:], rrowF[:, 5, 0:512], start=True, stop=True)
                te.matmul(pB[:, 512:1024], onr[:], rrowF[:, 5, 512:1024], start=True, stop=True)
                te.matmul(pA[:, 0:512], onr[:], rrowF[:, 0, 0:512], start=True, stop=True)
                M["bcF"] = C.bump("pe", te.matmul(pA[:, 512:1024], onr[:], rrowF[:, 0, 512:1024], start=True, stop=True))
            for hc in range(32):
                par = hc % 2
                @block.sync
                def _(sp, hc=hc, par=par):
                    C.wait(sp, "sp", C.dma_total("rsf0"))
                    C.wait(sp, "sp", C.dma_total("rsf1"))
                    if hc >= 2:
                        C.wait(sp, "sp", M[f"normF_{hc-2}"])
                    M[f"rrdF_{hc}"] = C.dma(f"rdf{par}", sp.dma_start(out=r_sb[:, par], in_=r_scr[128 * hc:128 * (hc + 1), :]))
                @block.vector
                def _(dv, hc=hc, par=par):
                    C.wait(dv, "dve", M[f"rrdF_{hc}"])
                    C.wait(dv, "dve", M["bcF"])
                    if hc >= 2:
                        C.wait(dv, "dve", M[f"xstF_{hc-2}"])
                    dv.tensor_tensor(xnF[:, par], r_sb[:, par], pA[:], OP.subtract)
                    M[f"normF_{hc}"] = C.bump("dve", dv.tensor_tensor(xnF[:, par], xnF[:, par], pB[:], OP.mult))
                @block.sync
                def _(sp, hc=hc, par=par):
                    C.wait(sp, "sp", M[f"normF_{hc}"])
                    M[f"xstF_{hc}"] = C.dma(f"xff{par}", sp.dma_start(out=x4_out[128 * hc:128 * (hc + 1), :], in_=xnF[:, par]))

            @block.gpsimd
            def _(gp):
                C.wait(gp, "pool", C.dma_total("xff0"))
                C.wait(gp, "pool", C.dma_total("xff1"))
    return nc


def stage_fused(d):
    x0 = np.ascontiguousarray(d["L0_attn_hidden_states"].reshape(T, H).T)
    esl = np.zeros((NST, NST * 128), np.float32)
    for n in range(NST):
        esl[n, 128 * n:128 * (n + 1)] = 1.0
    rep = np.zeros((8, 128), np.float32)
    for e in range(8):
        rep[e, 16 * e:16 * (e + 1)] = 2.0
    A2 = d["L0_ffn_lora_A"].transpose(1, 0, 2).reshape(H, 128)
    maskc = np.where((128 * np.arange(4)[None, :, None] + np.arange(128)[:, None, None]) > np.arange(512)[None, None, :],
                     np.float32(-1e30), np.float32(0.0)).astype(np.float32)
    kcolv = (128 * np.arange(64)[None, :] + np.arange(128)[:, None]).astype(np.float32)
    slopes = np.asarray(alibi_slopes_py(32), np.float32)
    WdtT = np.ascontiguousarray(d["L0_attn_Wdt"].T)
    A = d["L0_attn_A"]
    dtbv = d["L0_attn_dt_bias"]
    Dpv = d["L0_attn_D_param"]
    common = dict(
        W_inT=np.ascontiguousarray(d["L0_attn_W_in"].T),
        WxT=np.ascontiguousarray(d["L0_attn_Wx"].T),
        conv_w=np.ascontiguousarray(d["L0_attn_conv_w"]),
        conv_b2=np.ascontiguousarray(d["L0_attn_conv_b"].reshape(64, 128).T),
        dtlnw2=np.ascontiguousarray(d["L0_attn_dt_ln_w"].reshape(2, 128).T),
        b_ln_w=np.ascontiguousarray(d["L0_attn_b_ln_w"]),
        c_ln_w=np.ascontiguousarray(d["L0_attn_c_ln_w"]),
        esel=esl.astype(bf16),
        W_outT=np.ascontiguousarray(d["L0_attn_W_out"].T),
        base_WT=np.ascontiguousarray(d["L0_ffn_base_W"].T),
        router_WT=np.ascontiguousarray(d["L0_ffn_router_W"].T),
        loraA2=np.ascontiguousarray(A2),
        loraB2=np.ascontiguousarray(d["L0_ffn_lora_B"].reshape(128, H)),
        rep16=rep,
        w_qkv=np.ascontiguousarray(d["L1_attn_w_qkv"]),
        w_c=np.ascontiguousarray(d["L1_attn_w_c"]),
        rwT1=np.ascontiguousarray(d["L1_ffn_router_weight"].T),
        rb1=np.ascontiguousarray(d["L1_ffn_router_bias"].reshape(64, 1)),
        ebias=np.ascontiguousarray(d["L1_ffn_expert_bias"]),
        ident=np.eye(128, dtype=np.float32),
        identb=np.eye(128, dtype=np.float32).astype(bf16),
        ones_col=np.ones((128, 1), np.float32),
        ones_row=np.ones((1, 128), np.float32),
        onesb=np.ones((128, 1), np.float32).astype(bf16),
        onesrb=np.ones((1, 128), np.float32).astype(bf16),
        kcol=kcolv,
        maskc=maskc,
        ones128b=np.ones((128, 128), np.float32).astype(bf16),
    )
    WS_ATTN = [64, 256, 512, 512]
    SCL_ATTN = 128 ** -0.5
    maps = []
    for c in range(NCORE):
        cols, pres = [], []
        for k in groups_of(c):
            cols.append(x0[:, G * k:G * (k + 1)])
            pres.append(np.zeros((H, 3), np.float32) if k == 0 else x0[:, G * k - 3:G * k])
        rows = slice(1024 * c, 1024 * (c + 1))
        Ac = A[rows].reshape(8, 128, NST).transpose(1, 0, 2).reshape(128, 8 * NST)
        heads = [c, 8 + c, 16 + c, 24 + c]
        maps.append(dict(
            common,
            x0T=np.ascontiguousarray(np.concatenate(cols, axis=1)),
            x0Tpre=np.ascontiguousarray(np.concatenate(pres + [np.zeros((H, 4), np.float32)], axis=1)),
            wdtT_c=np.ascontiguousarray(WdtT[:, rows]),
            A_c=np.ascontiguousarray(Ac.astype(np.float32)),
            dtb_c=np.ascontiguousarray(dtbv[rows].reshape(8, 128).T.astype(np.float32)),
            Dp_c=np.ascontiguousarray(Dpv[rows].reshape(8, 128).T.astype(np.float32)),
            slopes_c=np.ascontiguousarray(np.broadcast_to(
                (slopes[heads] / SCL_ATTN)[None, :], (128, 4)).astype(np.float32)),
            qrefc=np.ascontiguousarray(np.broadcast_to(np.stack([
                -(slopes[heads[s]] / SCL_ATTN) * ((np.arange(512) // WS_ATTN[s] + 1) * WS_ATTN[s]).astype(np.float64)
                for s in range(4)], axis=0).astype(np.float32)[None, :, :], (128, 4, 512))),
        ))
    return maps


def gather_tokens(per_core):
    rows = per_core[0].shape[0]
    out = np.empty((rows, T), per_core[0].dtype)
    for c in range(NCORE):
        for g, k in enumerate(groups_of(c)):
            out[:, G * k:G * (k + 1)] = per_core[c][:, G * g:G * (g + 1)]
    return out
_PROGS = {}

def kernel(**inputs):
    d = {k: np.ascontiguousarray(np.asarray(v)) for k, v in inputs.items()}
    if "nc" not in _PROGS:
        with allow_ici_all_to_all():
            _PROGS["nc"] = build_fused()
    nc = _PROGS["nc"]
    maps = stage_fused(d)
    res = run_bass_kernel_spmd(nc, maps, core_ids=list(range(NCORE)))
    x4_full = gather_tokens([res.results[c]["x4_out"] for c in range(NCORE)])
    return np.ascontiguousarray(x4_full.T).reshape(1, T, H).astype(np.float32)


# version 08 (final)
# speedup vs baseline: 1.4220x; 1.0637x over previous
import os as _os
import numpy as np
import ml_dtypes
import contextlib
import concourse.bass as bass
import concourse.mybir as mybir
from concourse.bass_utils import run_bass_kernel_spmd
from concourse.aot_env import allow_ici_all_to_all

F32 = mybir.dt.float32
BF16 = mybir.dt.bfloat16
AF = mybir.ActivationFunctionType
OP = mybir.AluOpType
AX = mybir.AxisListType
bf16 = ml_dtypes.bfloat16

NCORE = 8
T = 8192
H = 4096
D = 8192
NST = 16
KCV = 4
RDT = 256
TLOC = 1024
G = 256
NGRP = 4
RG = [list(range(NCORE))]

def groups_of(c):
    return [c, 15 - c, 16 + c, 31 - c]

def owner(k):
    if k < 8: return k
    if k < 16: return 15 - k
    if k < 24: return k - 16
    return 31 - k

def gidx(k):
    return 0 if k < 8 else (1 if k < 16 else (2 if k < 24 else 3))

def alibi_slopes_py(n=32):
    import math
    cp2 = 2 ** int(math.floor(math.log2(n)))
    base = 2.0 ** (-(2.0 ** (-(math.log2(cp2) - 3))))
    slopes = [base ** i for i in range(1, cp2 + 1)]
    if cp2 != n:
        extra_base = 2.0 ** (-(2.0 ** (-(math.log2(2 * cp2) - 3))))
        rem = min(cp2, n - cp2)
        slopes += [extra_base ** i for i in range(1, 1 + 2 * rem, 2)]
    return slopes


class Ctx:
    def __init__(self, nc, es):
        self.nc = nc
        self.es = es
        self.esem = {}
        self.ecnt = {}
        self.dsem = {}
        self.dcnt = {}
        self.waited = {}

    def bump(self, eng_name, instr, n=1):
        if eng_name not in self.esem:
            self.esem[eng_name] = self.es.enter_context(self.nc.semaphore(f"se_{eng_name}"))
            self.ecnt[eng_name] = 0
        self.ecnt[eng_name] += n
        instr.then_inc(self.esem[eng_name], n)
        return ("E", eng_name, self.ecnt[eng_name])

    def dma(self, stream, instr):
        if stream not in self.dsem:
            self.dsem[stream] = self.es.enter_context(self.nc.semaphore(f"sd_{stream}"))
            self.dcnt[stream] = 0
        self.dcnt[stream] += 16
        instr.then_inc(self.dsem[stream], 16)
        return ("D", stream, self.dcnt[stream])

    def wait(self, eng_obj, waiter_name, milestone):
        kind, key, val = milestone
        if kind == "E" and key == waiter_name:
            return
        wkey = (waiter_name, kind, key)
        if self.waited.get(wkey, -1) >= val:
            return
        sem = self.esem[key] if kind == "E" else self.dsem[key]
        eng_obj.wait_ge(sem, val)
        self.waited[wkey] = val

    def dma_total(self, stream):
        return ("D", stream, self.dcnt.get(stream, 0))

    def self_edge(self, eng_name, eng_obj, instr):
        m = self.bump(eng_name, instr)
        eng_obj.wait_ge(self.esem[eng_name], m[2])
        return m

    def join(self, block):
        snap_e = dict(self.ecnt)
        snap_d = dict(self.dcnt)
        for engname, attr in (("pe", "tensor"), ("act", "scalar"), ("dve", "vector"),
                              ("pool", "gpsimd"), ("sp", "sync")):
            def body(eng, engname=engname):
                for k2, v in snap_e.items():
                    if v and k2 != engname:
                        self.wait(eng, engname, ("E", k2, v))
                for k2, v in snap_d.items():
                    if v:
                        self.wait(eng, engname, ("D", k2, v))
            getattr(block, attr)(body)


def build_fused():
    nc = bass.Bass("TRN2", target_bir_lowering=False, debug=False, detect_race_conditions=not _os.environ.get("BASS_NO_RD"))
    x0T = nc.dram_tensor("x0T", [H, TLOC], F32, kind="ExternalInput").ap()
    x0Tpre = nc.dram_tensor("x0Tpre", [H, 16], F32, kind="ExternalInput").ap()
    W_inT = nc.dram_tensor("W_inT", [H, 2 * D], F32, kind="ExternalInput").ap()
    WxT = nc.dram_tensor("WxT", [D, 288], F32, kind="ExternalInput").ap()
    conv_w = nc.dram_tensor("conv_w", [D, KCV], F32, kind="ExternalInput").ap()
    conv_b2 = nc.dram_tensor("conv_b2", [128, 64], F32, kind="ExternalInput").ap()
    dtlnw2 = nc.dram_tensor("dtlnw2", [128, 2], F32, kind="ExternalInput").ap()
    b_ln_w = nc.dram_tensor("b_ln_w", [NST], F32, kind="ExternalInput").ap()
    c_ln_w = nc.dram_tensor("c_ln_w", [NST], F32, kind="ExternalInput").ap()
    wdtT_c = nc.dram_tensor("wdtT_c", [RDT, 1024], F32, kind="ExternalInput").ap()
    A_c = nc.dram_tensor("A_c", [128, 8 * NST], F32, kind="ExternalInput").ap()
    dtb_c = nc.dram_tensor("dtb_c", [128, 8], F32, kind="ExternalInput").ap()
    Dp_c = nc.dram_tensor("Dp_c", [128, 8], F32, kind="ExternalInput").ap()
    esel = nc.dram_tensor("esel", [NST, NST * 128], BF16, kind="ExternalInput").ap()
    W_outT = nc.dram_tensor("W_outT", [D, H], F32, kind="ExternalInput").ap()
    base_WT = nc.dram_tensor("base_WT", [H, H], F32, kind="ExternalInput").ap()
    router_WT = nc.dram_tensor("router_WT", [H, 8], F32, kind="ExternalInput").ap()
    loraA2 = nc.dram_tensor("loraA2", [H, 128], F32, kind="ExternalInput").ap()
    loraB2 = nc.dram_tensor("loraB2", [128, H], F32, kind="ExternalInput").ap()
    rep16 = nc.dram_tensor("rep16", [8, 128], F32, kind="ExternalInput").ap()
    w_qkv = nc.dram_tensor("w_qkv", [H, 4352], F32, kind="ExternalInput").ap()
    w_c = nc.dram_tensor("w_c", [H, H], F32, kind="ExternalInput").ap()
    rwT1 = nc.dram_tensor("rwT1", [H, 64], F32, kind="ExternalInput").ap()
    rb1 = nc.dram_tensor("rb1", [64, 1], F32, kind="ExternalInput").ap()
    ebias = nc.dram_tensor("ebias", [64, H], F32, kind="ExternalInput").ap()
    ident = nc.dram_tensor("ident", [128, 128], F32, kind="ExternalInput").ap()
    identb = nc.dram_tensor("identb", [128, 128], BF16, kind="ExternalInput").ap()
    ones_col = nc.dram_tensor("ones_col", [128, 1], F32, kind="ExternalInput").ap()
    ones_row = nc.dram_tensor("ones_row", [1, 128], F32, kind="ExternalInput").ap()
    onesb = nc.dram_tensor("onesb", [128, 1], BF16, kind="ExternalInput").ap()
    onesrb = nc.dram_tensor("onesrb", [1, 128], BF16, kind="ExternalInput").ap()
    kcol = nc.dram_tensor("kcol", [128, 64], F32, kind="ExternalInput").ap()
    maskc = nc.dram_tensor("maskc", [128, 4, 512], F32, kind="ExternalInput").ap()
    slopes_c = nc.dram_tensor("slopes_c", [128, 4], F32, kind="ExternalInput").ap()
    qrefc = nc.dram_tensor("qrefc", [128, 4, 512], F32, kind="ExternalInput").ap()
    ones128b = nc.dram_tensor("ones128b", [128, 128], BF16, kind="ExternalInput").ap()
    x4_out = nc.dram_tensor("x4_out", [H, TLOC], F32, kind="ExternalOutput").ap()
    v_scr = nc.dram_tensor("v_scr", [D, TLOC], BF16, kind="Internal").ap()
    gate_scr = nc.dram_tensor("gate_scr", [D, TLOC], BF16, kind="Internal").ap()
    dtn_scr = nc.dram_tensor("dtn_scr", [288, TLOC], BF16, kind="Internal").ap()
    v_a2a = nc.dram_tensor("v_a2a", [D, TLOC], BF16, kind="Internal").ap()
    dtn_ag = nc.dram_tensor("dtn_ag", [NCORE * 288, TLOC], BF16, kind="Internal").ap()
    y_in = nc.dram_tensor("y_in", [D, TLOC], BF16, kind="Internal").ap()
    y_out = nc.dram_tensor("y_out2", [D, TLOC], BF16, kind="Internal").ap()
    x1_scr = nc.dram_tensor("x1_scr", [H, TLOC], F32, kind="Internal").ap()
    x2_scr = nc.dram_tensor("x2_scr", [H, TLOC], F32, kind="Internal").ap()
    x3_scr = nc.dram_tensor("x3_scr", [H, TLOC], F32, kind="Internal").ap()
    r_scr = nc.dram_tensor("r_scr", [H, TLOC], F32, kind="Internal").ap()
    q_in = nc.dram_tensor("q_in", [H, TLOC], BF16, kind="Internal").ap()
    q_out = nc.dram_tensor("q_out", [H, TLOC], BF16, kind="Internal").ap()
    kv_in = nc.dram_tensor("kv_in", [256, TLOC], BF16, kind="Internal").ap()
    kv_out = nc.dram_tensor("kv_out", [NCORE * 256, TLOC], BF16, kind="Internal").ap()
    at_in = nc.dram_tensor("at_in", [H, TLOC], BF16, kind="Internal").ap()
    at_out = nc.dram_tensor("at_out", [H, TLOC], BF16, kind="Internal").ap()

    es = contextlib.ExitStack()
    with es:
        onc = es.enter_context(nc.sbuf_tensor("onc", [128, 1], F32))
        onr = es.enter_context(nc.sbuf_tensor("onr", [1, 128], F32))
        onb = es.enter_context(nc.sbuf_tensor("onb", [128, 1], BF16))
        onrb = es.enter_context(nc.sbuf_tensor("onrb", [1, 128], BF16))
        idt = es.enter_context(nc.sbuf_tensor("idt", [128, 128], F32))
        idb = es.enter_context(nc.sbuf_tensor("idb", [128, 128], BF16))
        block = es.enter_context(nc.Block())
        C = Ctx(nc, es)
        M = {}

        @block.sync
        def _(sp):
            C.dma("cst", sp.dma_start(out=onc[:], in_=ones_col[:]))
            C.dma("cst", sp.dma_start(out=onr[:], in_=ones_row[:]))
            C.dma("cst", sp.dma_start(out=onb[:], in_=onesb[:]))
            C.dma("cst", sp.dma_start(out=onrb[:], in_=onesrb[:]))
            C.dma("cst", sp.dma_start(out=idt[:], in_=ident[:]))
            C.dma("cst", sp.dma_start(out=idb[:], in_=identb[:]))
        CST0 = C.dma_total("cst")

        with contextlib.ExitStack() as ps:
            x0bf = ps.enter_context(nc.sbuf_tensor("x0bf", [128, 32, TLOC], BF16))
            x0pre_bf = ps.enter_context(nc.sbuf_tensor("x0pre_bf", [128, 32, 16], BF16))
            wbuf = ps.enter_context(nc.sbuf_tensor("wbuf", [128, 2, 32, 128], BF16))
            wxbuf = ps.enter_context(nc.sbuf_tensor("wxbuf", [128, 64, 288], BF16))
            cwb = ps.enter_context(nc.sbuf_tensor("cwb", [128, 64, KCV], F32))
            cbb = ps.enter_context(nc.sbuf_tensor("cbb", [128, 64], F32))
            lnw = ps.enter_context(nc.sbuf_tensor("lnw", [128, 4], F32))
            u_pad = ps.enter_context(nc.sbuf_tensor("u_pad", [128, NGRP, G + 3], F32))
            vt = ps.enter_context(nc.sbuf_tensor("vt", [128, NGRP, G], F32))
            v_sb = ps.enter_context(nc.sbuf_tensor("v_sb", [128, 2, TLOC], BF16))
            g_sb = ps.enter_context(nc.sbuf_tensor("g_sb", [128, 2, TLOC], BF16))
            vwx = ps.enter_context(nc.sbuf_tensor("vwx", [128, 32, 512], BF16))
            ssm_raw = ps.enter_context(nc.sbuf_tensor("ssm_raw", [128, 2, 512], F32))
            bc_raw = ps.enter_context(nc.sbuf_tensor("bc_raw", [16, 2, 512], F32))
            sq1 = ps.enter_context(nc.sbuf_tensor("sq1", [128, 2, 512], F32))
            sqbc = ps.enter_context(nc.sbuf_tensor("sqbc", [16, 2, 512], F32))
            rrow = ps.enter_context(nc.sbuf_tensor("rrow1", [1, 10, 512], F32))
            dtn_sb = ps.enter_context(nc.sbuf_tensor("dtn_sb", [128, 3, 512], BF16))
            pproj = ps.enter_context(nc.psum_tensor("pproj", [128, 1040], F32))
            pwx1 = ps.enter_context(nc.psum_tensor("pwx1", [128, 512], F32))
            pwx2 = ps.enter_context(nc.psum_tensor("pwx2", [128, 512], F32))
            pwxbc = ps.enter_context(nc.psum_tensor("pwxbc", [64, 512], F32))
            pstat = ps.enter_context(nc.psum_tensor("pstat", [1, 512], F32))
            pbcast = ps.enter_context(nc.psum_tensor("pbcast", [128, 512], F32))

            @block.gpsimd
            def _(gp):
                C.dma("gcst", gp.dma_start(out=x0bf[:], in_=x0T.rearrange("(e p) t -> p e t", p=128)))
                C.dma("gcst", gp.dma_start(out=x0pre_bf[:, :, 0:12], in_=x0Tpre[:, 0:12].rearrange("(e p) t -> p e t", p=128)))
                C.dma("gcst", gp.dma_start(out=wxbuf[:], in_=WxT.rearrange("(f p) r -> p f r", p=128)))
            @block.sync
            def _(sp):
                C.dma("cst", sp.dma_start(out=cwb[:], in_=conv_w.rearrange("(f p) k -> p f k", p=128)))
                C.dma("cst", sp.dma_start(out=cbb[:], in_=conv_b2[:]))
                C.dma("cst", sp.dma_start(out=lnw[:, 0:2], in_=dtlnw2[:]))
                C.dma("cst", sp.dma_start(out=lnw[0:16, 2:3], in_=b_ln_w.rearrange("(n o) -> n o", o=1)))
                C.dma("cst", sp.dma_start(out=lnw[0:16, 3:4], in_=c_ln_w.rearrange("(n o) -> n o", o=1)))
            CST = C.dma_total("cst")
            GCST = C.dma_total("gcst")

            ev_mark = {}
            for f in range(128):
                uhalf = f < 64
                @block.gpsimd
                def _(gp, f=f):
                    if f >= 2:
                        C.wait(gp, "pool", M[f"pe_{f-2}"])
                    M[f"w_{f}"] = C.dma(f"wb{f % 2}", gp.dma_start(
                        out=wbuf[:, f % 2],
                        in_=W_inT[:, 128 * f:128 * (f + 1)].rearrange("(e p) j -> p e j", p=128)))
                @block.tensor
                def _(te, f=f, uhalf=uhalf):
                    C.wait(te, "pe", M[f"w_{f}"])
                    if f == 0:
                        C.wait(te, "pe", CST)
                        C.wait(te, "pe", GCST)
                    if f >= 1:
                        C.wait(te, "pe", ev_mark[f - 1])
                    last = None
                    for e in range(32):
                        lhsT = wbuf[:, f % 2, e]
                        te.ldweights(lhsT)
                        te.matmul(pproj[:, 0:512], lhsT, x0bf[:, e, 0:512], start=(e == 0), stop=(e == 31))
                        last = te.matmul(pproj[:, 512:1024], lhsT, x0bf[:, e, 512:1024], start=(e == 0), stop=(e == 31))
                        if uhalf:
                            last = te.matmul(pproj[:, 1024:1036], lhsT, x0pre_bf[:, e, 0:12], start=(e == 0), stop=(e == 31))
                    M[f"pe_{f}"] = C.bump("pe", last)
                @block.scalar
                def _(sc, f=f, uhalf=uhalf):
                    C.wait(sc, "act", M[f"pe_{f}"])
                    if uhalf:
                        if f >= 1:
                            C.wait(sc, "act", M[f"conv_{f-1}"])
                        last = None
                        for g in range(NGRP):
                            sc.copy(u_pad[:, g, 3:3 + G], pproj[:, G * g:G * (g + 1)])
                            last = sc.copy(u_pad[:, g, 0:3], pproj[:, 1024 + 3 * g:1024 + 3 * g + 3])
                        ev_mark[f] = M[f"ev_{f}"] = C.bump("act", last)
                    else:
                        fb_ = f - 64
                        if f >= 66:
                            C.wait(sc, "act", M[f"gst_{f-2}"])
                        half = fb_ % 2
                        sc.copy(g_sb[:, half, 0:512], pproj[:, 0:512])
                        ev_mark[f] = M[f"gev_{f}"] = C.bump("act", sc.copy(g_sb[:, half, 512:1024], pproj[:, 512:1024]))
                if uhalf:
                    @block.vector
                    def _(dv, f=f):
                        C.wait(dv, "dve", M[f"ev_{f}"])
                        C.wait(dv, "dve", CST)
                        if f >= 1:
                            C.wait(dv, "dve", M[f"vs_{f-1}"])
                        dv.tensor_scalar(vt[:], u_pad[:, :, 0:G], cwb[:, f, 0:1], None, OP.mult)
                        last = None
                        for k in range(1, KCV):
                            last = dv.scalar_tensor_tensor(vt[:], u_pad[:, :, k:k + G], cwb[:, f, k:k + 1], vt[:], OP.mult, OP.add)
                        M[f"conv_{f}"] = C.bump("dve", last)
                    @block.scalar
                    def _(sc, f=f):
                        C.wait(sc, "act", M[f"conv_{f}"])
                        if f >= 2:
                            C.wait(sc, "act", M[f"vst_{f-2}"])
                        M[f"vs_{f}"] = C.bump("act", sc.activation(
                            v_sb[:, f % 2].rearrange("p (g t) -> p g t", g=NGRP), vt[:],
                            AF.Silu, bias=cbb[:, f:f + 1], scale=1.0))
                @block.sync
                def _(sp, f=f, uhalf=uhalf):
                    if uhalf:
                        C.wait(sp, "sp", M[f"vs_{f}"])
                        M[f"vst_{f}"] = C.dma(f"vst{f % 2}", sp.dma_start(out=v_scr[128 * f:128 * (f + 1), :], in_=v_sb[:, f % 2]))
                    else:
                        C.wait(sp, "sp", M[f"gev_{f}"])
                        M[f"gst_{f}"] = C.dma(f"gst{(f - 64) % 2}", sp.dma_start(out=gate_scr[128 * (f - 64):128 * (f - 63), :], in_=g_sb[:, (f - 64) % 2]))

            for t in range(2):
                for fh in range(2):
                    @block.sync
                    def _(sp, t=t, fh=fh):
                        C.wait(sp, "sp", C.dma_total("vst0"))
                        C.wait(sp, "sp", C.dma_total("vst1"))
                        if t > 0 or fh > 0:
                            C.wait(sp, "sp", M[f"wx_mm_{t}_{fh-1}"] if fh > 0 else M[f"wx_mm_{t-1}_1"])
                        M[f"vwx_{t}_{fh}"] = C.dma("vwx", sp.dma_start(
                            out=vwx[:], in_=v_scr[4096 * fh:4096 * (fh + 1), 512 * t:512 * (t + 1)].rearrange("(f p) x -> p f x", p=128)))
                    @block.tensor
                    def _(te, t=t, fh=fh):
                        C.wait(te, "pe", M[f"vwx_{t}_{fh}"])
                        if t == 1 and fh == 0:
                            C.wait(te, "pe", M["nrm_done_0"])
                        last = None
                        for fi in range(32):
                            f = 32 * fh + fi
                            st = (f == 0)
                            sto = (f == 63)
                            te.ldweights(wxbuf[:, f, 0:128])
                            te.matmul(pwx1[:], wxbuf[:, f, 0:128], vwx[:, fi], start=st, stop=sto)
                            te.ldweights(wxbuf[:, f, 128:256])
                            te.matmul(pwx2[:], wxbuf[:, f, 128:256], vwx[:, fi], start=st, stop=sto)
                            te.ldweights(wxbuf[:, f, 256:272])
                            te.matmul(pwxbc[0:16, :], wxbuf[:, f, 256:272], vwx[:, fi], start=st, stop=sto)
                            te.ldweights(wxbuf[:, f, 272:288])
                            last = te.matmul(pwxbc[32:48, :], wxbuf[:, f, 272:288], vwx[:, fi], start=st, stop=sto)
                        M[f"wx_mm_{t}_{fh}"] = C.bump("pe", last)
                M[f"wx_mm_{t}"] = M[f"wx_mm_{t}_1"]

                @block.scalar
                def _(sc, t=t):
                    C.wait(sc, "act", M[f"wx_mm_{t}"])
                    if t == 1:
                        C.wait(sc, "act", M["dtnst_dt_0"])
                    sc.copy(ssm_raw[:, 0], pwx1[:])
                    sc.copy(ssm_raw[:, 1], pwx2[:])
                    sc.copy(bc_raw[:, 0], pwxbc[0:16])
                    sc.copy(bc_raw[:, 1], pwxbc[32:48])
                    sc.activation(sq1[:, 0], pwx1[:], AF.Square)
                    sc.activation(sq1[:, 1], pwx2[:], AF.Square)
                    sc.activation(sqbc[:, 0], pwxbc[0:16], AF.Square)
                    M[f"sqd_{t}"] = C.bump("act", sc.activation(sqbc[:, 1], pwxbc[32:48], AF.Square))
                @block.tensor
                def _(te, t=t):
                    C.wait(te, "pe", M[f"sqd_{t}"])
                    te.matmul(pstat[:], onc[:, 0:1], sq1[:, 0], start=True, stop=False)
                    M[f"st_dt_{t}"] = C.bump("pe", te.matmul(pstat[:], onc[:, 0:1], sq1[:, 1], start=False, stop=True))
                @block.vector
                def _(dv, t=t):
                    C.wait(dv, "dve", M[f"st_dt_{t}"])
                    M[f"ms_dt_{t}"] = C.self_edge("dve", dv, dv.tensor_scalar(rrow[:, 0], pstat[:], 1.0 / 256, 1e-6, OP.mult, OP.add))
                @block.scalar
                def _(sc, t=t):
                    C.wait(sc, "act", M[f"ms_dt_{t}"])
                    M[f"sqrt_dt_{t}"] = C.bump("act", sc.activation(rrow[:, 1], rrow[:, 0], AF.Sqrt))
                @block.vector
                def _(dv, t=t):
                    C.wait(dv, "dve", M[f"sqrt_dt_{t}"])
                    M[f"rstd_dt_{t}"] = C.self_edge("dve", dv, dv.reciprocal(rrow[:, 2], rrow[:, 1]))
                @block.tensor
                def _(te, t=t):
                    C.wait(te, "pe", M[f"rstd_dt_{t}"])
                    M[f"bc_dt_{t}"] = C.bump("pe", te.matmul(pbcast[:], onr[:], rrow[:, 2], start=True, stop=True))
                @block.vector
                def _(dv, t=t):
                    C.wait(dv, "dve", M[f"bc_dt_{t}"])
                    if t == 1:
                        C.wait(dv, "dve", M["dtnst_dt_0"])
                    dv.scalar_tensor_tensor(dtn_sb[:, 0], ssm_raw[:, 0], lnw[:, 0:1], pbcast[:], OP.mult, OP.mult)
                    M[f"dtn_dt_{t}"] = C.bump("dve", dv.scalar_tensor_tensor(
                        dtn_sb[:, 1], ssm_raw[:, 1], lnw[:, 1:2], pbcast[:], OP.mult, OP.mult))
                @block.sync
                def _(sp, t=t):
                    C.wait(sp, "sp", M[f"dtn_dt_{t}"])
                    M[f"dtnst_dt_{t}"] = C.dma("dtst", sp.dma_start(
                        out=dtn_scr[0:256, 512 * t:512 * (t + 1)].rearrange("(a p) x -> p a x", p=128), in_=dtn_sb[:, 0:2]))
                for bi, lncol in enumerate((2, 3)):
                    @block.tensor
                    def _(te, t=t, bi=bi):
                        C.wait(te, "pe", M[f"sqd_{t}"])
                        C.wait(te, "pe", M[f"ms_dt_{t}"] if bi == 0 else M[f"ms_b0_{t}"])
                        M[f"st_b{bi}_{t}"] = C.bump("pe", te.matmul(pstat[:], onc[0:16, 0:1], sqbc[:, bi], start=True, stop=True))
                    @block.vector
                    def _(dv, t=t, bi=bi):
                        C.wait(dv, "dve", M[f"st_b{bi}_{t}"])
                        M[f"ms_b{bi}_{t}"] = C.self_edge("dve", dv, dv.tensor_scalar(rrow[:, 3 + 3 * bi], pstat[:], 1.0 / 16, 1e-6, OP.mult, OP.add))
                    @block.scalar
                    def _(sc, t=t, bi=bi):
                        C.wait(sc, "act", M[f"ms_b{bi}_{t}"])
                        M[f"sqrt_b{bi}_{t}"] = C.bump("act", sc.activation(rrow[:, 4 + 3 * bi], rrow[:, 3 + 3 * bi], AF.Sqrt))
                    @block.vector
                    def _(dv, t=t, bi=bi):
                        C.wait(dv, "dve", M[f"sqrt_b{bi}_{t}"])
                        M[f"rstd_b{bi}_{t}"] = C.self_edge("dve", dv, dv.reciprocal(rrow[:, 5 + 3 * bi], rrow[:, 4 + 3 * bi]))
                    @block.tensor
                    def _(te, t=t, bi=bi):
                        C.wait(te, "pe", M[f"rstd_b{bi}_{t}"])
                        C.wait(te, "pe", M[f"dtn_dt_{t}"] if bi == 0 else M[f"dtn_b0_{t}"])
                        M[f"bc_b{bi}_{t}"] = C.bump("pe", te.matmul(pbcast[:], onr[:], rrow[:, 5 + 3 * bi], start=True, stop=True))
                    @block.vector
                    def _(dv, t=t, bi=bi, lncol=lncol):
                        C.wait(dv, "dve", M[f"bc_b{bi}_{t}"])
                        if bi == 1:
                            C.wait(dv, "dve", M[f"dtnst_b0_{t}"])
                        elif t == 1:
                            C.wait(dv, "dve", M["dtnst_b1_0"])
                        M[f"dtn_b{bi}_{t}"] = C.bump("dve", dv.scalar_tensor_tensor(
                            dtn_sb[0:16, 2], bc_raw[:, bi], lnw[0:16, lncol:lncol + 1], pbcast[0:16, :], OP.mult, OP.mult))
                    @block.sync
                    def _(sp, t=t, bi=bi):
                        C.wait(sp, "sp", M[f"dtn_b{bi}_{t}"])
                        M[f"dtnst_b{bi}_{t}"] = C.dma(f"bcst{bi}", sp.dma_start(
                            out=dtn_scr[256 + 16 * bi:256 + 16 * (bi + 1), 512 * t:512 * (t + 1)], in_=dtn_sb[0:16, 2]))
                M[f"nrm_done_{t}"] = M[f"dtn_b1_{t}"]

            @block.gpsimd
            def _(gp):
                for s in ("vst0", "vst1", "dtst", "bcst0", "bcst1"):
                    C.wait(gp, "pool", C.dma_total(s))
                M["cc_v"] = C.bump("cc", gp.collective_compute(
                    "AllToAll", OP.bypass, replica_groups=RG,
                    ins=[v_scr.tensor.ap().opt()], outs=[v_a2a.tensor.ap().opt()]))
                M["cc_dtn"] = C.bump("cc", gp.collective_compute(
                    "AllGather", OP.bypass, replica_groups=RG,
                    ins=[dtn_scr.tensor.ap().opt()], outs=[dtn_ag.tensor.ap().opt()]))
            C.join(block)

        with contextlib.ExitStack() as ps:
            SP = 2048
            vh = ps.enter_context(nc.sbuf_tensor("vh", [128, 4, SP], BF16))
            dtn_sb2 = ps.enter_context(nc.sbuf_tensor("dtn_sb2", [128, 2, SP], BF16))
            bt = ps.enter_context(nc.sbuf_tensor("bt", [NST, SP], BF16))
            ct = ps.enter_context(nc.sbuf_tensor("ct", [NST, SP], BF16))
            wdt = ps.enter_context(nc.sbuf_tensor("wdt", [128, 2, 8, 128], BF16))
            ab = ps.enter_context(nc.sbuf_tensor("ab", [128, 8 * NST], F32))
            dtb = ps.enter_context(nc.sbuf_tensor("dtb", [128, 8], F32))
            dpp = ps.enter_context(nc.sbuf_tensor("dpp", [128, 8], F32))
            esl = ps.enter_context(nc.sbuf_tensor("esl", [NST, NST * 128], BF16))
            delta = ps.enter_context(nc.sbuf_tensor("delta", [128, 4, SP], BF16))
            du = ps.enter_context(nc.sbuf_tensor("du", [128, 4, SP], BF16))
            sp_t = ps.enter_context(nc.sbuf_tensor("sp_t", [128, SP], F32))
            sp_t2 = ps.enter_context(nc.sbuf_tensor("sp_t2", [128, SP], F32))
            bbc = ps.enter_context(nc.sbuf_tensor("bbc", [128, SP], BF16))
            cbc = ps.enter_context(nc.sbuf_tensor("cbc", [128, SP], BF16))
            a_sb = ps.enter_context(nc.sbuf_tensor("a_sb", [128, 4, SP], F32))
            b_sb = ps.enter_context(nc.sbuf_tensor("b_sb", [128, 2, SP], BF16))
            s_sb = ps.enter_context(nc.sbuf_tensor("s_sb", [128, 2, SP], BF16))
            z_sb = ps.enter_context(nc.sbuf_tensor("z_sb", [128, 2, SP], BF16))
            y_acc = ps.enter_context(nc.sbuf_tensor("y_acc", [128, 4, SP], F32))
            y2_sb = ps.enter_context(nc.sbuf_tensor("y2_sb", [128, 2, SP], BF16))
            carrys = [ps.enter_context(nc.sbuf_tensor(f"cr{i}", [128, 1], F32)) for i in range(8 * NST)]
            pdelta = ps.enter_context(nc.psum_tensor("pdelta", [128, 2, 512], F32))
            pbc2 = ps.enter_context(nc.psum_tensor("pbc2", [128, SP], F32))

            @block.gpsimd
            def _(gp):
                C.dma("wdtl", gp.dma_start(out=wdt[:], in_=wdtT_c.rearrange("(rc p) (dt j) -> p rc dt j", p=128, j=128)))
            @block.sync
            def _(sp):
                C.dma("cst2", sp.dma_start(out=ab[:], in_=A_c[:]))
                C.dma("cst2", sp.dma_start(out=dtb[:], in_=dtb_c[:]))
                C.dma("cst2", sp.dma_start(out=dpp[:], in_=Dp_c[:]))
                C.dma("cst2", sp.dma_start(out=esl[:], in_=esel[:]))
            CST = C.dma_total("cst2")
            WDTL = C.dma_total("wdtl")

            pd_free = {}
            a_free = {}
            a_last = {}
            z_free = {}
            last_bbc_read = None
            last_pbc_ev = None
            prev_delta_add = None
            prev_y2 = {}
            y2st = {}

            for s in range(4):
                tsl = slice(SP * s, SP * (s + 1))
                @block.sync
                def _(sp, s=s, tsl=tsl):
                    if s >= 1:
                        C.wait(sp, "sp", M[f"dmm_last_{s-1}"])
                        C.wait(sp, "sp", M[f"bc_mm_last_{s-1}"])
                    for r in range(8):
                        k = 8 * s + r
                        ow, gg = owner(k), gidx(k)
                        C.dma("dtnl", sp.dma_start(
                            out=dtn_sb2[:, :, 256 * r:256 * (r + 1)],
                            in_=dtn_ag[288 * ow:288 * ow + 256, 256 * gg:256 * (gg + 1)].rearrange("(a p) t -> p a t", p=128)))
                        C.dma("dtnl", sp.dma_start(out=bt[:, 256 * r:256 * (r + 1)],
                                                   in_=dtn_ag[288 * ow + 256:288 * ow + 272, 256 * gg:256 * (gg + 1)]))
                        C.dma("dtnl", sp.dma_start(out=ct[:, 256 * r:256 * (r + 1)],
                                                   in_=dtn_ag[288 * ow + 272:288 * ow + 288, 256 * gg:256 * (gg + 1)]))
                    M[f"dtnl_{s}"] = C.dma_total("dtnl")

                for h in range(2):
                    @block.sync
                    def _(sp, s=s, h=h, tsl=tsl):
                        prev = y2st.get((s, h - 1, 3) if h else (s - 1, 1, 3))
                        if prev is not None:
                            C.wait(sp, "sp", prev)
                        for r in range(8):
                            k = 8 * s + r
                            ow, gg = owner(k), gidx(k)
                            C.dma("vhl", sp.dma_start(
                                out=vh[:, :, 256 * r:256 * (r + 1)],
                                in_=v_a2a[1024 * ow + 512 * h:1024 * ow + 512 * (h + 1), 256 * gg:256 * (gg + 1)].rearrange("(dt p) t -> p dt t", p=128)))
                        M[f"vl_{s}_{h}"] = C.dma_total("vhl")

                    for di in range(4):
                        dt_g = 4 * h + di
                        for q in range(4):
                            par = q % 2
                            @block.tensor
                            def _(te, s=s, di=di, dt_g=dt_g, q=q, par=par):
                                C.wait(te, "pe", M[f"dtnl_{s}"])
                                C.wait(te, "pe", CST)
                                C.wait(te, "pe", WDTL)
                                if par in pd_free:
                                    C.wait(te, "pe", pd_free[par])
                                te.matmul(pdelta[:, par], wdt[:, 0, dt_g], dtn_sb2[:, 0, 512 * q:512 * (q + 1)], start=True, stop=False)
                                M["dmm_cur"] = C.bump("pe", te.matmul(
                                    pdelta[:, par], wdt[:, 1, dt_g], dtn_sb2[:, 1, 512 * q:512 * (q + 1)], start=False, stop=True))
                            @block.scalar
                            def _(sc, s=s, h=h, di=di, dt_g=dt_g, q=q, par=par):
                                C.wait(sc, "act", CST)
                                C.wait(sc, "act", M["dmm_cur"])
                                if q == 0 and prev_delta_add is not None:
                                    C.wait(sc, "act", prev_delta_add)
                                qq = slice(512 * q, 512 * (q + 1))
                                sc.activation(sp_t[:, qq], pdelta[:, par], AF.Abs, bias=dtb[:, dt_g:dt_g + 1])
                                sc.activation(sp_t[:, qq], sp_t[:, qq], AF.Exp, scale=-1.0)
                                sc.activation(sp_t[:, qq], sp_t[:, qq], AF.Ln, bias=onc[:, 0:1])
                                pd_free[par] = M["spq_cur"] = C.bump("act", sc.activation(
                                    sp_t2[:, qq], pdelta[:, par], AF.Relu, bias=dtb[:, dt_g:dt_g + 1]))
                        M[f"spl_{s}_{h}_{di}"] = M["spq_cur"]
                        if di == 3 and h == 1:
                            M[f"dmm_last_{s}"] = M["dmm_cur"]
                        @block.vector
                        def _(dv, s=s, h=h, di=di):
                            C.wait(dv, "dve", M[f"spl_{s}_{h}_{di}"])
                            if di in a_last:
                                C.wait(dv, "dve", a_last[di])
                            M[f"delta_{s}_{h}_{di}"] = C.bump("dve", dv.tensor_tensor(delta[:, di], sp_t2[:], sp_t[:], OP.add))
                            C.wait(dv, "dve", M[f"vl_{s}_{h}"])
                            M[f"du_{s}_{h}_{di}"] = C.bump("dve", dv.tensor_tensor(du[:, di], delta[:, di], vh[:, di], OP.mult))
                        prev_delta_add = M[f"delta_{s}_{h}_{di}"]

                    for n in range(NST):
                        act_base_n = C.ecnt["act"]
                        @block.tensor
                        def _(te, s=s, n=n, act_base_n=act_base_n):
                            C.wait(te, "pe", M[f"dtnl_{s}"])
                            if last_pbc_ev is not None:
                                C.wait(te, "pe", last_pbc_ev)
                            last = None
                            for q in range(4):
                                last = te.matmul(pbc2[:, 512 * q:512 * (q + 1)], esl[:, 128 * n:128 * (n + 1)],
                                                 bt[:, 512 * q:512 * (q + 1)], start=True, stop=True)
                            M["b_mm"] = C.bump("pe", last)
                            C.wait(te, "pe", ("E", "act", act_base_n + 1))
                            for q in range(4):
                                last = te.matmul(pbc2[:, 512 * q:512 * (q + 1)], esl[:, 128 * n:128 * (n + 1)],
                                                 ct[:, 512 * q:512 * (q + 1)], start=True, stop=True)
                            M["bc_mm_cur"] = C.bump("pe", last)
                        M[f"bc_mm_last_{s}"] = M["bc_mm_cur"]

                        @block.scalar
                        def _(sc, s=s, h=h, n=n, lbr=last_bbc_read):
                            C.wait(sc, "act", M["b_mm"])
                            if lbr is not None:
                                C.wait(sc, "act", lbr)
                            M["bbc_ev"] = C.bump("act", sc.copy(bbc[:], pbc2[:]))
                            C.wait(sc, "act", M["bc_mm_cur"])
                            M["cbc_ev"] = C.bump("act", sc.copy(cbc[:], pbc2[:]))
                            for di in range(4):
                                dt_g = 4 * h + di
                                apar = di
                                C.wait(sc, "act", M[f"delta_{s}_{h}_{di}"])
                                if apar in a_free:
                                    C.wait(sc, "act", a_free[apar])
                                M[f"a_{apar}"] = C.bump("act", sc.activation(
                                    a_sb[:, apar], delta[:, di], AF.Exp, scale=ab[:, NST * dt_g + n:NST * dt_g + n + 1]))
                                if n == NST - 1:
                                    a_last[di] = M[f"a_{apar}"]
                        last_pbc_ev = M["cbc_ev"]

                        for di in range(4):
                            dt_g = 4 * h + di
                            par = di % 2
                            apar = di
                            @block.vector
                            def _(dv, s=s, h=h, n=n, di=di, dt_g=dt_g, par=par, apar=apar):
                                C.wait(dv, "dve", M[f"du_{s}_{h}_{di}"])
                                C.wait(dv, "dve", M["bbc_ev"])
                                dv.tensor_tensor(b_sb[:, par], du[:, di], bbc[:], OP.mult)
                                C.wait(dv, "dve", M[f"a_{apar}"])
                                cr = carrys[dt_g * NST + n]
                                if s == 0:
                                    sc_i = dv.tensor_tensor_scan(s_sb[:, par], a_sb[:, apar], b_sb[:, par], 0.0, OP.mult, OP.add)
                                else:
                                    sc_i = dv.tensor_tensor_scan(s_sb[:, par], a_sb[:, apar], b_sb[:, par], cr[:, 0:1], OP.mult, OP.add)
                                C.self_edge("dve", dv, sc_i)
                                a_free[apar] = C.bump("dve", dv.tensor_copy(cr[:], s_sb[:, par, SP - 1:SP]))
                                C.wait(dv, "dve", M["cbc_ev"])
                                if par in z_free:
                                    C.wait(dv, "dve", z_free[par])
                                M[f"z_{par}"] = C.bump("dve", dv.tensor_tensor(z_sb[:, par], s_sb[:, par], cbc[:], OP.mult))
                            @block.gpsimd
                            def _(gp, s=s, h=h, n=n, di=di, par=par):
                                C.wait(gp, "pool", M[f"z_{par}"])
                                if n == 0:
                                    prev = prev_y2.get(di)
                                    if prev is not None:
                                        C.wait(gp, "pool", prev)
                                    mm = gp.tensor_copy(y_acc[:, di], z_sb[:, par])
                                else:
                                    mm = gp.tensor_tensor(y_acc[:, di], y_acc[:, di], z_sb[:, par], OP.add)
                                z_free[par] = C.bump("pool", mm)
                                if n == NST - 1:
                                    M[f"y_done_{di}"] = z_free[par]
                        last_bbc_read = M["z_1"]

                    for di in range(4):
                        dt_g = 4 * h + di
                        @block.vector
                        def _(dv, s=s, h=h, di=di, dt_g=dt_g):
                            C.wait(dv, "dve", M[f"y_done_{di}"])
                            prevst = y2st.get((s, h, di - 2) if di >= 2 else ((s, h - 1, di + 2) if h else (s - 1, 1, di + 2)))
                            if prevst is not None:
                                C.wait(dv, "dve", prevst)
                            M["y2_cur"] = C.bump("dve", dv.scalar_tensor_tensor(
                                y2_sb[:, di % 2], vh[:, di], dpp[:, dt_g:dt_g + 1], y_acc[:, di], OP.mult, OP.add))
                            prev_y2[di] = M["y2_cur"]
                        @block.sync
                        def _(sp, s=s, h=h, di=di, tsl=tsl):
                            C.wait(sp, "sp", M["y2_cur"])
                            for r in range(8):
                                k = 8 * s + r
                                ow, gg = owner(k), gidx(k)
                                C.dma(f"y2st{di % 2}", sp.dma_start(
                                    out=y_in[1024 * ow + 512 * h + 128 * di:1024 * ow + 512 * h + 128 * (di + 1),
                                             256 * gg:256 * (gg + 1)],
                                    in_=y2_sb[:, di % 2, 256 * r:256 * (r + 1)]))
                            y2st[(s, h, di)] = C.dma_total(f"y2st{di % 2}")

            @block.gpsimd
            def _(gp):
                C.wait(gp, "pool", C.dma_total("y2st0"))
                C.wait(gp, "pool", C.dma_total("y2st1"))
                M["cc_y"] = C.bump("cc", gp.collective_compute(
                    "AllToAll", OP.bypass, replica_groups=RG,
                    ins=[y_in.tensor.ap().opt()], outs=[y_out.tensor.ap().opt()]))
            C.join(block)

        with contextlib.ExitStack() as ps:
            y3 = ps.enter_context(nc.sbuf_tensor("y3", [128, 64, TLOC], BF16))
            t_y2 = ps.enter_context(nc.sbuf_tensor("t_y2", [128, 2, TLOC], BF16))
            t_g = ps.enter_context(nc.sbuf_tensor("t_g", [128, 2, TLOC], BF16))
            t_sg = ps.enter_context(nc.sbuf_tensor("t_sg", [128, TLOC], BF16))
            wbuf3 = ps.enter_context(nc.sbuf_tensor("wbuf3", [128, 2, 32, 128], BF16))
            x0t = ps.enter_context(nc.sbuf_tensor("x0t", [128, 2, TLOC], F32))
            r_sb = ps.enter_context(nc.sbuf_tensor("r_sb3", [128, 2, TLOC], F32))
            sq3 = ps.enter_context(nc.sbuf_tensor("sq3", [128, TLOC], F32))
            rrow3 = ps.enter_context(nc.sbuf_tensor("rrow3", [1, 6, TLOC], F32))
            xn = ps.enter_context(nc.sbuf_tensor("xn3", [128, 2, TLOC], F32))
            pW = ps.enter_context(nc.psum_tensor("pW3", [128, TLOC], F32))
            psum_s = ps.enter_context(nc.psum_tensor("pS3", [1, TLOC], F32))
            psum_q = ps.enter_context(nc.psum_tensor("pQ3", [1, TLOC], F32))
            pbc3 = ps.enter_context(nc.psum_tensor("pbc3", [128, TLOC], F32))

            for dc in range(64):
                par = dc % 2
                @block.sync
                def _(sp, dc=dc, par=par):
                    if dc >= 2:
                        C.wait(sp, "sp", M[f"y3_{dc-2}"])
                    C.dma(f"y2l{par}", sp.dma_start(out=t_y2[:, par], in_=y_out[128 * dc:128 * (dc + 1), :]))
                    M[f"ld3_{dc}"] = C.dma(f"gl{par}", sp.dma_start(out=t_g[:, par], in_=gate_scr[128 * dc:128 * (dc + 1), :]))
                @block.scalar
                def _(sc, dc=dc, par=par):
                    C.wait(sc, "act", C.dma_total(f"y2l{par}"))
                    C.wait(sc, "act", C.dma_total(f"gl{par}"))
                    if dc >= 1:
                        C.wait(sc, "act", M[f"y3_{dc-1}"])
                    M[f"sg_{dc}"] = C.bump("act", sc.activation(t_sg[:], t_g[:, par], AF.Silu))
                @block.vector
                def _(dv, dc=dc, par=par):
                    C.wait(dv, "dve", M[f"sg_{dc}"])
                    M[f"y3_{dc}"] = C.bump("dve", dv.tensor_tensor(y3[:, dc], t_y2[:, par], t_sg[:], OP.mult))

            for hc in range(32):
                par = hc % 2
                @block.gpsimd
                def _(gp, hc=hc):
                    for half in range(2):
                        g = 2 * hc + half
                        if g >= 2:
                            C.wait(gp, "pool", M[f"mmh3_{g-2}"])
                        M[f"wl3_{g}"] = C.dma(f"wl3{g % 2}", gp.dma_start(
                            out=wbuf3[:, g % 2],
                            in_=W_outT[4096 * half:4096 * (half + 1), 128 * hc:128 * (hc + 1)].rearrange(
                                "(dc p) j -> p dc j", p=128)))
                @block.sync
                def _(sp, hc=hc, par=par):
                    if hc >= 2:
                        C.wait(sp, "sp", M[f"resid3_{hc-2}"])
                    M[f"x0l_{hc}"] = C.dma(f"x0l{par}", sp.dma_start(out=x0t[:, par], in_=x0T[128 * hc:128 * (hc + 1), :]))
                @block.tensor
                def _(te, hc=hc):
                    if hc == 0:
                        for dc in range(64):
                            C.wait(te, "pe", M[f"y3_{dc}"])
                    if hc >= 1:
                        C.wait(te, "pe", M[f"ev3_{hc-1}"])
                    last = None
                    for half in range(2):
                        g = 2 * hc + half
                        C.wait(te, "pe", M[f"wl3_{g}"])
                        for dj in range(32):
                            dc = 32 * half + dj
                            te.ldweights(wbuf3[:, g % 2, dj])
                            te.matmul(pW[:, 0:512], wbuf3[:, g % 2, dj], y3[:, dc, 0:512], start=(dc == 0), stop=(dc == 63))
                            last = te.matmul(pW[:, 512:1024], wbuf3[:, g % 2, dj], y3[:, dc, 512:1024], start=(dc == 0), stop=(dc == 63))
                        M[f"mmh3_{g}"] = C.bump("pe", last)
                    M[f"mm3_{hc}"] = M[f"mmh3_{2 * hc + 1}"]
                @block.vector
                def _(dv, hc=hc, par=par):
                    C.wait(dv, "dve", M[f"mm3_{hc}"])
                    C.wait(dv, "dve", M[f"x0l_{hc}"])
                    if hc >= 2:
                        C.wait(dv, "dve", M[f"rst3_{hc-2}"])
                    M[f"resid3_{hc}"] = C.bump("dve", dv.tensor_tensor(r_sb[:, par], pW[:], x0t[:, par], OP.add))
                @block.scalar
                def _(sc, hc=hc, par=par):
                    C.wait(sc, "act", M[f"resid3_{hc}"])
                    if hc >= 1:
                        C.wait(sc, "act", M[f"stat3_{hc-1}"])
                    M[f"ev3_{hc}"] = M[f"sq3_{hc}"] = C.bump("act", sc.activation(sq3[:], r_sb[:, par], AF.Square))
                @block.tensor
                def _(te, hc=hc, par=par):
                    C.wait(te, "pe", M[f"sq3_{hc}"])
                    te.matmul(psum_s[:, 0:512], onc[:, 0:1], r_sb[:, par, 0:512], start=(hc == 0), stop=(hc == 31))
                    te.matmul(psum_s[:, 512:1024], onc[:, 0:1], r_sb[:, par, 512:1024], start=(hc == 0), stop=(hc == 31))
                    te.matmul(psum_q[:, 0:512], onc[:, 0:1], sq3[:, 0:512], start=(hc == 0), stop=(hc == 31))
                    M[f"stat3_{hc}"] = C.bump("pe", te.matmul(psum_q[:, 512:1024], onc[:, 0:1], sq3[:, 512:1024], start=(hc == 0), stop=(hc == 31)))
                @block.sync
                def _(sp, hc=hc, par=par):
                    C.wait(sp, "sp", M[f"stat3_{hc}"])
                    M[f"rst3_{hc}"] = C.dma(f"rst{par}", sp.dma_start(out=r_scr[128 * hc:128 * (hc + 1), :], in_=r_sb[:, par]))

            @block.vector
            def _(dv):
                C.wait(dv, "dve", M["stat3_31"])
                C.self_edge("dve", dv, dv.tensor_scalar(rrow3[:, 0], psum_s[:], 1.0 / H, None, OP.mult))
                C.self_edge("dve", dv, dv.tensor_scalar(rrow3[:, 1], psum_q[:], 1.0 / H, None, OP.mult))
                C.self_edge("dve", dv, dv.tensor_tensor(rrow3[:, 2], rrow3[:, 0], rrow3[:, 0], OP.mult))
                C.self_edge("dve", dv, dv.tensor_tensor(rrow3[:, 3], rrow3[:, 1], rrow3[:, 2], OP.subtract))
                M["var3"] = C.bump("dve", dv.tensor_scalar(rrow3[:, 3], rrow3[:, 3], 1.0, 1e-5, OP.mult, OP.add))
            @block.scalar
            def _(sc):
                C.wait(sc, "act", M["var3"])
                M["sqrt3"] = C.bump("act", sc.activation(rrow3[:, 4], rrow3[:, 3], AF.Sqrt))
            @block.vector
            def _(dv):
                C.wait(dv, "dve", M["sqrt3"])
                M["rstd3"] = C.self_edge("dve", dv, dv.reciprocal(rrow3[:, 5], rrow3[:, 4]))
            @block.tensor
            def _(te):
                C.wait(te, "pe", M["rstd3"])
                te.matmul(pbc3[:, 0:512], onr[:], rrow3[:, 5, 0:512], start=True, stop=True)
                te.matmul(pbc3[:, 512:1024], onr[:], rrow3[:, 5, 512:1024], start=True, stop=True)
                te.matmul(pW[:, 0:512], onr[:], rrow3[:, 0, 0:512], start=True, stop=True)
                M["bc3"] = C.bump("pe", te.matmul(pW[:, 512:1024], onr[:], rrow3[:, 0, 512:1024], start=True, stop=True))
            for hc in range(32):
                par = hc % 2
                @block.sync
                def _(sp, hc=hc, par=par):
                    C.wait(sp, "sp", C.dma_total("rst0"))
                    C.wait(sp, "sp", C.dma_total("rst1"))
                    if hc >= 2:
                        C.wait(sp, "sp", M[f"norm3_{hc-2}"])
                    M[f"rrd3_{hc}"] = C.dma(f"rrd{par}", sp.dma_start(out=r_sb[:, par], in_=r_scr[128 * hc:128 * (hc + 1), :]))
                @block.vector
                def _(dv, hc=hc, par=par):
                    C.wait(dv, "dve", M[f"rrd3_{hc}"])
                    C.wait(dv, "dve", M["bc3"])
                    if hc >= 2:
                        C.wait(dv, "dve", M[f"xst3_{hc-2}"])
                    dv.tensor_tensor(xn[:, par], r_sb[:, par], pW[:], OP.subtract)
                    M[f"norm3_{hc}"] = C.bump("dve", dv.tensor_tensor(xn[:, par], xn[:, par], pbc3[:], OP.mult))
                @block.sync
                def _(sp, hc=hc, par=par):
                    C.wait(sp, "sp", M[f"norm3_{hc}"])
                    M[f"xst3_{hc}"] = C.dma(f"xst{par}", sp.dma_start(out=x1_scr[128 * hc:128 * (hc + 1), :], in_=xn[:, par]))
            C.join(block)

        with contextlib.ExitStack() as ps:
            x1bf = ps.enter_context(nc.sbuf_tensor("x1bf", [128, 32, TLOC], BF16))
            bwbuf = ps.enter_context(nc.sbuf_tensor("bwbuf", [128, 2, 32, 128], BF16))
            a2bf = ps.enter_context(nc.sbuf_tensor("a2bf", [128, 32, 128], BF16))
            b2bf = ps.enter_context(nc.sbuf_tensor("b2bf", [128, 32, 128], BF16))
            repb = ps.enter_context(nc.sbuf_tensor("repb", [8, 128], BF16))
            rwb = ps.enter_context(nc.sbuf_tensor("rwb", [128, 32, 8], BF16))
            lgT = ps.enter_context(nc.sbuf_tensor("lgT", [8, TLOC], F32))
            lg = ps.enter_context(nc.sbuf_tensor("lg", [128, 8, 8], F32))
            esb = ps.enter_context(nc.sbuf_tensor("esb", [128, 8, 8], F32))
            wd = ps.enter_context(nc.sbuf_tensor("wd", [128, 8, 8], F32))
            mrow = ps.enter_context(nc.sbuf_tensor("mrow", [128, 8, 4], F32))
            top8 = ps.enter_context(nc.sbuf_tensor("top8", [128, 8, 8], F32))
            wdT = ps.enter_context(nc.sbuf_tensor("wdT", [8, TLOC], BF16))
            w16 = ps.enter_context(nc.sbuf_tensor("w16", [128, TLOC], BF16))
            midg = ps.enter_context(nc.sbuf_tensor("midg", [128, TLOC], BF16))
            fb = ps.enter_context(nc.sbuf_tensor("fb", [128, TLOC], F32))
            x1f = ps.enter_context(nc.sbuf_tensor("x1f", [128, 2, TLOC], F32))
            r_sb = ps.enter_context(nc.sbuf_tensor("r_sb0", [128, 2, TLOC], F32))
            sq0 = ps.enter_context(nc.sbuf_tensor("sq0", [128, TLOC], F32))
            rrow0 = ps.enter_context(nc.sbuf_tensor("rrow0", [1, 6, TLOC], F32))
            xn0 = ps.enter_context(nc.sbuf_tensor("xn0", [128, 2, TLOC], F32))
            pA = ps.enter_context(nc.psum_tensor("pA", [128, TLOC], F32))
            pB = ps.enter_context(nc.psum_tensor("pB2", [128, TLOC], F32))
            pS = ps.enter_context(nc.psum_tensor("pS", [1, TLOC], F32))
            pQ = ps.enter_context(nc.psum_tensor("pQ", [1, TLOC], F32))

            @block.gpsimd
            def _(gp):
                C.dma("gc0", gp.dma_start(out=x1bf[:], in_=x1_scr.rearrange("(e p) t -> p e t", p=128)))
                C.dma("gc0", gp.dma_start(out=a2bf[:], in_=loraA2.rearrange("(e p) j -> p e j", p=128)))
                C.dma("gc0", gp.dma_start(out=b2bf[:], in_=loraB2.rearrange("p (o j) -> p o j", j=128)))
                C.dma("gc0", gp.dma_start(out=repb[:], in_=rep16[:]))
                C.dma("gc0", gp.dma_start(out=rwb[:], in_=router_WT.rearrange("(e p) j -> p e j", p=128)))
            GC0 = C.dma_total("gc0")

            @block.tensor
            def _(te):
                C.wait(te, "pe", GC0)
                last = None
                for ec in range(32):
                    te.ldweights(rwb[:, ec])
                    te.matmul(pA[0:8, 0:512], rwb[:, ec], x1bf[:, ec, 0:512], start=(ec == 0), stop=(ec == 31))
                    last = te.matmul(pA[0:8, 512:1024], rwb[:, ec], x1bf[:, ec, 512:1024], start=(ec == 0), stop=(ec == 31))
                M["lg_mm"] = C.bump("pe", last)
            @block.scalar
            def _(sc):
                C.wait(sc, "act", M["lg_mm"])
                M["lg_ev"] = C.bump("act", sc.copy(lgT[:], pA[0:8, :]))
            @block.tensor
            def _(te):
                C.wait(te, "pe", M["lg_ev"])
                last = None
                for ti in range(8):
                    last = te.transpose(pB[0:128, 8 * ti:8 * (ti + 1)], lgT[:, 128 * ti:128 * (ti + 1)], idt[0:8, 0:8])
                M["lg_tr"] = C.bump("pe", last)
            @block.scalar
            def _(sc):
                C.wait(sc, "act", M["lg_tr"])
                M["lg2"] = C.bump("act", sc.copy(lg[:].rearrange("p a b -> p (a b)"), pB[0:128, 0:64]))

            for ti in range(8):
                @block.vector
                def _(dv, ti=ti):
                    C.wait(dv, "dve", M["lg2"])
                    C.self_edge("dve", dv, dv.reduce_max(mrow[:, ti, 0:1], lg[:, ti], axis=AX.X))
                    M[f"mx_{ti}"] = C.bump("dve", dv.tensor_scalar(mrow[:, ti, 1:2], mrow[:, ti, 0:1], -1.0, None, OP.mult))
                @block.scalar
                def _(sc, ti=ti):
                    C.wait(sc, "act", M[f"mx_{ti}"])
                    M[f"e_{ti}"] = C.bump("act", sc.activation(esb[:, ti], lg[:, ti], AF.Exp, bias=mrow[:, ti, 1:2]))
                @block.vector
                def _(dv, ti=ti):
                    C.wait(dv, "dve", M[f"e_{ti}"])
                    C.self_edge("dve", dv, dv.max(top8[:, ti], esb[:, ti]))
                    C.self_edge("dve", dv, dv.tensor_tensor(mrow[:, ti, 2:3], top8[:, ti, 0:1], top8[:, ti, 1:2], OP.add))
                    C.self_edge("dve", dv, dv.reciprocal(mrow[:, ti, 3:4], mrow[:, ti, 2:3]))
                    C.self_edge("dve", dv, dv.tensor_scalar(wd[:, ti], esb[:, ti], top8[:, ti, 1:2], None, OP.is_ge))
                    C.self_edge("dve", dv, dv.tensor_tensor(wd[:, ti], wd[:, ti], esb[:, ti], OP.mult))
                    M[f"wd_{ti}"] = C.bump("dve", dv.tensor_scalar(wd[:, ti], wd[:, ti], mrow[:, ti, 3:4], None, OP.mult))

            @block.tensor
            def _(te):
                for ti in range(8):
                    C.wait(te, "pe", M[f"wd_{ti}"])
                last = None
                for ti in range(8):
                    last = te.transpose(pA[0:8, 128 * ti:128 * (ti + 1)], wd[:, ti], idt[:])
                M["wd_tr"] = C.bump("pe", last)
            @block.scalar
            def _(sc):
                C.wait(sc, "act", M["wd_tr"])
                M["wdT_ev"] = C.bump("act", sc.copy(wdT[:], pA[0:8, :]))
            @block.tensor
            def _(te):
                C.wait(te, "pe", M["wdT_ev"])
                te.ldweights(repb[:])
                te.matmul(pA[0:128, 0:512], repb[:], wdT[:, 0:512], start=True, stop=True)
                M["w16_mm"] = C.bump("pe", te.matmul(pA[0:128, 512:1024], repb[:], wdT[:, 512:1024], start=True, stop=True))
            @block.scalar
            def _(sc):
                C.wait(sc, "act", M["w16_mm"])
                M["w16_ev"] = C.bump("act", sc.copy(w16[:], pA[:]))
            @block.tensor
            def _(te):
                C.wait(te, "pe", M["w16_ev"])
                last = None
                for ec in range(32):
                    te.ldweights(a2bf[:, ec])
                    te.matmul(pB[:, 0:512], a2bf[:, ec], x1bf[:, ec, 0:512], start=(ec == 0), stop=(ec == 31))
                    last = te.matmul(pB[:, 512:1024], a2bf[:, ec], x1bf[:, ec, 512:1024], start=(ec == 0), stop=(ec == 31))
                M["mid_mm"] = C.bump("pe", last)
            @block.vector
            def _(dv):
                C.wait(dv, "dve", M["mid_mm"])
                M["midg"] = C.bump("dve", dv.tensor_tensor(midg[:], pB[:], w16[:], OP.mult))

            for oc in range(32):
                par = oc % 2
                @block.gpsimd
                def _(gp, oc=oc, par=par):
                    if oc >= 2:
                        C.wait(gp, "pool", M[f"bmm_{oc-2}"])
                    M[f"bwl_{oc}"] = C.dma(f"bwl{par}", gp.dma_start(
                        out=bwbuf[:, par],
                        in_=base_WT[:, 128 * oc:128 * (oc + 1)].rearrange("(e p) j -> p e j", p=128)))
                @block.sync
                def _(sp, oc=oc, par=par):
                    if oc >= 2:
                        C.wait(sp, "sp", M[f"resid0_{oc-2}"])
                    M[f"x1l_{oc}"] = C.dma(f"x1l{par}", sp.dma_start(out=x1f[:, par], in_=x1_scr[128 * oc:128 * (oc + 1), :]))
                @block.tensor
                def _(te, oc=oc, par=par):
                    C.wait(te, "pe", M[f"bwl_{oc}"])
                    C.wait(te, "pe", M["midg"])
                    if oc >= 1:
                        C.wait(te, "pe", M[f"bev_{oc-1}"])
                        C.wait(te, "pe", M[f"resid0_{oc-1}"])
                    last = None
                    for ec in range(32):
                        te.ldweights(bwbuf[:, par, ec])
                        te.matmul(pA[:, 0:512], bwbuf[:, par, ec], x1bf[:, ec, 0:512], start=(ec == 0), stop=(ec == 31))
                        last = te.matmul(pA[:, 512:1024], bwbuf[:, par, ec], x1bf[:, ec, 512:1024], start=(ec == 0), stop=(ec == 31))
                    M[f"bmm_{oc}"] = C.bump("pe", last)
                    te.ldweights(b2bf[:, oc])
                    te.matmul(pB[:, 0:512], b2bf[:, oc], midg[:, 0:512], start=True, stop=True)
                    M[f"imm_{oc}"] = C.bump("pe", te.matmul(pB[:, 512:1024], b2bf[:, oc], midg[:, 512:1024], start=True, stop=True))
                @block.scalar
                def _(sc, oc=oc):
                    C.wait(sc, "act", M[f"bmm_{oc}"])
                    if oc >= 1:
                        C.wait(sc, "act", M[f"resid0_{oc-1}"])
                    M[f"bev_{oc}"] = C.bump("act", sc.copy(fb[:], pA[:]))
                @block.vector
                def _(dv, oc=oc, par=par):
                    C.wait(dv, "dve", M[f"bev_{oc}"])
                    C.wait(dv, "dve", M[f"imm_{oc}"])
                    C.wait(dv, "dve", M[f"x1l_{oc}"])
                    if oc >= 2:
                        C.wait(dv, "dve", M[f"rst0_{oc-2}"])
                    dv.tensor_tensor(r_sb[:, par], fb[:], pB[:], OP.add)
                    M[f"resid0_{oc}"] = C.bump("dve", dv.tensor_tensor(r_sb[:, par], r_sb[:, par], x1f[:, par], OP.add))
                @block.scalar
                def _(sc, oc=oc, par=par):
                    C.wait(sc, "act", M[f"resid0_{oc}"])
                    if oc >= 1:
                        C.wait(sc, "act", M[f"stat0_{oc-1}"])
                    M[f"sq0_{oc}"] = C.bump("act", sc.activation(sq0[:], r_sb[:, par], AF.Square))
                @block.tensor
                def _(te, oc=oc, par=par):
                    C.wait(te, "pe", M[f"sq0_{oc}"])
                    te.matmul(pS[:, 0:512], onc[:, 0:1], r_sb[:, par, 0:512], start=(oc == 0), stop=(oc == 31))
                    te.matmul(pS[:, 512:1024], onc[:, 0:1], r_sb[:, par, 512:1024], start=(oc == 0), stop=(oc == 31))
                    te.matmul(pQ[:, 0:512], onc[:, 0:1], sq0[:, 0:512], start=(oc == 0), stop=(oc == 31))
                    M[f"stat0_{oc}"] = C.bump("pe", te.matmul(pQ[:, 512:1024], onc[:, 0:1], sq0[:, 512:1024], start=(oc == 0), stop=(oc == 31)))
                @block.sync
                def _(sp, oc=oc, par=par):
                    C.wait(sp, "sp", M[f"stat0_{oc}"])
                    M[f"rst0_{oc}"] = C.dma(f"rsb{par}", sp.dma_start(out=r_scr[128 * oc:128 * (oc + 1), :], in_=r_sb[:, par]))

            @block.vector
            def _(dv):
                C.wait(dv, "dve", M["stat0_31"])
                C.self_edge("dve", dv, dv.tensor_scalar(rrow0[:, 0], pS[:], 1.0 / H, None, OP.mult))
                C.self_edge("dve", dv, dv.tensor_scalar(rrow0[:, 1], pQ[:], 1.0 / H, None, OP.mult))
                C.self_edge("dve", dv, dv.tensor_tensor(rrow0[:, 2], rrow0[:, 0], rrow0[:, 0], OP.mult))
                C.self_edge("dve", dv, dv.tensor_tensor(rrow0[:, 3], rrow0[:, 1], rrow0[:, 2], OP.subtract))
                M["var0"] = C.bump("dve", dv.tensor_scalar(rrow0[:, 3], rrow0[:, 3], 1.0, 1e-5, OP.mult, OP.add))
            @block.scalar
            def _(sc):
                C.wait(sc, "act", M["var0"])
                M["sqrt0"] = C.bump("act", sc.activation(rrow0[:, 4], rrow0[:, 3], AF.Sqrt))
            @block.vector
            def _(dv):
                C.wait(dv, "dve", M["sqrt0"])
                M["rstd0"] = C.self_edge("dve", dv, dv.reciprocal(rrow0[:, 5], rrow0[:, 4]))
            @block.tensor
            def _(te):
                C.wait(te, "pe", M["rstd0"])
                te.matmul(pB[:, 0:512], onr[:], rrow0[:, 5, 0:512], start=True, stop=True)
                te.matmul(pB[:, 512:1024], onr[:], rrow0[:, 5, 512:1024], start=True, stop=True)
                te.matmul(pA[:, 0:512], onr[:], rrow0[:, 0, 0:512], start=True, stop=True)
                M["bc0"] = C.bump("pe", te.matmul(pA[:, 512:1024], onr[:], rrow0[:, 0, 512:1024], start=True, stop=True))
            for oc in range(32):
                par = oc % 2
                @block.sync
                def _(sp, oc=oc, par=par):
                    C.wait(sp, "sp", C.dma_total("rsb0"))
                    C.wait(sp, "sp", C.dma_total("rsb1"))
                    if oc >= 2:
                        C.wait(sp, "sp", M[f"norm0_{oc-2}"])
                    M[f"rrd0_{oc}"] = C.dma(f"rrd{par}", sp.dma_start(out=r_sb[:, par], in_=r_scr[128 * oc:128 * (oc + 1), :]))
                @block.vector
                def _(dv, oc=oc, par=par):
                    C.wait(dv, "dve", M[f"rrd0_{oc}"])
                    C.wait(dv, "dve", M["bc0"])
                    if oc >= 2:
                        C.wait(dv, "dve", M[f"xst0_{oc-2}"])
                    dv.tensor_tensor(xn0[:, par], r_sb[:, par], pA[:], OP.subtract)
                    M[f"norm0_{oc}"] = C.bump("dve", dv.tensor_tensor(xn0[:, par], xn0[:, par], pB[:], OP.mult))
                @block.sync
                def _(sp, oc=oc, par=par):
                    C.wait(sp, "sp", M[f"norm0_{oc}"])
                    M[f"xst0_{oc}"] = C.dma(f"xsc{par}", sp.dma_start(out=x2_scr[128 * oc:128 * (oc + 1), :], in_=xn0[:, par]))
            C.join(block)

        with contextlib.ExitStack() as ps:
            x2bf = ps.enter_context(nc.sbuf_tensor("x2bf", [128, 32, TLOC], BF16))
            wqb = ps.enter_context(nc.sbuf_tensor("wqb", [128, 2, 32, 128], BF16))
            evq = ps.enter_context(nc.sbuf_tensor("evq", [128, 2, TLOC], BF16))
            pq = ps.enter_context(nc.psum_tensor("pq", [128, TLOC], F32))

            @block.gpsimd
            def _(gp):
                C.dma("gcq", gp.dma_start(out=x2bf[:], in_=x2_scr.rearrange("(e p) t -> p e t", p=128)))
            GCQ = C.dma_total("gcq")
            for fc in range(34):
                par = fc % 2
                @block.gpsimd
                def _(gp, fc=fc, par=par):
                    if fc >= 2:
                        C.wait(gp, "pool", M[f"qmm_{fc-2}"])
                    M[f"qwl_{fc}"] = C.dma(f"qwl{par}", gp.dma_start(
                        out=wqb[:, par],
                        in_=w_qkv[:, 128 * fc:128 * (fc + 1)].rearrange("(e p) j -> p e j", p=128)))
                @block.tensor
                def _(te, fc=fc, par=par):
                    C.wait(te, "pe", M[f"qwl_{fc}"])
                    if fc == 0:
                        C.wait(te, "pe", GCQ)
                    if fc >= 1:
                        C.wait(te, "pe", M[f"qev_{fc-1}"])
                    last = None
                    for ec in range(32):
                        te.ldweights(wqb[:, par, ec])
                        te.matmul(pq[:, 0:512], wqb[:, par, ec], x2bf[:, ec, 0:512], start=(ec == 0), stop=(ec == 31))
                        last = te.matmul(pq[:, 512:1024], wqb[:, par, ec], x2bf[:, ec, 512:1024], start=(ec == 0), stop=(ec == 31))
                    M[f"qmm_{fc}"] = C.bump("pe", last)
                @block.scalar
                def _(sc, fc=fc, par=par):
                    C.wait(sc, "act", M[f"qmm_{fc}"])
                    if fc >= 2:
                        C.wait(sc, "act", M[f"qst_{fc-2}"])
                    M[f"qev_{fc}"] = C.bump("act", sc.copy(evq[:, par], pq[:]))
                @block.sync
                def _(sp, fc=fc, par=par):
                    C.wait(sp, "sp", M[f"qev_{fc}"])
                    if fc < 32:
                        row = 512 * (fc % 8) + 128 * (fc // 8)
                        M[f"qst_{fc}"] = C.dma(f"qst{par}", sp.dma_start(out=q_in[row:row + 128, :], in_=evq[:, par]))
                    else:
                        M[f"qst_{fc}"] = C.dma(f"qst{par}", sp.dma_start(
                            out=kv_in[128 * (fc - 32):128 * (fc - 31), :], in_=evq[:, par]))

            @block.gpsimd
            def _(gp):
                C.wait(gp, "pool", C.dma_total("qst0"))
                C.wait(gp, "pool", C.dma_total("qst1"))
                M["cc_q"] = C.bump("cc", gp.collective_compute(
                    "AllToAll", OP.bypass, replica_groups=RG,
                    ins=[q_in.tensor.ap().opt()], outs=[q_out.tensor.ap().opt()]))
                M["cc_kv"] = C.bump("cc", gp.collective_compute(
                    "AllGather", OP.bypass, replica_groups=RG,
                    ins=[kv_in.tensor.ap().opt()], outs=[kv_out.tensor.ap().opt()]))
            C.join(block)

        with contextlib.ExitStack() as ps:
            NQT = 16
            SCL = 128 ** -0.5
            qsb = ps.enter_context(nc.sbuf_tensor("qsb", [128, 4, T], BF16))
            ksb = ps.enter_context(nc.sbuf_tensor("ksb", [128, 64, 128], BF16))
            vtb = ps.enter_context(nc.sbuf_tensor("vtb", [128, 64, 128], BF16))
            vsb = ps.enter_context(nc.sbuf_tensor("vsb", [128, 64, 128], BF16))
            kcb = ps.enter_context(nc.sbuf_tensor("kcb", [128, 64], F32))
            mkb = ps.enter_context(nc.sbuf_tensor("mkb", [128, 4, 512], F32))
            slps = ps.enter_context(nc.sbuf_tensor("slps", [128, 4], F32))
            qrc = ps.enter_context(nc.sbuf_tensor("qrc", [128, 4, 512], F32))
            o128 = ps.enter_context(nc.sbuf_tensor("o128", [128, 128], BF16))
            slpk2 = ps.enter_context(nc.sbuf_tensor("slpk2", [128, 2, 64], F32))
            qtm = ps.enter_context(nc.sbuf_tensor("qtm", [128, 1], F32))
            qbq = ps.enter_context(nc.sbuf_tensor("qbq", [128, 2, 512], F32))
            probs = ps.enter_context(nc.sbuf_tensor("probs", [128, 4, 512], BF16))
            asb = ps.enter_context(nc.sbuf_tensor("asb", [128, 2, 512], F32))
            ssb = ps.enter_context(nc.sbuf_tensor("ssb", [128, 2, 512], F32))
            rrec = ps.enter_context(nc.sbuf_tensor("rrec", [128, 2, 512], F32))
            ob = ps.enter_context(nc.sbuf_tensor("ob", [128, 2, 512], BF16))
            pqk = ps.enter_context(nc.psum_tensor("pqk", [128, 4, 512], F32))
            pattn = ps.enter_context(nc.psum_tensor("pattn", [128, 512], F32))
            pbcS = ps.enter_context(nc.psum_tensor("pbcS", [128, 512], F32))
            pv = ps.enter_context(nc.psum_tensor("pv", [128, 2, 1024], BF16))

            @block.sync
            def _(sp):
                for k in range(32):
                    ow, gg = owner(k), gidx(k)
                    for hh in range(4):
                        C.dma("aql", sp.dma_start(
                            out=qsb[:, hh, 256 * k:256 * (k + 1)],
                            in_=q_out[512 * ow + 128 * hh:512 * ow + 128 * (hh + 1), 256 * gg:256 * (gg + 1)]))
                    C.dma("aql", sp.dma_start(
                        out=ksb[:].rearrange("p kt j -> p (kt j)")[:, 256 * k:256 * (k + 1)],
                        in_=kv_out[256 * ow:256 * ow + 128, 256 * gg:256 * (gg + 1)]))
                    C.dma("aql", sp.dma_start(
                        out=vtb[:].rearrange("p kt j -> p (kt j)")[:, 256 * k:256 * (k + 1)],
                        in_=kv_out[256 * ow + 128:256 * ow + 256, 256 * gg:256 * (gg + 1)]))
                C.dma("cst3", sp.dma_start(out=kcb[:], in_=kcol[:]))
                C.dma("cst3", sp.dma_start(out=mkb[:], in_=maskc[:]))
                C.dma("cst3", sp.dma_start(out=slps[:], in_=slopes_c[:]))
                C.dma("cst3", sp.dma_start(out=qrc[:].rearrange("p a b -> p (a b)"), in_=qrefc[:].rearrange("p a b -> p (a b)")))
                C.dma("cst3", sp.dma_start(out=o128[:], in_=ones128b[:]))
            AQL = C.dma_total("aql")
            CST3 = C.dma_total("cst3")

            actV_base = C.ecnt.get("act", 0)
            @block.tensor
            def _(te):
                C.wait(te, "pe", AQL)
                C.wait(te, "pe", CST0)
                for kt in range(64):
                    par = kt % 2
                    if kt >= 2:
                        C.wait(te, "pe", ("E", "act", actV_base + kt - 1))
                    M[f"vtr_{kt}"] = C.bump("pe", te.transpose(pv[:, par, 0:128], vtb[:, kt], idb[:]))
            @block.scalar
            def _(sc):
                for kt in range(64):
                    par = kt % 2
                    C.wait(sc, "act", M[f"vtr_{kt}"])
                    M[f"vev_{kt}"] = C.bump("act", sc.copy(vsb[:, kt], pv[:, par, 0:128]))
            assert C.ecnt["act"] == actV_base + 64

            pqk_free = {}
            for hh in range(4):
                sph = hh % 2
                @block.vector
                def _(dv, hh=hh, sph=sph):
                    C.wait(dv, "dve", CST3)
                    C.self_edge("dve", dv, dv.tensor_scalar(
                        slpk2[:, sph], kcb[:], slps[:, hh:hh + 1], None, OP.mult))

                for qt in range(NQT):
                    q0 = 512 * qt
                    nk = q0 // 128 + 4
                    qp = (hh * NQT + qt) % 2
                    act_base = C.ecnt["act"]
                    pe_base = C.ecnt["pe"]
                    ops = []
                    for kti in range(nk + 4):
                        if kti < nk:
                            ops.append(("qk", kti))
                        if kti >= 4:
                            ops.append(("pv", kti - 4))
                    aqk_val = {kt: pe_base + ops.index(("qk", kt)) + 1 for kt in range(nk)}
                    apr_val = {kt: act_base + kt + 1 for kt in range(nk)}
                    aaev_val = act_base + nk + 1
                    stt_ms = {}
                    apv_ms = {}
                    prev_afin = M.get(f"afin_{qp}")

                    @block.vector
                    def _(dv, hh=hh, q0=q0, qp=qp, nk=nk, sph=sph, aqk_val=aqk_val,
                          aaev_val=aaev_val, stt_ms=stt_ms):
                        C.self_edge("dve", dv, dv.tensor_scalar(
                            qtm[:], slps[:, hh:hh + 1], float(1 - q0), None, OP.mult))
                        dv.tensor_scalar(qbq[:, qp], qrc[:, hh], qtm[:, 0:1], None, OP.add)
                        for kt in range(nk):
                            par = kt % 4
                            diag = kt - (nk - 4)
                            C.wait(dv, "dve", ("E", "pe", aqk_val[kt]))
                            last = dv.scalar_tensor_tensor(
                                pqk[:, par], pqk[:, par], slpk2[:, sph, kt:kt + 1], qbq[:, qp], OP.add, OP.add)
                            if diag >= 0:
                                last = dv.tensor_tensor(pqk[:, par], pqk[:, par], mkb[:, diag], OP.add)
                            stt_ms[kt] = C.bump("dve", last)
                        C.wait(dv, "dve", ("E", "pe", pe_base + 2 * nk))
                        M[f"sclamp_{qp}"] = C.bump("dve", dv.tensor_scalar(
                            ssb[:, qp], pbcS[:], 1e-35, None, OP.max))
                        C.self_edge("dve", dv, dv.reciprocal(rrec[:, qp], ssb[:, qp]))
                        C.wait(dv, "dve", ("E", "act", aaev_val))
                        if f"aost_{qp}" in M:
                            C.wait(dv, "dve", M[f"aost_{qp}"])
                        M[f"afin_{qp}"] = C.bump("dve", dv.tensor_tensor(
                            ob[:, qp], asb[:, qp], rrec[:, qp], OP.mult))

                    @block.tensor
                    def _(te, hh=hh, q0=q0, qp=qp, nk=nk, ops=ops, apr_val=apr_val, apv_ms=apv_ms):
                        for what, kt in ops:
                            if what == "qk":
                                par = kt % 4
                                if kt >= 4:
                                    C.wait(te, "pe", ("E", "act", apr_val[kt - 4]))
                                elif par in pqk_free:
                                    C.wait(te, "pe", pqk_free[par])
                                te.ldweights(ksb[:, kt])
                                C.bump("pe", te.matmul(
                                    pqk[:, par], ksb[:, kt], qsb[:, hh, q0:q0 + 512], start=True, stop=True))
                            else:
                                par = kt % 4
                                C.wait(te, "pe", ("E", "act", apr_val[kt]))
                                st = (kt == 0)
                                sto = (kt == nk - 1)
                                if st:
                                    if "aaev_last" in M:
                                        C.wait(te, "pe", M["aaev_last"])
                                    if "sclamp_last" in M:
                                        C.wait(te, "pe", M["sclamp_last"])
                                te.ldweights(vsb[:, kt])
                                te.matmul(pattn[:], vsb[:, kt], probs[:, par], start=st, stop=sto)
                                te.ldweights(o128[:])
                                apv_ms[kt] = C.bump("pe", te.matmul(
                                    pbcS[:], o128[:], probs[:, par], start=st, stop=sto))
                    assert C.ecnt["pe"] == pe_base + 2 * nk, (C.ecnt["pe"], pe_base, nk)
                    for i in range(4):
                        pqk_free[(nk - 1 - i) % 4] = ("E", "act", apr_val[nk - 1 - i])
                    M["sclamp_last"] = M[f"sclamp_{qp}"]

                    @block.scalar
                    def _(sc, qp=qp, nk=nk, stt_ms=stt_ms, apv_ms=apv_ms, prev_afin=prev_afin):
                        for kt in range(nk):
                            par = kt % 4
                            C.wait(sc, "act", stt_ms[kt])
                            if kt >= 4:
                                C.wait(sc, "act", apv_ms[kt - 4])
                            elif f"probs_free_{par}" in M:
                                C.wait(sc, "act", M[f"probs_free_{par}"])
                            C.bump("act", sc.activation(
                                probs[:, par], pqk[:, par], AF.Exp, scale=SCL))
                        C.wait(sc, "act", apv_ms[nk - 1])
                        if prev_afin is not None:
                            C.wait(sc, "act", prev_afin)
                        M["aaev_last"] = M[f"aaev_{qp}"] = C.bump("act", sc.copy(asb[:, qp], pattn[:]))
                    assert C.ecnt["act"] == aaev_val, (C.ecnt["act"], aaev_val)
                    for i in range(4):
                        M[f"probs_free_{(nk - 1 - i) % 4}"] = apv_ms[nk - 1 - i]

                    @block.sync
                    def _(sp, hh=hh, qt=qt, qp=qp):
                        C.wait(sp, "sp", M[f"afin_{qp}"])
                        for ci in range(2):
                            k = 2 * qt + ci
                            ow, gg = owner(k), gidx(k)
                            C.dma(f"aost{qp}", sp.dma_start(
                                out=at_in[512 * ow + 128 * hh:512 * ow + 128 * (hh + 1), 256 * gg:256 * (gg + 1)],
                                in_=ob[:, qp, 256 * ci:256 * (ci + 1)]))
                        M[f"aost_{qp}"] = C.dma_total(f"aost{qp}")

            @block.gpsimd
            def _(gp):
                C.wait(gp, "pool", C.dma_total("aost0"))
                C.wait(gp, "pool", C.dma_total("aost1"))
                M["cc_at"] = C.bump("cc", gp.collective_compute(
                    "AllToAll", OP.bypass, replica_groups=RG,
                    ins=[at_in.tensor.ap().opt()], outs=[at_out.tensor.ap().opt()]))
            C.join(block)

        with contextlib.ExitStack() as ps:
            abf = ps.enter_context(nc.sbuf_tensor("abf", [128, 32, TLOC], BF16))
            wbufW = ps.enter_context(nc.sbuf_tensor("wbufW", [128, 2, 32, 128], BF16))
            x2t = ps.enter_context(nc.sbuf_tensor("x2tW", [128, 2, TLOC], F32))
            r_sb = ps.enter_context(nc.sbuf_tensor("r_sbW", [128, 2, TLOC], F32))
            sqW = ps.enter_context(nc.sbuf_tensor("sqW", [128, TLOC], F32))
            rrowW = ps.enter_context(nc.sbuf_tensor("rrowW", [1, 6, TLOC], F32))
            xnW = ps.enter_context(nc.sbuf_tensor("xnW", [128, 2, TLOC], F32))
            pW = ps.enter_context(nc.psum_tensor("pWW", [128, TLOC], F32))
            psum_s = ps.enter_context(nc.psum_tensor("pSW", [1, TLOC], F32))
            psum_q = ps.enter_context(nc.psum_tensor("pQW", [1, TLOC], F32))
            pbcW = ps.enter_context(nc.psum_tensor("pbcW", [128, TLOC], F32))

            @block.sync
            def _(sp):
                for g in range(32):
                    C.dma("awl", sp.dma_start(
                        out=abf[:, g],
                        in_=at_out[512 * (g % 8) + 128 * (g // 8):512 * (g % 8) + 128 * (g // 8) + 128, :]))
            AWL = C.dma_total("awl")

            for hc in range(32):
                par = hc % 2
                @block.gpsimd
                def _(gp, hc=hc, par=par):
                    if hc >= 2:
                        C.wait(gp, "pool", M[f"wmm_{hc-2}"])
                    M[f"wwl_{hc}"] = C.dma(f"wwl{par}", gp.dma_start(
                        out=wbufW[:, par],
                        in_=w_c[:, 128 * hc:128 * (hc + 1)].rearrange("(e p) j -> p e j", p=128)))
                @block.sync
                def _(sp, hc=hc, par=par):
                    if hc >= 2:
                        C.wait(sp, "sp", M[f"residW_{hc-2}"])
                    M[f"x2l_{hc}"] = C.dma(f"x2l{par}", sp.dma_start(out=x2t[:, par], in_=x2_scr[128 * hc:128 * (hc + 1), :]))
                @block.tensor
                def _(te, hc=hc, par=par):
                    C.wait(te, "pe", M[f"wwl_{hc}"])
                    if hc == 0:
                        C.wait(te, "pe", AWL)
                    if hc >= 1:
                        C.wait(te, "pe", M[f"residW_{hc-1}"])
                    last = None
                    for ec in range(32):
                        te.ldweights(wbufW[:, par, ec])
                        te.matmul(pW[:, 0:512], wbufW[:, par, ec], abf[:, ec, 0:512], start=(ec == 0), stop=(ec == 31))
                        last = te.matmul(pW[:, 512:1024], wbufW[:, par, ec], abf[:, ec, 512:1024], start=(ec == 0), stop=(ec == 31))
                    M[f"wmm_{hc}"] = C.bump("pe", last)
                @block.vector
                def _(dv, hc=hc, par=par):
                    C.wait(dv, "dve", M[f"wmm_{hc}"])
                    C.wait(dv, "dve", M[f"x2l_{hc}"])
                    if hc >= 2:
                        C.wait(dv, "dve", M[f"rstW_{hc-2}"])
                    M[f"residW_{hc}"] = C.bump("dve", dv.tensor_tensor(r_sb[:, par], pW[:], x2t[:, par], OP.add))
                @block.scalar
                def _(sc, hc=hc, par=par):
                    C.wait(sc, "act", M[f"residW_{hc}"])
                    if hc >= 1:
                        C.wait(sc, "act", M[f"statW_{hc-1}"])
                    M[f"sqW_{hc}"] = C.bump("act", sc.activation(sqW[:], r_sb[:, par], AF.Square))
                @block.tensor
                def _(te, hc=hc, par=par):
                    C.wait(te, "pe", M[f"sqW_{hc}"])
                    te.matmul(psum_s[:, 0:512], onc[:, 0:1], r_sb[:, par, 0:512], start=(hc == 0), stop=(hc == 31))
                    te.matmul(psum_s[:, 512:1024], onc[:, 0:1], r_sb[:, par, 512:1024], start=(hc == 0), stop=(hc == 31))
                    te.matmul(psum_q[:, 0:512], onc[:, 0:1], sqW[:, 0:512], start=(hc == 0), stop=(hc == 31))
                    M[f"statW_{hc}"] = C.bump("pe", te.matmul(psum_q[:, 512:1024], onc[:, 0:1], sqW[:, 512:1024], start=(hc == 0), stop=(hc == 31)))
                @block.sync
                def _(sp, hc=hc, par=par):
                    C.wait(sp, "sp", M[f"statW_{hc}"])
                    M[f"rstW_{hc}"] = C.dma(f"rsw{par}", sp.dma_start(out=r_scr[128 * hc:128 * (hc + 1), :], in_=r_sb[:, par]))

            @block.vector
            def _(dv):
                C.wait(dv, "dve", M["statW_31"])
                C.self_edge("dve", dv, dv.tensor_scalar(rrowW[:, 0], psum_s[:], 1.0 / H, None, OP.mult))
                C.self_edge("dve", dv, dv.tensor_scalar(rrowW[:, 1], psum_q[:], 1.0 / H, None, OP.mult))
                C.self_edge("dve", dv, dv.tensor_tensor(rrowW[:, 2], rrowW[:, 0], rrowW[:, 0], OP.mult))
                C.self_edge("dve", dv, dv.tensor_tensor(rrowW[:, 3], rrowW[:, 1], rrowW[:, 2], OP.subtract))
                M["varW"] = C.bump("dve", dv.tensor_scalar(rrowW[:, 3], rrowW[:, 3], 1.0, 1e-5, OP.mult, OP.add))
            @block.scalar
            def _(sc):
                C.wait(sc, "act", M["varW"])
                M["sqrtW"] = C.bump("act", sc.activation(rrowW[:, 4], rrowW[:, 3], AF.Sqrt))
            @block.vector
            def _(dv):
                C.wait(dv, "dve", M["sqrtW"])
                M["rstdW"] = C.self_edge("dve", dv, dv.reciprocal(rrowW[:, 5], rrowW[:, 4]))
            @block.tensor
            def _(te):
                C.wait(te, "pe", M["rstdW"])
                te.matmul(pbcW[:, 0:512], onr[:], rrowW[:, 5, 0:512], start=True, stop=True)
                te.matmul(pbcW[:, 512:1024], onr[:], rrowW[:, 5, 512:1024], start=True, stop=True)
                te.matmul(pW[:, 0:512], onr[:], rrowW[:, 0, 0:512], start=True, stop=True)
                M["bcW"] = C.bump("pe", te.matmul(pW[:, 512:1024], onr[:], rrowW[:, 0, 512:1024], start=True, stop=True))
            for hc in range(32):
                par = hc % 2
                @block.sync
                def _(sp, hc=hc, par=par):
                    C.wait(sp, "sp", C.dma_total("rsw0"))
                    C.wait(sp, "sp", C.dma_total("rsw1"))
                    if hc >= 2:
                        C.wait(sp, "sp", M[f"normW_{hc-2}"])
                    M[f"rrdW_{hc}"] = C.dma(f"rdw{par}", sp.dma_start(out=r_sb[:, par], in_=r_scr[128 * hc:128 * (hc + 1), :]))
                @block.vector
                def _(dv, hc=hc, par=par):
                    C.wait(dv, "dve", M[f"rrdW_{hc}"])
                    C.wait(dv, "dve", M["bcW"])
                    if hc >= 2:
                        C.wait(dv, "dve", M[f"xstW_{hc-2}"])
                    dv.tensor_tensor(xnW[:, par], r_sb[:, par], pW[:], OP.subtract)
                    M[f"normW_{hc}"] = C.bump("dve", dv.tensor_tensor(xnW[:, par], xnW[:, par], pbcW[:], OP.mult))
                @block.sync
                def _(sp, hc=hc, par=par):
                    C.wait(sp, "sp", M[f"normW_{hc}"])
                    M[f"xstW_{hc}"] = C.dma(f"xsw{par}", sp.dma_start(out=x3_scr[128 * hc:128 * (hc + 1), :], in_=xnW[:, par]))
            C.join(block)

        with contextlib.ExitStack() as ps:
            x3bf = ps.enter_context(nc.sbuf_tensor("x3bf", [128, 32, TLOC], BF16))
            rwb1 = ps.enter_context(nc.sbuf_tensor("rwb1", [128, 32, 64], BF16))
            rbb = ps.enter_context(nc.sbuf_tensor("rbb", [64, 1], F32))
            ebb = ps.enter_context(nc.sbuf_tensor("ebb", [64, 32, 128], BF16))
            lgTF = ps.enter_context(nc.sbuf_tensor("lgTF", [64, TLOC], F32))
            lgF = ps.enter_context(nc.sbuf_tensor("lgF", [128, 8, 64], F32))
            esbF = ps.enter_context(nc.sbuf_tensor("esbF", [128, 8, 64], F32))
            wdd = ps.enter_context(nc.sbuf_tensor("wdd", [128, 8, 65], F32))
            top8F = ps.enter_context(nc.sbuf_tensor("top8F", [128, 8, 8], F32))
            mrowF = ps.enter_context(nc.sbuf_tensor("mrowF", [128, 8, 4], F32))
            wdTF = ps.enter_context(nc.sbuf_tensor("wdTF", [65, TLOC], BF16))
            wsrow = ps.enter_context(nc.sbuf_tensor("wsrow", [1, TLOC], BF16))
            msbF = ps.enter_context(nc.sbuf_tensor("msbF", [128, TLOC], F32))
            x3f = ps.enter_context(nc.sbuf_tensor("x3fF", [128, 2, TLOC], F32))
            r_sb = ps.enter_context(nc.sbuf_tensor("r_sbF", [128, 2, TLOC], F32))
            sqF = ps.enter_context(nc.sbuf_tensor("sqF", [128, TLOC], F32))
            rrowF = ps.enter_context(nc.sbuf_tensor("rrowF", [1, 6, TLOC], F32))
            xnF = ps.enter_context(nc.sbuf_tensor("xnF", [128, 2, TLOC], F32))
            pA = ps.enter_context(nc.psum_tensor("pAF", [128, TLOC], F32))
            pB = ps.enter_context(nc.psum_tensor("pBF", [128, TLOC], F32))
            pS = ps.enter_context(nc.psum_tensor("pSF", [1, TLOC], F32))
            pQ = ps.enter_context(nc.psum_tensor("pQF", [1, TLOC], F32))

            @block.gpsimd
            def _(gp):
                C.dma("gcf", gp.dma_start(out=x3bf[:], in_=x3_scr.rearrange("(e p) t -> p e t", p=128)))
                C.dma("gcf", gp.dma_start(out=rwb1[:], in_=rwT1.rearrange("(e p) j -> p e j", p=128)))
                C.dma("gcf", gp.dma_start(out=ebb[:], in_=ebias.rearrange("p (o j) -> p o j", j=128)))
            @block.sync
            def _(sp):
                C.dma("cst4", sp.dma_start(out=rbb[:], in_=rb1[:]))
            GCF = C.dma_total("gcf")
            CST4 = C.dma_total("cst4")

            @block.tensor
            def _(te):
                C.wait(te, "pe", GCF)
                last = None
                for ec in range(32):
                    te.ldweights(rwb1[:, ec])
                    te.matmul(pA[0:64, 0:512], rwb1[:, ec], x3bf[:, ec, 0:512], start=(ec == 0), stop=(ec == 31))
                    last = te.matmul(pA[0:64, 512:1024], rwb1[:, ec], x3bf[:, ec, 512:1024], start=(ec == 0), stop=(ec == 31))
                M["flg_mm"] = C.bump("pe", last)
            @block.vector
            def _(dv):
                C.wait(dv, "dve", M["flg_mm"])
                C.wait(dv, "dve", CST4)
                M["flg_ev"] = C.bump("dve", dv.tensor_scalar(lgTF[:], pA[0:64, :], rbb[:, 0:1], None, OP.add))
            @block.tensor
            def _(te):
                C.wait(te, "pe", M["flg_ev"])
                last = None
                for ti in range(8):
                    last = te.transpose(pB[0:128, 64 * ti:64 * (ti + 1)], lgTF[:, 128 * ti:128 * (ti + 1)], idt[0:64, 0:64])
                M["flg_tr"] = C.bump("pe", last)
            @block.scalar
            def _(sc):
                C.wait(sc, "act", M["flg_tr"])
                M["flg2"] = C.bump("act", sc.copy(lgF[:].rearrange("p a b -> p (a b)"), pB[0:128, 0:512]))

            for ti in range(8):
                @block.vector
                def _(dv, ti=ti):
                    C.wait(dv, "dve", M["flg2"])
                    C.self_edge("dve", dv, dv.reduce_max(mrowF[:, ti, 0:1], lgF[:, ti], axis=AX.X))
                    M[f"fmx_{ti}"] = C.bump("dve", dv.tensor_scalar(mrowF[:, ti, 1:2], mrowF[:, ti, 0:1], -1.0, None, OP.mult))
                @block.scalar
                def _(sc, ti=ti):
                    C.wait(sc, "act", M[f"fmx_{ti}"])
                    M[f"fe_{ti}"] = C.bump("act", sc.activation(esbF[:, ti], lgF[:, ti], AF.Exp, bias=mrowF[:, ti, 1:2]))
                @block.vector
                def _(dv, ti=ti):
                    C.wait(dv, "dve", M[f"fe_{ti}"])
                    C.self_edge("dve", dv, dv.reduce_sum(mrowF[:, ti, 2:3], esbF[:, ti], axis=AX.X))
                    C.self_edge("dve", dv, dv.reciprocal(mrowF[:, ti, 3:4], mrowF[:, ti, 2:3]))
                    C.self_edge("dve", dv, dv.tensor_scalar(esbF[:, ti], esbF[:, ti], mrowF[:, ti, 3:4], None, OP.mult))
                    C.self_edge("dve", dv, dv.max(top8F[:, ti], esbF[:, ti]))
                    C.self_edge("dve", dv, dv.reduce_sum(wdd[:, ti, 64:65], top8F[:, ti], axis=AX.X))
                    C.self_edge("dve", dv, dv.tensor_scalar(wdd[:, ti, 0:64], esbF[:, ti], top8F[:, ti, 7:8], None, OP.is_ge))
                    M[f"fwd_{ti}"] = C.bump("dve", dv.tensor_tensor(wdd[:, ti, 0:64], wdd[:, ti, 0:64], esbF[:, ti], OP.mult))

            @block.tensor
            def _(te):
                for ti in range(8):
                    C.wait(te, "pe", M[f"fwd_{ti}"])
                last = None
                for ti in range(8):
                    last = te.transpose(pA[0:65, 128 * ti:128 * (ti + 1)], wdd[:, ti], idt[:])
                M["fwd_tr"] = C.bump("pe", last)
            @block.scalar
            def _(sc):
                C.wait(sc, "act", M["fwd_tr"])
                sc.copy(wdTF[:], pA[0:65, :])
                M["fwdT_ev"] = C.bump("act", sc.copy(wsrow[:], pA[64:65, :]))

            for hc in range(32):
                par = hc % 2
                @block.sync
                def _(sp, hc=hc, par=par):
                    if hc >= 2:
                        C.wait(sp, "sp", M[f"residF_{hc-2}"])
                    M[f"x3l_{hc}"] = C.dma(f"x3l{par}", sp.dma_start(out=x3f[:, par], in_=x3_scr[128 * hc:128 * (hc + 1), :]))
                @block.tensor
                def _(te, hc=hc, par=par):
                    C.wait(te, "pe", M["fwdT_ev"])
                    if hc >= 1:
                        C.wait(te, "pe", M[f"fmev_{hc-1}"])
                        C.wait(te, "pe", M[f"residF_{hc-1}"])
                    te.ldweights(ebb[:, hc])
                    te.matmul(pB[:, 0:512], ebb[:, hc], wdTF[0:64, 0:512], start=True, stop=True)
                    M[f"fmix_{hc}"] = C.bump("pe", te.matmul(pB[:, 512:1024], ebb[:, hc], wdTF[0:64, 512:1024], start=True, stop=True))
                    te.ldweights(onrb[:])
                    te.matmul(pA[:, 0:512], onrb[:], wsrow[:, 0:512], start=True, stop=True)
                    M[f"fws_{hc}"] = C.bump("pe", te.matmul(pA[:, 512:1024], onrb[:], wsrow[:, 512:1024], start=True, stop=True))
                @block.scalar
                def _(sc, hc=hc):
                    C.wait(sc, "act", M[f"fmix_{hc}"])
                    if hc >= 1:
                        C.wait(sc, "act", M[f"residF_{hc-1}"])
                    M[f"fmev_{hc}"] = C.bump("act", sc.copy(msbF[:], pB[:]))
                @block.vector
                def _(dv, hc=hc, par=par):
                    C.wait(dv, "dve", M[f"fws_{hc}"])
                    C.wait(dv, "dve", M[f"fmev_{hc}"])
                    C.wait(dv, "dve", M[f"x3l_{hc}"])
                    if hc >= 2:
                        C.wait(dv, "dve", M[f"rstF_{hc-2}"])
                    dv.tensor_tensor(r_sb[:, par], x3f[:, par], pA[:], OP.mult)
                    dv.tensor_tensor(r_sb[:, par], r_sb[:, par], msbF[:], OP.add)
                    M[f"residF_{hc}"] = C.bump("dve", dv.tensor_tensor(r_sb[:, par], r_sb[:, par], x3f[:, par], OP.add))
                @block.scalar
                def _(sc, hc=hc, par=par):
                    C.wait(sc, "act", M[f"residF_{hc}"])
                    if hc >= 1:
                        C.wait(sc, "act", M[f"statF_{hc-1}"])
                    M[f"sqF_{hc}"] = C.bump("act", sc.activation(sqF[:], r_sb[:, par], AF.Square))
                @block.tensor
                def _(te, hc=hc, par=par):
                    C.wait(te, "pe", M[f"sqF_{hc}"])
                    te.matmul(pS[:, 0:512], onc[:, 0:1], r_sb[:, par, 0:512], start=(hc == 0), stop=(hc == 31))
                    te.matmul(pS[:, 512:1024], onc[:, 0:1], r_sb[:, par, 512:1024], start=(hc == 0), stop=(hc == 31))
                    te.matmul(pQ[:, 0:512], onc[:, 0:1], sqF[:, 0:512], start=(hc == 0), stop=(hc == 31))
                    M[f"statF_{hc}"] = C.bump("pe", te.matmul(pQ[:, 512:1024], onc[:, 0:1], sqF[:, 512:1024], start=(hc == 0), stop=(hc == 31)))
                @block.sync
                def _(sp, hc=hc, par=par):
                    C.wait(sp, "sp", M[f"statF_{hc}"])
                    M[f"rstF_{hc}"] = C.dma(f"rsf{par}", sp.dma_start(out=r_scr[128 * hc:128 * (hc + 1), :], in_=r_sb[:, par]))

            @block.vector
            def _(dv):
                C.wait(dv, "dve", M["statF_31"])
                C.self_edge("dve", dv, dv.tensor_scalar(rrowF[:, 0], pS[:], 1.0 / H, None, OP.mult))
                C.self_edge("dve", dv, dv.tensor_scalar(rrowF[:, 1], pQ[:], 1.0 / H, None, OP.mult))
                C.self_edge("dve", dv, dv.tensor_tensor(rrowF[:, 2], rrowF[:, 0], rrowF[:, 0], OP.mult))
                C.self_edge("dve", dv, dv.tensor_tensor(rrowF[:, 3], rrowF[:, 1], rrowF[:, 2], OP.subtract))
                M["varF"] = C.bump("dve", dv.tensor_scalar(rrowF[:, 3], rrowF[:, 3], 1.0, 1e-5, OP.mult, OP.add))
            @block.scalar
            def _(sc):
                C.wait(sc, "act", M["varF"])
                M["sqrtF"] = C.bump("act", sc.activation(rrowF[:, 4], rrowF[:, 3], AF.Sqrt))
            @block.vector
            def _(dv):
                C.wait(dv, "dve", M["sqrtF"])
                M["rstdF"] = C.self_edge("dve", dv, dv.reciprocal(rrowF[:, 5], rrowF[:, 4]))
            @block.tensor
            def _(te):
                C.wait(te, "pe", M["rstdF"])
                te.matmul(pB[:, 0:512], onr[:], rrowF[:, 5, 0:512], start=True, stop=True)
                te.matmul(pB[:, 512:1024], onr[:], rrowF[:, 5, 512:1024], start=True, stop=True)
                te.matmul(pA[:, 0:512], onr[:], rrowF[:, 0, 0:512], start=True, stop=True)
                M["bcF"] = C.bump("pe", te.matmul(pA[:, 512:1024], onr[:], rrowF[:, 0, 512:1024], start=True, stop=True))
            for hc in range(32):
                par = hc % 2
                @block.sync
                def _(sp, hc=hc, par=par):
                    C.wait(sp, "sp", C.dma_total("rsf0"))
                    C.wait(sp, "sp", C.dma_total("rsf1"))
                    if hc >= 2:
                        C.wait(sp, "sp", M[f"normF_{hc-2}"])
                    M[f"rrdF_{hc}"] = C.dma(f"rdf{par}", sp.dma_start(out=r_sb[:, par], in_=r_scr[128 * hc:128 * (hc + 1), :]))
                @block.vector
                def _(dv, hc=hc, par=par):
                    C.wait(dv, "dve", M[f"rrdF_{hc}"])
                    C.wait(dv, "dve", M["bcF"])
                    if hc >= 2:
                        C.wait(dv, "dve", M[f"xstF_{hc-2}"])
                    dv.tensor_tensor(xnF[:, par], r_sb[:, par], pA[:], OP.subtract)
                    M[f"normF_{hc}"] = C.bump("dve", dv.tensor_tensor(xnF[:, par], xnF[:, par], pB[:], OP.mult))
                @block.sync
                def _(sp, hc=hc, par=par):
                    C.wait(sp, "sp", M[f"normF_{hc}"])
                    M[f"xstF_{hc}"] = C.dma(f"xff{par}", sp.dma_start(out=x4_out[128 * hc:128 * (hc + 1), :], in_=xnF[:, par]))

            @block.gpsimd
            def _(gp):
                C.wait(gp, "pool", C.dma_total("xff0"))
                C.wait(gp, "pool", C.dma_total("xff1"))
    return nc


def stage_fused(d):
    x0 = np.ascontiguousarray(d["L0_attn_hidden_states"].reshape(T, H).T)
    esl = np.zeros((NST, NST * 128), np.float32)
    for n in range(NST):
        esl[n, 128 * n:128 * (n + 1)] = 1.0
    rep = np.zeros((8, 128), np.float32)
    for e in range(8):
        rep[e, 16 * e:16 * (e + 1)] = 2.0
    A2 = d["L0_ffn_lora_A"].transpose(1, 0, 2).reshape(H, 128)
    maskc = np.where((128 * np.arange(4)[None, :, None] + np.arange(128)[:, None, None]) > np.arange(512)[None, None, :],
                     np.float32(-1e30), np.float32(0.0)).astype(np.float32)
    kcolv = (128 * np.arange(64)[None, :] + np.arange(128)[:, None]).astype(np.float32)
    slopes = np.asarray(alibi_slopes_py(32), np.float32)
    WdtT = np.ascontiguousarray(d["L0_attn_Wdt"].T)
    A = d["L0_attn_A"]
    dtbv = d["L0_attn_dt_bias"]
    Dpv = d["L0_attn_D_param"]
    common = dict(
        W_inT=np.ascontiguousarray(d["L0_attn_W_in"].T),
        WxT=np.ascontiguousarray(d["L0_attn_Wx"].T),
        conv_w=np.ascontiguousarray(d["L0_attn_conv_w"]),
        conv_b2=np.ascontiguousarray(d["L0_attn_conv_b"].reshape(64, 128).T),
        dtlnw2=np.ascontiguousarray(d["L0_attn_dt_ln_w"].reshape(2, 128).T),
        b_ln_w=np.ascontiguousarray(d["L0_attn_b_ln_w"]),
        c_ln_w=np.ascontiguousarray(d["L0_attn_c_ln_w"]),
        esel=esl.astype(bf16),
        W_outT=np.ascontiguousarray(d["L0_attn_W_out"].T),
        base_WT=np.ascontiguousarray(d["L0_ffn_base_W"].T),
        router_WT=np.ascontiguousarray(d["L0_ffn_router_W"].T),
        loraA2=np.ascontiguousarray(A2),
        loraB2=np.ascontiguousarray(d["L0_ffn_lora_B"].reshape(128, H)),
        rep16=rep,
        w_qkv=np.ascontiguousarray(d["L1_attn_w_qkv"]),
        w_c=np.ascontiguousarray(d["L1_attn_w_c"]),
        rwT1=np.ascontiguousarray(d["L1_ffn_router_weight"].T),
        rb1=np.ascontiguousarray(d["L1_ffn_router_bias"].reshape(64, 1)),
        ebias=np.ascontiguousarray(d["L1_ffn_expert_bias"]),
        ident=np.eye(128, dtype=np.float32),
        identb=np.eye(128, dtype=np.float32).astype(bf16),
        ones_col=np.ones((128, 1), np.float32),
        ones_row=np.ones((1, 128), np.float32),
        onesb=np.ones((128, 1), np.float32).astype(bf16),
        onesrb=np.ones((1, 128), np.float32).astype(bf16),
        kcol=kcolv,
        maskc=maskc,
        ones128b=np.ones((128, 128), np.float32).astype(bf16),
    )
    WS_ATTN = [64, 256, 512, 512]
    SCL_ATTN = 128 ** -0.5
    maps = []
    for c in range(NCORE):
        cols, pres = [], []
        for k in groups_of(c):
            cols.append(x0[:, G * k:G * (k + 1)])
            pres.append(np.zeros((H, 3), np.float32) if k == 0 else x0[:, G * k - 3:G * k])
        rows = slice(1024 * c, 1024 * (c + 1))
        Ac = A[rows].reshape(8, 128, NST).transpose(1, 0, 2).reshape(128, 8 * NST)
        heads = [c, 8 + c, 16 + c, 24 + c]
        maps.append(dict(
            common,
            x0T=np.ascontiguousarray(np.concatenate(cols, axis=1)),
            x0Tpre=np.ascontiguousarray(np.concatenate(pres + [np.zeros((H, 4), np.float32)], axis=1)),
            wdtT_c=np.ascontiguousarray(WdtT[:, rows]),
            A_c=np.ascontiguousarray(Ac.astype(np.float32)),
            dtb_c=np.ascontiguousarray(dtbv[rows].reshape(8, 128).T.astype(np.float32)),
            Dp_c=np.ascontiguousarray(Dpv[rows].reshape(8, 128).T.astype(np.float32)),
            slopes_c=np.ascontiguousarray(np.broadcast_to(
                (slopes[heads] / SCL_ATTN)[None, :], (128, 4)).astype(np.float32)),
            qrefc=np.ascontiguousarray(np.broadcast_to(np.stack([
                -(slopes[heads[s]] / SCL_ATTN) * ((np.arange(512) // WS_ATTN[s] + 1) * WS_ATTN[s]).astype(np.float64)
                for s in range(4)], axis=0).astype(np.float32)[None, :, :], (128, 4, 512))),
        ))
    return maps


def gather_tokens(per_core):
    rows = per_core[0].shape[0]
    out = np.empty((rows, T), per_core[0].dtype)
    for c in range(NCORE):
        for g, k in enumerate(groups_of(c)):
            out[:, G * k:G * (k + 1)] = per_core[c][:, G * g:G * (g + 1)]
    return out
_PROGS = {}

def kernel(**inputs):
    d = {k: np.ascontiguousarray(np.asarray(v)) for k, v in inputs.items()}
    if "nc" not in _PROGS:
        with allow_ici_all_to_all():
            _PROGS["nc"] = build_fused()
    nc = _PROGS["nc"]
    maps = stage_fused(d)
    res = run_bass_kernel_spmd(nc, maps, core_ids=list(range(NCORE)))
    x4_full = gather_tokens([res.results[c]["x4_out"] for c in range(NCORE)])
    return np.ascontiguousarray(x4_full.T).reshape(1, T, H).astype(np.float32)


# version 09
# speedup vs baseline: 1.4267x; 1.0033x over previous
import os as _os
import numpy as np
import ml_dtypes
import contextlib
import concourse.bass as bass
import concourse.mybir as mybir
from concourse.bass_utils import run_bass_kernel_spmd
from concourse.aot_env import allow_ici_all_to_all

F32 = mybir.dt.float32
BF16 = mybir.dt.bfloat16
AF = mybir.ActivationFunctionType
OP = mybir.AluOpType
AX = mybir.AxisListType
bf16 = ml_dtypes.bfloat16

NCORE = 8
T = 8192
H = 4096
D = 8192
NST = 16
KCV = 4
RDT = 256
TLOC = 1024
G = 256
NGRP = 4
RG = [list(range(NCORE))]

def groups_of(c):
    return [c, 15 - c, 16 + c, 31 - c]

def owner(k):
    if k < 8: return k
    if k < 16: return 15 - k
    if k < 24: return k - 16
    return 31 - k

def gidx(k):
    return 0 if k < 8 else (1 if k < 16 else (2 if k < 24 else 3))

def alibi_slopes_py(n=32):
    import math
    cp2 = 2 ** int(math.floor(math.log2(n)))
    base = 2.0 ** (-(2.0 ** (-(math.log2(cp2) - 3))))
    slopes = [base ** i for i in range(1, cp2 + 1)]
    if cp2 != n:
        extra_base = 2.0 ** (-(2.0 ** (-(math.log2(2 * cp2) - 3))))
        rem = min(cp2, n - cp2)
        slopes += [extra_base ** i for i in range(1, 1 + 2 * rem, 2)]
    return slopes


class Ctx:
    def __init__(self, nc, es):
        self.nc = nc
        self.es = es
        self.esem = {}
        self.ecnt = {}
        self.dsem = {}
        self.dcnt = {}
        self.waited = {}

    def bump(self, eng_name, instr, n=1):
        if eng_name not in self.esem:
            self.esem[eng_name] = self.es.enter_context(self.nc.semaphore(f"se_{eng_name}"))
            self.ecnt[eng_name] = 0
        self.ecnt[eng_name] += n
        instr.then_inc(self.esem[eng_name], n)
        return ("E", eng_name, self.ecnt[eng_name])

    def dma(self, stream, instr):
        if stream not in self.dsem:
            self.dsem[stream] = self.es.enter_context(self.nc.semaphore(f"sd_{stream}"))
            self.dcnt[stream] = 0
        self.dcnt[stream] += 16
        instr.then_inc(self.dsem[stream], 16)
        return ("D", stream, self.dcnt[stream])

    def wait(self, eng_obj, waiter_name, milestone):
        kind, key, val = milestone
        if kind == "E" and key == waiter_name:
            return
        wkey = (waiter_name, kind, key)
        if self.waited.get(wkey, -1) >= val:
            return
        sem = self.esem[key] if kind == "E" else self.dsem[key]
        eng_obj.wait_ge(sem, val)
        self.waited[wkey] = val

    def dma_total(self, stream):
        return ("D", stream, self.dcnt.get(stream, 0))

    def self_edge(self, eng_name, eng_obj, instr):
        m = self.bump(eng_name, instr)
        eng_obj.wait_ge(self.esem[eng_name], m[2])
        return m

    def join(self, block):
        snap_e = dict(self.ecnt)
        snap_d = dict(self.dcnt)
        for engname, attr in (("pe", "tensor"), ("act", "scalar"), ("dve", "vector"),
                              ("pool", "gpsimd"), ("sp", "sync")):
            def body(eng, engname=engname):
                for k2, v in snap_e.items():
                    if v and k2 != engname:
                        self.wait(eng, engname, ("E", k2, v))
                for k2, v in snap_d.items():
                    if v:
                        self.wait(eng, engname, ("D", k2, v))
            getattr(block, attr)(body)


def build_fused():
    nc = bass.Bass("TRN2", target_bir_lowering=False, debug=False, detect_race_conditions=not _os.environ.get("BASS_NO_RD"))
    x0T = nc.dram_tensor("x0T", [H, TLOC], F32, kind="ExternalInput").ap()
    x0Tpre = nc.dram_tensor("x0Tpre", [H, 16], F32, kind="ExternalInput").ap()
    W_inT = nc.dram_tensor("W_inT", [H, 2 * D], F32, kind="ExternalInput").ap()
    WxT = nc.dram_tensor("WxT", [D, 288], F32, kind="ExternalInput").ap()
    conv_w = nc.dram_tensor("conv_w", [D, KCV], F32, kind="ExternalInput").ap()
    conv_b2 = nc.dram_tensor("conv_b2", [128, 64], F32, kind="ExternalInput").ap()
    dtlnw2 = nc.dram_tensor("dtlnw2", [128, 2], F32, kind="ExternalInput").ap()
    b_ln_w = nc.dram_tensor("b_ln_w", [NST], F32, kind="ExternalInput").ap()
    c_ln_w = nc.dram_tensor("c_ln_w", [NST], F32, kind="ExternalInput").ap()
    wdtT_c = nc.dram_tensor("wdtT_c", [RDT, 1024], F32, kind="ExternalInput").ap()
    A_c = nc.dram_tensor("A_c", [128, 8 * NST], F32, kind="ExternalInput").ap()
    dtb_c = nc.dram_tensor("dtb_c", [128, 8], F32, kind="ExternalInput").ap()
    Dp_c = nc.dram_tensor("Dp_c", [128, 8], F32, kind="ExternalInput").ap()
    esel = nc.dram_tensor("esel", [NST, NST * 128], BF16, kind="ExternalInput").ap()
    W_outT = nc.dram_tensor("W_outT", [D, H], F32, kind="ExternalInput").ap()
    base_WT = nc.dram_tensor("base_WT", [H, H], F32, kind="ExternalInput").ap()
    router_WT = nc.dram_tensor("router_WT", [H, 8], F32, kind="ExternalInput").ap()
    loraA2 = nc.dram_tensor("loraA2", [H, 128], F32, kind="ExternalInput").ap()
    loraB2 = nc.dram_tensor("loraB2", [128, H], F32, kind="ExternalInput").ap()
    rep16 = nc.dram_tensor("rep16", [8, 128], F32, kind="ExternalInput").ap()
    w_qkv = nc.dram_tensor("w_qkv", [H, 4352], F32, kind="ExternalInput").ap()
    w_c = nc.dram_tensor("w_c", [H, H], F32, kind="ExternalInput").ap()
    rwT1 = nc.dram_tensor("rwT1", [H, 64], F32, kind="ExternalInput").ap()
    rb1 = nc.dram_tensor("rb1", [64, 1], F32, kind="ExternalInput").ap()
    ebias = nc.dram_tensor("ebias", [64, H], F32, kind="ExternalInput").ap()
    ident = nc.dram_tensor("ident", [128, 128], F32, kind="ExternalInput").ap()
    identb = nc.dram_tensor("identb", [128, 128], BF16, kind="ExternalInput").ap()
    ones_col = nc.dram_tensor("ones_col", [128, 1], F32, kind="ExternalInput").ap()
    ones_row = nc.dram_tensor("ones_row", [1, 128], F32, kind="ExternalInput").ap()
    onesb = nc.dram_tensor("onesb", [128, 1], BF16, kind="ExternalInput").ap()
    onesrb = nc.dram_tensor("onesrb", [1, 128], BF16, kind="ExternalInput").ap()
    kcol = nc.dram_tensor("kcol", [128, 64], F32, kind="ExternalInput").ap()
    maskc = nc.dram_tensor("maskc", [128, 4, 512], F32, kind="ExternalInput").ap()
    slopes_c = nc.dram_tensor("slopes_c", [128, 4], F32, kind="ExternalInput").ap()
    qrefc = nc.dram_tensor("qrefc", [128, 4, 512], F32, kind="ExternalInput").ap()
    ones128b = nc.dram_tensor("ones128b", [128, 128], BF16, kind="ExternalInput").ap()
    x4_out = nc.dram_tensor("x4_out", [H, TLOC], F32, kind="ExternalOutput").ap()
    v_scr = nc.dram_tensor("v_scr", [D, TLOC], BF16, kind="Internal").ap()
    gate_scr = nc.dram_tensor("gate_scr", [D, TLOC], BF16, kind="Internal").ap()
    dtn_scr = nc.dram_tensor("dtn_scr", [288, TLOC], BF16, kind="Internal").ap()
    v_a2a = nc.dram_tensor("v_a2a", [D, TLOC], BF16, kind="Internal").ap()
    dtn_ag = nc.dram_tensor("dtn_ag", [NCORE * 288, TLOC], BF16, kind="Internal", addr_space="Shared").ap()
    y_in = nc.dram_tensor("y_in", [D, TLOC], BF16, kind="Internal").ap()
    y_out = nc.dram_tensor("y_out2", [D, TLOC], BF16, kind="Internal").ap()
    x1_scr = nc.dram_tensor("x1_scr", [H, TLOC], F32, kind="Internal").ap()
    x2_scr = nc.dram_tensor("x2_scr", [H, TLOC], F32, kind="Internal").ap()
    x3_scr = nc.dram_tensor("x3_scr", [H, TLOC], F32, kind="Internal").ap()
    r_scr = nc.dram_tensor("r_scr", [H, TLOC], F32, kind="Internal").ap()
    q_in = nc.dram_tensor("q_in", [H, TLOC], BF16, kind="Internal").ap()
    q_out = nc.dram_tensor("q_out", [H, TLOC], BF16, kind="Internal").ap()
    kv_in = nc.dram_tensor("kv_in", [256, TLOC], BF16, kind="Internal").ap()
    kv_out = nc.dram_tensor("kv_out", [NCORE * 256, TLOC], BF16, kind="Internal", addr_space="Shared").ap()
    at_in = nc.dram_tensor("at_in", [H, TLOC], BF16, kind="Internal").ap()
    at_out = nc.dram_tensor("at_out", [H, TLOC], BF16, kind="Internal").ap()

    es = contextlib.ExitStack()
    with es:
        onc = es.enter_context(nc.sbuf_tensor("onc", [128, 1], F32))
        onr = es.enter_context(nc.sbuf_tensor("onr", [1, 128], F32))
        onb = es.enter_context(nc.sbuf_tensor("onb", [128, 1], BF16))
        onrb = es.enter_context(nc.sbuf_tensor("onrb", [1, 128], BF16))
        idt = es.enter_context(nc.sbuf_tensor("idt", [128, 128], F32))
        idb = es.enter_context(nc.sbuf_tensor("idb", [128, 128], BF16))
        block = es.enter_context(nc.Block())
        C = Ctx(nc, es)
        M = {}

        @block.sync
        def _(sp):
            C.dma("cst", sp.dma_start(out=onc[:], in_=ones_col[:]))
            C.dma("cst", sp.dma_start(out=onr[:], in_=ones_row[:]))
            C.dma("cst", sp.dma_start(out=onb[:], in_=onesb[:]))
            C.dma("cst", sp.dma_start(out=onrb[:], in_=onesrb[:]))
            C.dma("cst", sp.dma_start(out=idt[:], in_=ident[:]))
            C.dma("cst", sp.dma_start(out=idb[:], in_=identb[:]))
        CST0 = C.dma_total("cst")

        with contextlib.ExitStack() as ps:
            x0bf = ps.enter_context(nc.sbuf_tensor("x0bf", [128, 32, TLOC], BF16))
            x0pre_bf = ps.enter_context(nc.sbuf_tensor("x0pre_bf", [128, 32, 16], BF16))
            wbuf = ps.enter_context(nc.sbuf_tensor("wbuf", [128, 2, 32, 128], BF16))
            wxbuf = ps.enter_context(nc.sbuf_tensor("wxbuf", [128, 64, 288], BF16))
            cwb = ps.enter_context(nc.sbuf_tensor("cwb", [128, 64, KCV], F32))
            cbb = ps.enter_context(nc.sbuf_tensor("cbb", [128, 64], F32))
            lnw = ps.enter_context(nc.sbuf_tensor("lnw", [128, 4], F32))
            u_pad = ps.enter_context(nc.sbuf_tensor("u_pad", [128, NGRP, G + 3], F32))
            vt = ps.enter_context(nc.sbuf_tensor("vt", [128, NGRP, G], F32))
            v_sb = ps.enter_context(nc.sbuf_tensor("v_sb", [128, 2, TLOC], BF16))
            g_sb = ps.enter_context(nc.sbuf_tensor("g_sb", [128, 2, TLOC], BF16))
            vwx = ps.enter_context(nc.sbuf_tensor("vwx", [128, 32, 512], BF16))
            ssm_raw = ps.enter_context(nc.sbuf_tensor("ssm_raw", [128, 2, 512], F32))
            bc_raw = ps.enter_context(nc.sbuf_tensor("bc_raw", [16, 2, 512], F32))
            sq1 = ps.enter_context(nc.sbuf_tensor("sq1", [128, 2, 512], F32))
            sqbc = ps.enter_context(nc.sbuf_tensor("sqbc", [16, 2, 512], F32))
            rrow = ps.enter_context(nc.sbuf_tensor("rrow1", [1, 10, 512], F32))
            dtn_sb = ps.enter_context(nc.sbuf_tensor("dtn_sb", [128, 3, 512], BF16))
            pproj = ps.enter_context(nc.psum_tensor("pproj", [128, 1040], F32))
            pwx1 = ps.enter_context(nc.psum_tensor("pwx1", [128, 512], F32))
            pwx2 = ps.enter_context(nc.psum_tensor("pwx2", [128, 512], F32))
            pwxbc = ps.enter_context(nc.psum_tensor("pwxbc", [64, 512], F32))
            pstat = ps.enter_context(nc.psum_tensor("pstat", [1, 512], F32))
            pbcast = ps.enter_context(nc.psum_tensor("pbcast", [128, 512], F32))

            @block.gpsimd
            def _(gp):
                C.dma("gcst", gp.dma_start(out=x0bf[:], in_=x0T.rearrange("(e p) t -> p e t", p=128)))
                C.dma("gcst", gp.dma_start(out=x0pre_bf[:, :, 0:12], in_=x0Tpre[:, 0:12].rearrange("(e p) t -> p e t", p=128)))
                C.dma("gcst", gp.dma_start(out=wxbuf[:], in_=WxT.rearrange("(f p) r -> p f r", p=128)))
            @block.sync
            def _(sp):
                C.dma("cst", sp.dma_start(out=cwb[:], in_=conv_w.rearrange("(f p) k -> p f k", p=128)))
                C.dma("cst", sp.dma_start(out=cbb[:], in_=conv_b2[:]))
                C.dma("cst", sp.dma_start(out=lnw[:, 0:2], in_=dtlnw2[:]))
                C.dma("cst", sp.dma_start(out=lnw[0:16, 2:3], in_=b_ln_w.rearrange("(n o) -> n o", o=1)))
                C.dma("cst", sp.dma_start(out=lnw[0:16, 3:4], in_=c_ln_w.rearrange("(n o) -> n o", o=1)))
            CST = C.dma_total("cst")
            GCST = C.dma_total("gcst")

            ev_mark = {}
            for f in range(128):
                uhalf = f < 64
                @block.gpsimd
                def _(gp, f=f):
                    if f >= 2:
                        C.wait(gp, "pool", M[f"pe_{f-2}"])
                    M[f"w_{f}"] = C.dma(f"wb{f % 2}", gp.dma_start(
                        out=wbuf[:, f % 2],
                        in_=W_inT[:, 128 * f:128 * (f + 1)].rearrange("(e p) j -> p e j", p=128)))
                @block.tensor
                def _(te, f=f, uhalf=uhalf):
                    C.wait(te, "pe", M[f"w_{f}"])
                    if f == 0:
                        C.wait(te, "pe", CST)
                        C.wait(te, "pe", GCST)
                    if f >= 1:
                        C.wait(te, "pe", ev_mark[f - 1])
                    last = None
                    for e in range(32):
                        lhsT = wbuf[:, f % 2, e]
                        te.ldweights(lhsT)
                        te.matmul(pproj[:, 0:512], lhsT, x0bf[:, e, 0:512], start=(e == 0), stop=(e == 31))
                        last = te.matmul(pproj[:, 512:1024], lhsT, x0bf[:, e, 512:1024], start=(e == 0), stop=(e == 31))
                        if uhalf:
                            last = te.matmul(pproj[:, 1024:1036], lhsT, x0pre_bf[:, e, 0:12], start=(e == 0), stop=(e == 31))
                    M[f"pe_{f}"] = C.bump("pe", last)
                @block.scalar
                def _(sc, f=f, uhalf=uhalf):
                    C.wait(sc, "act", M[f"pe_{f}"])
                    if uhalf:
                        if f >= 1:
                            C.wait(sc, "act", M[f"conv_{f-1}"])
                        last = None
                        for g in range(NGRP):
                            sc.copy(u_pad[:, g, 3:3 + G], pproj[:, G * g:G * (g + 1)])
                            last = sc.copy(u_pad[:, g, 0:3], pproj[:, 1024 + 3 * g:1024 + 3 * g + 3])
                        ev_mark[f] = M[f"ev_{f}"] = C.bump("act", last)
                    else:
                        fb_ = f - 64
                        if f >= 66:
                            C.wait(sc, "act", M[f"gst_{f-2}"])
                        half = fb_ % 2
                        sc.copy(g_sb[:, half, 0:512], pproj[:, 0:512])
                        ev_mark[f] = M[f"gev_{f}"] = C.bump("act", sc.copy(g_sb[:, half, 512:1024], pproj[:, 512:1024]))
                if uhalf:
                    @block.vector
                    def _(dv, f=f):
                        C.wait(dv, "dve", M[f"ev_{f}"])
                        C.wait(dv, "dve", CST)
                        if f >= 1:
                            C.wait(dv, "dve", M[f"vs_{f-1}"])
                        dv.tensor_scalar(vt[:], u_pad[:, :, 0:G], cwb[:, f, 0:1], None, OP.mult)
                        last = None
                        for k in range(1, KCV):
                            last = dv.scalar_tensor_tensor(vt[:], u_pad[:, :, k:k + G], cwb[:, f, k:k + 1], vt[:], OP.mult, OP.add)
                        M[f"conv_{f}"] = C.bump("dve", last)
                    @block.scalar
                    def _(sc, f=f):
                        C.wait(sc, "act", M[f"conv_{f}"])
                        if f >= 2:
                            C.wait(sc, "act", M[f"vst_{f-2}"])
                        M[f"vs_{f}"] = C.bump("act", sc.activation(
                            v_sb[:, f % 2].rearrange("p (g t) -> p g t", g=NGRP), vt[:],
                            AF.Silu, bias=cbb[:, f:f + 1], scale=1.0))
                @block.sync
                def _(sp, f=f, uhalf=uhalf):
                    if uhalf:
                        C.wait(sp, "sp", M[f"vs_{f}"])
                        M[f"vst_{f}"] = C.dma(f"vst{f % 2}", sp.dma_start(out=v_scr[128 * f:128 * (f + 1), :], in_=v_sb[:, f % 2]))
                    else:
                        C.wait(sp, "sp", M[f"gev_{f}"])
                        M[f"gst_{f}"] = C.dma(f"gst{(f - 64) % 2}", sp.dma_start(out=gate_scr[128 * (f - 64):128 * (f - 63), :], in_=g_sb[:, (f - 64) % 2]))

            for t in range(2):
                for fh in range(2):
                    @block.sync
                    def _(sp, t=t, fh=fh):
                        C.wait(sp, "sp", C.dma_total("vst0"))
                        C.wait(sp, "sp", C.dma_total("vst1"))
                        if t > 0 or fh > 0:
                            C.wait(sp, "sp", M[f"wx_mm_{t}_{fh-1}"] if fh > 0 else M[f"wx_mm_{t-1}_1"])
                        M[f"vwx_{t}_{fh}"] = C.dma("vwx", sp.dma_start(
                            out=vwx[:], in_=v_scr[4096 * fh:4096 * (fh + 1), 512 * t:512 * (t + 1)].rearrange("(f p) x -> p f x", p=128)))
                    @block.tensor
                    def _(te, t=t, fh=fh):
                        C.wait(te, "pe", M[f"vwx_{t}_{fh}"])
                        if t == 1 and fh == 0:
                            C.wait(te, "pe", M["nrm_done_0"])
                        last = None
                        for fi in range(32):
                            f = 32 * fh + fi
                            st = (f == 0)
                            sto = (f == 63)
                            te.ldweights(wxbuf[:, f, 0:128])
                            te.matmul(pwx1[:], wxbuf[:, f, 0:128], vwx[:, fi], start=st, stop=sto)
                            te.ldweights(wxbuf[:, f, 128:256])
                            te.matmul(pwx2[:], wxbuf[:, f, 128:256], vwx[:, fi], start=st, stop=sto)
                            te.ldweights(wxbuf[:, f, 256:272])
                            te.matmul(pwxbc[0:16, :], wxbuf[:, f, 256:272], vwx[:, fi], start=st, stop=sto)
                            te.ldweights(wxbuf[:, f, 272:288])
                            last = te.matmul(pwxbc[32:48, :], wxbuf[:, f, 272:288], vwx[:, fi], start=st, stop=sto)
                        M[f"wx_mm_{t}_{fh}"] = C.bump("pe", last)
                M[f"wx_mm_{t}"] = M[f"wx_mm_{t}_1"]

                @block.scalar
                def _(sc, t=t):
                    C.wait(sc, "act", M[f"wx_mm_{t}"])
                    if t == 1:
                        C.wait(sc, "act", M["dtnst_dt_0"])
                    sc.copy(ssm_raw[:, 0], pwx1[:])
                    sc.copy(ssm_raw[:, 1], pwx2[:])
                    sc.copy(bc_raw[:, 0], pwxbc[0:16])
                    sc.copy(bc_raw[:, 1], pwxbc[32:48])
                    sc.activation(sq1[:, 0], pwx1[:], AF.Square)
                    sc.activation(sq1[:, 1], pwx2[:], AF.Square)
                    sc.activation(sqbc[:, 0], pwxbc[0:16], AF.Square)
                    M[f"sqd_{t}"] = C.bump("act", sc.activation(sqbc[:, 1], pwxbc[32:48], AF.Square))
                @block.tensor
                def _(te, t=t):
                    C.wait(te, "pe", M[f"sqd_{t}"])
                    te.matmul(pstat[:], onc[:, 0:1], sq1[:, 0], start=True, stop=False)
                    M[f"st_dt_{t}"] = C.bump("pe", te.matmul(pstat[:], onc[:, 0:1], sq1[:, 1], start=False, stop=True))
                @block.vector
                def _(dv, t=t):
                    C.wait(dv, "dve", M[f"st_dt_{t}"])
                    M[f"ms_dt_{t}"] = C.self_edge("dve", dv, dv.tensor_scalar(rrow[:, 0], pstat[:], 1.0 / 256, 1e-6, OP.mult, OP.add))
                @block.scalar
                def _(sc, t=t):
                    C.wait(sc, "act", M[f"ms_dt_{t}"])
                    M[f"sqrt_dt_{t}"] = C.bump("act", sc.activation(rrow[:, 1], rrow[:, 0], AF.Sqrt))
                @block.vector
                def _(dv, t=t):
                    C.wait(dv, "dve", M[f"sqrt_dt_{t}"])
                    M[f"rstd_dt_{t}"] = C.self_edge("dve", dv, dv.reciprocal(rrow[:, 2], rrow[:, 1]))
                @block.tensor
                def _(te, t=t):
                    C.wait(te, "pe", M[f"rstd_dt_{t}"])
                    M[f"bc_dt_{t}"] = C.bump("pe", te.matmul(pbcast[:], onr[:], rrow[:, 2], start=True, stop=True))
                @block.vector
                def _(dv, t=t):
                    C.wait(dv, "dve", M[f"bc_dt_{t}"])
                    if t == 1:
                        C.wait(dv, "dve", M["dtnst_dt_0"])
                    dv.scalar_tensor_tensor(dtn_sb[:, 0], ssm_raw[:, 0], lnw[:, 0:1], pbcast[:], OP.mult, OP.mult)
                    M[f"dtn_dt_{t}"] = C.bump("dve", dv.scalar_tensor_tensor(
                        dtn_sb[:, 1], ssm_raw[:, 1], lnw[:, 1:2], pbcast[:], OP.mult, OP.mult))
                @block.sync
                def _(sp, t=t):
                    C.wait(sp, "sp", M[f"dtn_dt_{t}"])
                    M[f"dtnst_dt_{t}"] = C.dma("dtst", sp.dma_start(
                        out=dtn_scr[0:256, 512 * t:512 * (t + 1)].rearrange("(a p) x -> p a x", p=128), in_=dtn_sb[:, 0:2]))
                for bi, lncol in enumerate((2, 3)):
                    @block.tensor
                    def _(te, t=t, bi=bi):
                        C.wait(te, "pe", M[f"sqd_{t}"])
                        C.wait(te, "pe", M[f"ms_dt_{t}"] if bi == 0 else M[f"ms_b0_{t}"])
                        M[f"st_b{bi}_{t}"] = C.bump("pe", te.matmul(pstat[:], onc[0:16, 0:1], sqbc[:, bi], start=True, stop=True))
                    @block.vector
                    def _(dv, t=t, bi=bi):
                        C.wait(dv, "dve", M[f"st_b{bi}_{t}"])
                        M[f"ms_b{bi}_{t}"] = C.self_edge("dve", dv, dv.tensor_scalar(rrow[:, 3 + 3 * bi], pstat[:], 1.0 / 16, 1e-6, OP.mult, OP.add))
                    @block.scalar
                    def _(sc, t=t, bi=bi):
                        C.wait(sc, "act", M[f"ms_b{bi}_{t}"])
                        M[f"sqrt_b{bi}_{t}"] = C.bump("act", sc.activation(rrow[:, 4 + 3 * bi], rrow[:, 3 + 3 * bi], AF.Sqrt))
                    @block.vector
                    def _(dv, t=t, bi=bi):
                        C.wait(dv, "dve", M[f"sqrt_b{bi}_{t}"])
                        M[f"rstd_b{bi}_{t}"] = C.self_edge("dve", dv, dv.reciprocal(rrow[:, 5 + 3 * bi], rrow[:, 4 + 3 * bi]))
                    @block.tensor
                    def _(te, t=t, bi=bi):
                        C.wait(te, "pe", M[f"rstd_b{bi}_{t}"])
                        C.wait(te, "pe", M[f"dtn_dt_{t}"] if bi == 0 else M[f"dtn_b0_{t}"])
                        M[f"bc_b{bi}_{t}"] = C.bump("pe", te.matmul(pbcast[:], onr[:], rrow[:, 5 + 3 * bi], start=True, stop=True))
                    @block.vector
                    def _(dv, t=t, bi=bi, lncol=lncol):
                        C.wait(dv, "dve", M[f"bc_b{bi}_{t}"])
                        if bi == 1:
                            C.wait(dv, "dve", M[f"dtnst_b0_{t}"])
                        elif t == 1:
                            C.wait(dv, "dve", M["dtnst_b1_0"])
                        M[f"dtn_b{bi}_{t}"] = C.bump("dve", dv.scalar_tensor_tensor(
                            dtn_sb[0:16, 2], bc_raw[:, bi], lnw[0:16, lncol:lncol + 1], pbcast[0:16, :], OP.mult, OP.mult))
                    @block.sync
                    def _(sp, t=t, bi=bi):
                        C.wait(sp, "sp", M[f"dtn_b{bi}_{t}"])
                        M[f"dtnst_b{bi}_{t}"] = C.dma(f"bcst{bi}", sp.dma_start(
                            out=dtn_scr[256 + 16 * bi:256 + 16 * (bi + 1), 512 * t:512 * (t + 1)], in_=dtn_sb[0:16, 2]))
                M[f"nrm_done_{t}"] = M[f"dtn_b1_{t}"]

            @block.gpsimd
            def _(gp):
                for s in ("vst0", "vst1", "dtst", "bcst0", "bcst1"):
                    C.wait(gp, "pool", C.dma_total(s))
                M["cc_v"] = C.bump("cc", gp.collective_compute(
                    "AllToAll", OP.bypass, replica_groups=RG,
                    ins=[v_scr.tensor.ap().opt()], outs=[v_a2a.tensor.ap().opt()]))
                M["cc_dtn"] = C.bump("cc", gp.collective_compute(
                    "AllGather", OP.bypass, replica_groups=RG,
                    ins=[dtn_scr.tensor.ap().opt()], outs=[dtn_ag.tensor.ap().opt()]))
            C.join(block)

        with contextlib.ExitStack() as ps:
            SP = 2048
            vh = ps.enter_context(nc.sbuf_tensor("vh", [128, 4, SP], BF16))
            dtn_sb2 = ps.enter_context(nc.sbuf_tensor("dtn_sb2", [128, 2, SP], BF16))
            bt = ps.enter_context(nc.sbuf_tensor("bt", [NST, SP], BF16))
            ct = ps.enter_context(nc.sbuf_tensor("ct", [NST, SP], BF16))
            wdt = ps.enter_context(nc.sbuf_tensor("wdt", [128, 2, 8, 128], BF16))
            ab = ps.enter_context(nc.sbuf_tensor("ab", [128, 8 * NST], F32))
            dtb = ps.enter_context(nc.sbuf_tensor("dtb", [128, 8], F32))
            dpp = ps.enter_context(nc.sbuf_tensor("dpp", [128, 8], F32))
            esl = ps.enter_context(nc.sbuf_tensor("esl", [NST, NST * 128], BF16))
            delta = ps.enter_context(nc.sbuf_tensor("delta", [128, 4, SP], BF16))
            du = ps.enter_context(nc.sbuf_tensor("du", [128, 4, SP], BF16))
            sp_t = ps.enter_context(nc.sbuf_tensor("sp_t", [128, SP], F32))
            sp_t2 = ps.enter_context(nc.sbuf_tensor("sp_t2", [128, SP], F32))
            bbc = ps.enter_context(nc.sbuf_tensor("bbc", [128, SP], BF16))
            cbc = ps.enter_context(nc.sbuf_tensor("cbc", [128, SP], BF16))
            a_sb = ps.enter_context(nc.sbuf_tensor("a_sb", [128, 4, SP], F32))
            b_sb = ps.enter_context(nc.sbuf_tensor("b_sb", [128, 2, SP], BF16))
            s_sb = ps.enter_context(nc.sbuf_tensor("s_sb", [128, 2, SP], BF16))
            z_sb = ps.enter_context(nc.sbuf_tensor("z_sb", [128, 2, SP], BF16))
            y_acc = ps.enter_context(nc.sbuf_tensor("y_acc", [128, 4, SP], F32))
            y2_sb = ps.enter_context(nc.sbuf_tensor("y2_sb", [128, 2, SP], BF16))
            carrys = [ps.enter_context(nc.sbuf_tensor(f"cr{i}", [128, 1], F32)) for i in range(8 * NST)]
            pdelta = ps.enter_context(nc.psum_tensor("pdelta", [128, 2, 512], F32))
            pbc2 = ps.enter_context(nc.psum_tensor("pbc2", [128, SP], F32))

            @block.gpsimd
            def _(gp):
                C.dma("wdtl", gp.dma_start(out=wdt[:], in_=wdtT_c.rearrange("(rc p) (dt j) -> p rc dt j", p=128, j=128)))
            @block.sync
            def _(sp):
                C.dma("cst2", sp.dma_start(out=ab[:], in_=A_c[:]))
                C.dma("cst2", sp.dma_start(out=dtb[:], in_=dtb_c[:]))
                C.dma("cst2", sp.dma_start(out=dpp[:], in_=Dp_c[:]))
                C.dma("cst2", sp.dma_start(out=esl[:], in_=esel[:]))
            CST = C.dma_total("cst2")
            WDTL = C.dma_total("wdtl")

            pd_free = {}
            a_free = {}
            a_last = {}
            z_free = {}
            last_bbc_read = None
            last_pbc_ev = None
            prev_delta_add = None
            prev_y2 = {}
            y2st = {}

            for s in range(4):
                tsl = slice(SP * s, SP * (s + 1))
                @block.sync
                def _(sp, s=s, tsl=tsl):
                    if s >= 1:
                        C.wait(sp, "sp", M[f"dmm_last_{s-1}"])
                        C.wait(sp, "sp", M[f"bc_mm_last_{s-1}"])
                    for r in range(8):
                        k = 8 * s + r
                        ow, gg = owner(k), gidx(k)
                        C.dma("dtnl", sp.dma_start(
                            out=dtn_sb2[:, :, 256 * r:256 * (r + 1)],
                            in_=dtn_ag[288 * ow:288 * ow + 256, 256 * gg:256 * (gg + 1)].rearrange("(a p) t -> p a t", p=128)))
                        C.dma("dtnl", sp.dma_start(out=bt[:, 256 * r:256 * (r + 1)],
                                                   in_=dtn_ag[288 * ow + 256:288 * ow + 272, 256 * gg:256 * (gg + 1)]))
                        C.dma("dtnl", sp.dma_start(out=ct[:, 256 * r:256 * (r + 1)],
                                                   in_=dtn_ag[288 * ow + 272:288 * ow + 288, 256 * gg:256 * (gg + 1)]))
                    M[f"dtnl_{s}"] = C.dma_total("dtnl")

                for h in range(2):
                    @block.sync
                    def _(sp, s=s, h=h, tsl=tsl):
                        prev = y2st.get((s, h - 1, 3) if h else (s - 1, 1, 3))
                        if prev is not None:
                            C.wait(sp, "sp", prev)
                        for r in range(8):
                            k = 8 * s + r
                            ow, gg = owner(k), gidx(k)
                            C.dma("vhl", sp.dma_start(
                                out=vh[:, :, 256 * r:256 * (r + 1)],
                                in_=v_a2a[1024 * ow + 512 * h:1024 * ow + 512 * (h + 1), 256 * gg:256 * (gg + 1)].rearrange("(dt p) t -> p dt t", p=128)))
                        M[f"vl_{s}_{h}"] = C.dma_total("vhl")

                    for di in range(4):
                        dt_g = 4 * h + di
                        for q in range(4):
                            par = q % 2
                            @block.tensor
                            def _(te, s=s, di=di, dt_g=dt_g, q=q, par=par):
                                C.wait(te, "pe", M[f"dtnl_{s}"])
                                C.wait(te, "pe", CST)
                                C.wait(te, "pe", WDTL)
                                if par in pd_free:
                                    C.wait(te, "pe", pd_free[par])
                                te.matmul(pdelta[:, par], wdt[:, 0, dt_g], dtn_sb2[:, 0, 512 * q:512 * (q + 1)], start=True, stop=False)
                                M["dmm_cur"] = C.bump("pe", te.matmul(
                                    pdelta[:, par], wdt[:, 1, dt_g], dtn_sb2[:, 1, 512 * q:512 * (q + 1)], start=False, stop=True))
                            @block.scalar
                            def _(sc, s=s, h=h, di=di, dt_g=dt_g, q=q, par=par):
                                C.wait(sc, "act", CST)
                                C.wait(sc, "act", M["dmm_cur"])
                                if q == 0 and prev_delta_add is not None:
                                    C.wait(sc, "act", prev_delta_add)
                                qq = slice(512 * q, 512 * (q + 1))
                                sc.activation(sp_t[:, qq], pdelta[:, par], AF.Abs, bias=dtb[:, dt_g:dt_g + 1])
                                sc.activation(sp_t[:, qq], sp_t[:, qq], AF.Exp, scale=-1.0)
                                sc.activation(sp_t[:, qq], sp_t[:, qq], AF.Ln, bias=onc[:, 0:1])
                                pd_free[par] = M["spq_cur"] = C.bump("act", sc.activation(
                                    sp_t2[:, qq], pdelta[:, par], AF.Relu, bias=dtb[:, dt_g:dt_g + 1]))
                        M[f"spl_{s}_{h}_{di}"] = M["spq_cur"]
                        if di == 3 and h == 1:
                            M[f"dmm_last_{s}"] = M["dmm_cur"]
                        @block.vector
                        def _(dv, s=s, h=h, di=di):
                            C.wait(dv, "dve", M[f"spl_{s}_{h}_{di}"])
                            if di in a_last:
                                C.wait(dv, "dve", a_last[di])
                            M[f"delta_{s}_{h}_{di}"] = C.bump("dve", dv.tensor_tensor(delta[:, di], sp_t2[:], sp_t[:], OP.add))
                            C.wait(dv, "dve", M[f"vl_{s}_{h}"])
                            M[f"du_{s}_{h}_{di}"] = C.bump("dve", dv.tensor_tensor(du[:, di], delta[:, di], vh[:, di], OP.mult))
                        prev_delta_add = M[f"delta_{s}_{h}_{di}"]

                    for n in range(NST):
                        act_base_n = C.ecnt["act"]
                        @block.tensor
                        def _(te, s=s, n=n, act_base_n=act_base_n):
                            C.wait(te, "pe", M[f"dtnl_{s}"])
                            if last_pbc_ev is not None:
                                C.wait(te, "pe", last_pbc_ev)
                            last = None
                            for q in range(4):
                                last = te.matmul(pbc2[:, 512 * q:512 * (q + 1)], esl[:, 128 * n:128 * (n + 1)],
                                                 bt[:, 512 * q:512 * (q + 1)], start=True, stop=True)
                            M["b_mm"] = C.bump("pe", last)
                            C.wait(te, "pe", ("E", "act", act_base_n + 1))
                            for q in range(4):
                                last = te.matmul(pbc2[:, 512 * q:512 * (q + 1)], esl[:, 128 * n:128 * (n + 1)],
                                                 ct[:, 512 * q:512 * (q + 1)], start=True, stop=True)
                            M["bc_mm_cur"] = C.bump("pe", last)
                        M[f"bc_mm_last_{s}"] = M["bc_mm_cur"]

                        @block.scalar
                        def _(sc, s=s, h=h, n=n, lbr=last_bbc_read):
                            C.wait(sc, "act", M["b_mm"])
                            if lbr is not None:
                                C.wait(sc, "act", lbr)
                            M["bbc_ev"] = C.bump("act", sc.copy(bbc[:], pbc2[:]))
                            C.wait(sc, "act", M["bc_mm_cur"])
                            M["cbc_ev"] = C.bump("act", sc.copy(cbc[:], pbc2[:]))
                            for di in range(4):
                                dt_g = 4 * h + di
                                apar = di
                                C.wait(sc, "act", M[f"delta_{s}_{h}_{di}"])
                                if apar in a_free:
                                    C.wait(sc, "act", a_free[apar])
                                M[f"a_{apar}"] = C.bump("act", sc.activation(
                                    a_sb[:, apar], delta[:, di], AF.Exp, scale=ab[:, NST * dt_g + n:NST * dt_g + n + 1]))
                                if n == NST - 1:
                                    a_last[di] = M[f"a_{apar}"]
                        last_pbc_ev = M["cbc_ev"]

                        for di in range(4):
                            dt_g = 4 * h + di
                            par = di % 2
                            apar = di
                            @block.vector
                            def _(dv, s=s, h=h, n=n, di=di, dt_g=dt_g, par=par, apar=apar):
                                C.wait(dv, "dve", M[f"du_{s}_{h}_{di}"])
                                C.wait(dv, "dve", M["bbc_ev"])
                                dv.tensor_tensor(b_sb[:, par], du[:, di], bbc[:], OP.mult)
                                C.wait(dv, "dve", M[f"a_{apar}"])
                                cr = carrys[dt_g * NST + n]
                                if s == 0:
                                    sc_i = dv.tensor_tensor_scan(s_sb[:, par], a_sb[:, apar], b_sb[:, par], 0.0, OP.mult, OP.add)
                                else:
                                    sc_i = dv.tensor_tensor_scan(s_sb[:, par], a_sb[:, apar], b_sb[:, par], cr[:, 0:1], OP.mult, OP.add)
                                C.self_edge("dve", dv, sc_i)
                                a_free[apar] = C.bump("dve", dv.tensor_copy(cr[:], s_sb[:, par, SP - 1:SP]))
                                C.wait(dv, "dve", M["cbc_ev"])
                                if par in z_free:
                                    C.wait(dv, "dve", z_free[par])
                                M[f"z_{par}"] = C.bump("dve", dv.tensor_tensor(z_sb[:, par], s_sb[:, par], cbc[:], OP.mult))
                            @block.gpsimd
                            def _(gp, s=s, h=h, n=n, di=di, par=par):
                                C.wait(gp, "pool", M[f"z_{par}"])
                                if n == 0:
                                    prev = prev_y2.get(di)
                                    if prev is not None:
                                        C.wait(gp, "pool", prev)
                                    mm = gp.tensor_copy(y_acc[:, di], z_sb[:, par])
                                else:
                                    mm = gp.tensor_tensor(y_acc[:, di], y_acc[:, di], z_sb[:, par], OP.add)
                                z_free[par] = C.bump("pool", mm)
                                if n == NST - 1:
                                    M[f"y_done_{di}"] = z_free[par]
                        last_bbc_read = M["z_1"]

                    for di in range(4):
                        dt_g = 4 * h + di
                        @block.vector
                        def _(dv, s=s, h=h, di=di, dt_g=dt_g):
                            C.wait(dv, "dve", M[f"y_done_{di}"])
                            prevst = y2st.get((s, h, di - 2) if di >= 2 else ((s, h - 1, di + 2) if h else (s - 1, 1, di + 2)))
                            if prevst is not None:
                                C.wait(dv, "dve", prevst)
                            M["y2_cur"] = C.bump("dve", dv.scalar_tensor_tensor(
                                y2_sb[:, di % 2], vh[:, di], dpp[:, dt_g:dt_g + 1], y_acc[:, di], OP.mult, OP.add))
                            prev_y2[di] = M["y2_cur"]
                        @block.sync
                        def _(sp, s=s, h=h, di=di, tsl=tsl):
                            C.wait(sp, "sp", M["y2_cur"])
                            for r in range(8):
                                k = 8 * s + r
                                ow, gg = owner(k), gidx(k)
                                C.dma(f"y2st{di % 2}", sp.dma_start(
                                    out=y_in[1024 * ow + 512 * h + 128 * di:1024 * ow + 512 * h + 128 * (di + 1),
                                             256 * gg:256 * (gg + 1)],
                                    in_=y2_sb[:, di % 2, 256 * r:256 * (r + 1)]))
                            y2st[(s, h, di)] = C.dma_total(f"y2st{di % 2}")

            @block.gpsimd
            def _(gp):
                C.wait(gp, "pool", C.dma_total("y2st0"))
                C.wait(gp, "pool", C.dma_total("y2st1"))
                M["cc_y"] = C.bump("cc", gp.collective_compute(
                    "AllToAll", OP.bypass, replica_groups=RG,
                    ins=[y_in.tensor.ap().opt()], outs=[y_out.tensor.ap().opt()]))
            C.join(block)

        with contextlib.ExitStack() as ps:
            y3 = ps.enter_context(nc.sbuf_tensor("y3", [128, 64, TLOC], BF16))
            t_y2 = ps.enter_context(nc.sbuf_tensor("t_y2", [128, 2, TLOC], BF16))
            t_g = ps.enter_context(nc.sbuf_tensor("t_g", [128, 2, TLOC], BF16))
            t_sg = ps.enter_context(nc.sbuf_tensor("t_sg", [128, TLOC], BF16))
            wbuf3 = ps.enter_context(nc.sbuf_tensor("wbuf3", [128, 2, 32, 128], BF16))
            x0t = ps.enter_context(nc.sbuf_tensor("x0t", [128, 2, TLOC], F32))
            r_sb = ps.enter_context(nc.sbuf_tensor("r_sb3", [128, 2, TLOC], F32))
            sq3 = ps.enter_context(nc.sbuf_tensor("sq3", [128, TLOC], F32))
            rrow3 = ps.enter_context(nc.sbuf_tensor("rrow3", [1, 6, TLOC], F32))
            xn = ps.enter_context(nc.sbuf_tensor("xn3", [128, 2, TLOC], F32))
            pW = ps.enter_context(nc.psum_tensor("pW3", [128, TLOC], F32))
            psum_s = ps.enter_context(nc.psum_tensor("pS3", [1, TLOC], F32))
            psum_q = ps.enter_context(nc.psum_tensor("pQ3", [1, TLOC], F32))
            pbc3 = ps.enter_context(nc.psum_tensor("pbc3", [128, TLOC], F32))

            for dc in range(64):
                par = dc % 2
                @block.sync
                def _(sp, dc=dc, par=par):
                    if dc >= 2:
                        C.wait(sp, "sp", M[f"y3_{dc-2}"])
                    C.dma(f"y2l{par}", sp.dma_start(out=t_y2[:, par], in_=y_out[128 * dc:128 * (dc + 1), :]))
                    M[f"ld3_{dc}"] = C.dma(f"gl{par}", sp.dma_start(out=t_g[:, par], in_=gate_scr[128 * dc:128 * (dc + 1), :]))
                @block.scalar
                def _(sc, dc=dc, par=par):
                    C.wait(sc, "act", C.dma_total(f"y2l{par}"))
                    C.wait(sc, "act", C.dma_total(f"gl{par}"))
                    if dc >= 1:
                        C.wait(sc, "act", M[f"y3_{dc-1}"])
                    M[f"sg_{dc}"] = C.bump("act", sc.activation(t_sg[:], t_g[:, par], AF.Silu))
                @block.vector
                def _(dv, dc=dc, par=par):
                    C.wait(dv, "dve", M[f"sg_{dc}"])
                    M[f"y3_{dc}"] = C.bump("dve", dv.tensor_tensor(y3[:, dc], t_y2[:, par], t_sg[:], OP.mult))

            for hc in range(32):
                par = hc % 2
                @block.gpsimd
                def _(gp, hc=hc):
                    for half in range(2):
                        g = 2 * hc + half
                        if g >= 2:
                            C.wait(gp, "pool", M[f"mmh3_{g-2}"])
                        M[f"wl3_{g}"] = C.dma(f"wl3{g % 2}", gp.dma_start(
                            out=wbuf3[:, g % 2],
                            in_=W_outT[4096 * half:4096 * (half + 1), 128 * hc:128 * (hc + 1)].rearrange(
                                "(dc p) j -> p dc j", p=128)))
                @block.sync
                def _(sp, hc=hc, par=par):
                    if hc >= 2:
                        C.wait(sp, "sp", M[f"resid3_{hc-2}"])
                    M[f"x0l_{hc}"] = C.dma(f"x0l{par}", sp.dma_start(out=x0t[:, par], in_=x0T[128 * hc:128 * (hc + 1), :]))
                @block.tensor
                def _(te, hc=hc):
                    if hc == 0:
                        for dc in range(64):
                            C.wait(te, "pe", M[f"y3_{dc}"])
                    if hc >= 1:
                        C.wait(te, "pe", M[f"ev3_{hc-1}"])
                    last = None
                    for half in range(2):
                        g = 2 * hc + half
                        C.wait(te, "pe", M[f"wl3_{g}"])
                        for dj in range(32):
                            dc = 32 * half + dj
                            te.ldweights(wbuf3[:, g % 2, dj])
                            te.matmul(pW[:, 0:512], wbuf3[:, g % 2, dj], y3[:, dc, 0:512], start=(dc == 0), stop=(dc == 63))
                            last = te.matmul(pW[:, 512:1024], wbuf3[:, g % 2, dj], y3[:, dc, 512:1024], start=(dc == 0), stop=(dc == 63))
                        M[f"mmh3_{g}"] = C.bump("pe", last)
                    M[f"mm3_{hc}"] = M[f"mmh3_{2 * hc + 1}"]
                @block.vector
                def _(dv, hc=hc, par=par):
                    C.wait(dv, "dve", M[f"mm3_{hc}"])
                    C.wait(dv, "dve", M[f"x0l_{hc}"])
                    if hc >= 2:
                        C.wait(dv, "dve", M[f"rst3_{hc-2}"])
                    M[f"resid3_{hc}"] = C.bump("dve", dv.tensor_tensor(r_sb[:, par], pW[:], x0t[:, par], OP.add))
                @block.scalar
                def _(sc, hc=hc, par=par):
                    C.wait(sc, "act", M[f"resid3_{hc}"])
                    if hc >= 1:
                        C.wait(sc, "act", M[f"stat3_{hc-1}"])
                    M[f"ev3_{hc}"] = M[f"sq3_{hc}"] = C.bump("act", sc.activation(sq3[:], r_sb[:, par], AF.Square))
                @block.tensor
                def _(te, hc=hc, par=par):
                    C.wait(te, "pe", M[f"sq3_{hc}"])
                    te.matmul(psum_s[:, 0:512], onc[:, 0:1], r_sb[:, par, 0:512], start=(hc == 0), stop=(hc == 31))
                    te.matmul(psum_s[:, 512:1024], onc[:, 0:1], r_sb[:, par, 512:1024], start=(hc == 0), stop=(hc == 31))
                    te.matmul(psum_q[:, 0:512], onc[:, 0:1], sq3[:, 0:512], start=(hc == 0), stop=(hc == 31))
                    M[f"stat3_{hc}"] = C.bump("pe", te.matmul(psum_q[:, 512:1024], onc[:, 0:1], sq3[:, 512:1024], start=(hc == 0), stop=(hc == 31)))
                @block.sync
                def _(sp, hc=hc, par=par):
                    C.wait(sp, "sp", M[f"stat3_{hc}"])
                    M[f"rst3_{hc}"] = C.dma(f"rst{par}", sp.dma_start(out=r_scr[128 * hc:128 * (hc + 1), :], in_=r_sb[:, par]))

            @block.vector
            def _(dv):
                C.wait(dv, "dve", M["stat3_31"])
                C.self_edge("dve", dv, dv.tensor_scalar(rrow3[:, 0], psum_s[:], 1.0 / H, None, OP.mult))
                C.self_edge("dve", dv, dv.tensor_scalar(rrow3[:, 1], psum_q[:], 1.0 / H, None, OP.mult))
                C.self_edge("dve", dv, dv.tensor_tensor(rrow3[:, 2], rrow3[:, 0], rrow3[:, 0], OP.mult))
                C.self_edge("dve", dv, dv.tensor_tensor(rrow3[:, 3], rrow3[:, 1], rrow3[:, 2], OP.subtract))
                M["var3"] = C.bump("dve", dv.tensor_scalar(rrow3[:, 3], rrow3[:, 3], 1.0, 1e-5, OP.mult, OP.add))
            @block.scalar
            def _(sc):
                C.wait(sc, "act", M["var3"])
                M["sqrt3"] = C.bump("act", sc.activation(rrow3[:, 4], rrow3[:, 3], AF.Sqrt))
            @block.vector
            def _(dv):
                C.wait(dv, "dve", M["sqrt3"])
                M["rstd3"] = C.self_edge("dve", dv, dv.reciprocal(rrow3[:, 5], rrow3[:, 4]))
            @block.tensor
            def _(te):
                C.wait(te, "pe", M["rstd3"])
                te.matmul(pbc3[:, 0:512], onr[:], rrow3[:, 5, 0:512], start=True, stop=True)
                te.matmul(pbc3[:, 512:1024], onr[:], rrow3[:, 5, 512:1024], start=True, stop=True)
                te.matmul(pW[:, 0:512], onr[:], rrow3[:, 0, 0:512], start=True, stop=True)
                M["bc3"] = C.bump("pe", te.matmul(pW[:, 512:1024], onr[:], rrow3[:, 0, 512:1024], start=True, stop=True))
            for hc in range(32):
                par = hc % 2
                @block.sync
                def _(sp, hc=hc, par=par):
                    C.wait(sp, "sp", C.dma_total("rst0"))
                    C.wait(sp, "sp", C.dma_total("rst1"))
                    if hc >= 2:
                        C.wait(sp, "sp", M[f"norm3_{hc-2}"])
                    M[f"rrd3_{hc}"] = C.dma(f"rrd{par}", sp.dma_start(out=r_sb[:, par], in_=r_scr[128 * hc:128 * (hc + 1), :]))
                @block.vector
                def _(dv, hc=hc, par=par):
                    C.wait(dv, "dve", M[f"rrd3_{hc}"])
                    C.wait(dv, "dve", M["bc3"])
                    if hc >= 2:
                        C.wait(dv, "dve", M[f"xst3_{hc-2}"])
                    dv.tensor_tensor(xn[:, par], r_sb[:, par], pW[:], OP.subtract)
                    M[f"norm3_{hc}"] = C.bump("dve", dv.tensor_tensor(xn[:, par], xn[:, par], pbc3[:], OP.mult))
                @block.sync
                def _(sp, hc=hc, par=par):
                    C.wait(sp, "sp", M[f"norm3_{hc}"])
                    M[f"xst3_{hc}"] = C.dma(f"xst{par}", sp.dma_start(out=x1_scr[128 * hc:128 * (hc + 1), :], in_=xn[:, par]))
            C.join(block)

        with contextlib.ExitStack() as ps:
            x1bf = ps.enter_context(nc.sbuf_tensor("x1bf", [128, 32, TLOC], BF16))
            bwbuf = ps.enter_context(nc.sbuf_tensor("bwbuf", [128, 2, 32, 128], BF16))
            a2bf = ps.enter_context(nc.sbuf_tensor("a2bf", [128, 32, 128], BF16))
            b2bf = ps.enter_context(nc.sbuf_tensor("b2bf", [128, 32, 128], BF16))
            repb = ps.enter_context(nc.sbuf_tensor("repb", [8, 128], BF16))
            rwb = ps.enter_context(nc.sbuf_tensor("rwb", [128, 32, 8], BF16))
            lgT = ps.enter_context(nc.sbuf_tensor("lgT", [8, TLOC], F32))
            lg = ps.enter_context(nc.sbuf_tensor("lg", [128, 8, 8], F32))
            esb = ps.enter_context(nc.sbuf_tensor("esb", [128, 8, 8], F32))
            wd = ps.enter_context(nc.sbuf_tensor("wd", [128, 8, 8], F32))
            mrow = ps.enter_context(nc.sbuf_tensor("mrow", [128, 8, 4], F32))
            top8 = ps.enter_context(nc.sbuf_tensor("top8", [128, 8, 8], F32))
            wdT = ps.enter_context(nc.sbuf_tensor("wdT", [8, TLOC], BF16))
            w16 = ps.enter_context(nc.sbuf_tensor("w16", [128, TLOC], BF16))
            midg = ps.enter_context(nc.sbuf_tensor("midg", [128, TLOC], BF16))
            fb = ps.enter_context(nc.sbuf_tensor("fb", [128, TLOC], F32))
            x1f = ps.enter_context(nc.sbuf_tensor("x1f", [128, 2, TLOC], F32))
            r_sb = ps.enter_context(nc.sbuf_tensor("r_sb0", [128, 2, TLOC], F32))
            sq0 = ps.enter_context(nc.sbuf_tensor("sq0", [128, TLOC], F32))
            rrow0 = ps.enter_context(nc.sbuf_tensor("rrow0", [1, 6, TLOC], F32))
            xn0 = ps.enter_context(nc.sbuf_tensor("xn0", [128, 2, TLOC], F32))
            pA = ps.enter_context(nc.psum_tensor("pA", [128, TLOC], F32))
            pB = ps.enter_context(nc.psum_tensor("pB2", [128, TLOC], F32))
            pS = ps.enter_context(nc.psum_tensor("pS", [1, TLOC], F32))
            pQ = ps.enter_context(nc.psum_tensor("pQ", [1, TLOC], F32))

            @block.gpsimd
            def _(gp):
                C.dma("gc0", gp.dma_start(out=x1bf[:], in_=x1_scr.rearrange("(e p) t -> p e t", p=128)))
                C.dma("gc0", gp.dma_start(out=a2bf[:], in_=loraA2.rearrange("(e p) j -> p e j", p=128)))
                C.dma("gc0", gp.dma_start(out=b2bf[:], in_=loraB2.rearrange("p (o j) -> p o j", j=128)))
                C.dma("gc0", gp.dma_start(out=repb[:], in_=rep16[:]))
                C.dma("gc0", gp.dma_start(out=rwb[:], in_=router_WT.rearrange("(e p) j -> p e j", p=128)))
            GC0 = C.dma_total("gc0")

            @block.tensor
            def _(te):
                C.wait(te, "pe", GC0)
                last = None
                for ec in range(32):
                    te.ldweights(rwb[:, ec])
                    te.matmul(pA[0:8, 0:512], rwb[:, ec], x1bf[:, ec, 0:512], start=(ec == 0), stop=(ec == 31))
                    last = te.matmul(pA[0:8, 512:1024], rwb[:, ec], x1bf[:, ec, 512:1024], start=(ec == 0), stop=(ec == 31))
                M["lg_mm"] = C.bump("pe", last)
            @block.scalar
            def _(sc):
                C.wait(sc, "act", M["lg_mm"])
                M["lg_ev"] = C.bump("act", sc.copy(lgT[:], pA[0:8, :]))
            @block.tensor
            def _(te):
                C.wait(te, "pe", M["lg_ev"])
                last = None
                for ti in range(8):
                    last = te.transpose(pB[0:128, 8 * ti:8 * (ti + 1)], lgT[:, 128 * ti:128 * (ti + 1)], idt[0:8, 0:8])
                M["lg_tr"] = C.bump("pe", last)
            @block.scalar
            def _(sc):
                C.wait(sc, "act", M["lg_tr"])
                M["lg2"] = C.bump("act", sc.copy(lg[:].rearrange("p a b -> p (a b)"), pB[0:128, 0:64]))

            for ti in range(8):
                @block.vector
                def _(dv, ti=ti):
                    C.wait(dv, "dve", M["lg2"])
                    C.self_edge("dve", dv, dv.reduce_max(mrow[:, ti, 0:1], lg[:, ti], axis=AX.X))
                    M[f"mx_{ti}"] = C.bump("dve", dv.tensor_scalar(mrow[:, ti, 1:2], mrow[:, ti, 0:1], -1.0, None, OP.mult))
                @block.scalar
                def _(sc, ti=ti):
                    C.wait(sc, "act", M[f"mx_{ti}"])
                    M[f"e_{ti}"] = C.bump("act", sc.activation(esb[:, ti], lg[:, ti], AF.Exp, bias=mrow[:, ti, 1:2]))
                @block.vector
                def _(dv, ti=ti):
                    C.wait(dv, "dve", M[f"e_{ti}"])
                    C.self_edge("dve", dv, dv.max(top8[:, ti], esb[:, ti]))
                    C.self_edge("dve", dv, dv.tensor_tensor(mrow[:, ti, 2:3], top8[:, ti, 0:1], top8[:, ti, 1:2], OP.add))
                    C.self_edge("dve", dv, dv.reciprocal(mrow[:, ti, 3:4], mrow[:, ti, 2:3]))
                    C.self_edge("dve", dv, dv.tensor_scalar(wd[:, ti], esb[:, ti], top8[:, ti, 1:2], None, OP.is_ge))
                    C.self_edge("dve", dv, dv.tensor_tensor(wd[:, ti], wd[:, ti], esb[:, ti], OP.mult))
                    M[f"wd_{ti}"] = C.bump("dve", dv.tensor_scalar(wd[:, ti], wd[:, ti], mrow[:, ti, 3:4], None, OP.mult))

            @block.tensor
            def _(te):
                for ti in range(8):
                    C.wait(te, "pe", M[f"wd_{ti}"])
                last = None
                for ti in range(8):
                    last = te.transpose(pA[0:8, 128 * ti:128 * (ti + 1)], wd[:, ti], idt[:])
                M["wd_tr"] = C.bump("pe", last)
            @block.scalar
            def _(sc):
                C.wait(sc, "act", M["wd_tr"])
                M["wdT_ev"] = C.bump("act", sc.copy(wdT[:], pA[0:8, :]))
            @block.tensor
            def _(te):
                C.wait(te, "pe", M["wdT_ev"])
                te.ldweights(repb[:])
                te.matmul(pA[0:128, 0:512], repb[:], wdT[:, 0:512], start=True, stop=True)
                M["w16_mm"] = C.bump("pe", te.matmul(pA[0:128, 512:1024], repb[:], wdT[:, 512:1024], start=True, stop=True))
            @block.scalar
            def _(sc):
                C.wait(sc, "act", M["w16_mm"])
                M["w16_ev"] = C.bump("act", sc.copy(w16[:], pA[:]))
            @block.tensor
            def _(te):
                C.wait(te, "pe", M["w16_ev"])
                last = None
                for ec in range(32):
                    te.ldweights(a2bf[:, ec])
                    te.matmul(pB[:, 0:512], a2bf[:, ec], x1bf[:, ec, 0:512], start=(ec == 0), stop=(ec == 31))
                    last = te.matmul(pB[:, 512:1024], a2bf[:, ec], x1bf[:, ec, 512:1024], start=(ec == 0), stop=(ec == 31))
                M["mid_mm"] = C.bump("pe", last)
            @block.vector
            def _(dv):
                C.wait(dv, "dve", M["mid_mm"])
                M["midg"] = C.bump("dve", dv.tensor_tensor(midg[:], pB[:], w16[:], OP.mult))

            for oc in range(32):
                par = oc % 2
                @block.gpsimd
                def _(gp, oc=oc, par=par):
                    if oc >= 2:
                        C.wait(gp, "pool", M[f"bmm_{oc-2}"])
                    M[f"bwl_{oc}"] = C.dma(f"bwl{par}", gp.dma_start(
                        out=bwbuf[:, par],
                        in_=base_WT[:, 128 * oc:128 * (oc + 1)].rearrange("(e p) j -> p e j", p=128)))
                @block.sync
                def _(sp, oc=oc, par=par):
                    if oc >= 2:
                        C.wait(sp, "sp", M[f"resid0_{oc-2}"])
                    M[f"x1l_{oc}"] = C.dma(f"x1l{par}", sp.dma_start(out=x1f[:, par], in_=x1_scr[128 * oc:128 * (oc + 1), :]))
                @block.tensor
                def _(te, oc=oc, par=par):
                    C.wait(te, "pe", M[f"bwl_{oc}"])
                    C.wait(te, "pe", M["midg"])
                    if oc >= 1:
                        C.wait(te, "pe", M[f"bev_{oc-1}"])
                        C.wait(te, "pe", M[f"resid0_{oc-1}"])
                    last = None
                    for ec in range(32):
                        te.ldweights(bwbuf[:, par, ec])
                        te.matmul(pA[:, 0:512], bwbuf[:, par, ec], x1bf[:, ec, 0:512], start=(ec == 0), stop=(ec == 31))
                        last = te.matmul(pA[:, 512:1024], bwbuf[:, par, ec], x1bf[:, ec, 512:1024], start=(ec == 0), stop=(ec == 31))
                    M[f"bmm_{oc}"] = C.bump("pe", last)
                    te.ldweights(b2bf[:, oc])
                    te.matmul(pB[:, 0:512], b2bf[:, oc], midg[:, 0:512], start=True, stop=True)
                    M[f"imm_{oc}"] = C.bump("pe", te.matmul(pB[:, 512:1024], b2bf[:, oc], midg[:, 512:1024], start=True, stop=True))
                @block.scalar
                def _(sc, oc=oc):
                    C.wait(sc, "act", M[f"bmm_{oc}"])
                    if oc >= 1:
                        C.wait(sc, "act", M[f"resid0_{oc-1}"])
                    M[f"bev_{oc}"] = C.bump("act", sc.copy(fb[:], pA[:]))
                @block.vector
                def _(dv, oc=oc, par=par):
                    C.wait(dv, "dve", M[f"bev_{oc}"])
                    C.wait(dv, "dve", M[f"imm_{oc}"])
                    C.wait(dv, "dve", M[f"x1l_{oc}"])
                    if oc >= 2:
                        C.wait(dv, "dve", M[f"rst0_{oc-2}"])
                    dv.tensor_tensor(r_sb[:, par], fb[:], pB[:], OP.add)
                    M[f"resid0_{oc}"] = C.bump("dve", dv.tensor_tensor(r_sb[:, par], r_sb[:, par], x1f[:, par], OP.add))
                @block.scalar
                def _(sc, oc=oc, par=par):
                    C.wait(sc, "act", M[f"resid0_{oc}"])
                    if oc >= 1:
                        C.wait(sc, "act", M[f"stat0_{oc-1}"])
                    M[f"sq0_{oc}"] = C.bump("act", sc.activation(sq0[:], r_sb[:, par], AF.Square))
                @block.tensor
                def _(te, oc=oc, par=par):
                    C.wait(te, "pe", M[f"sq0_{oc}"])
                    te.matmul(pS[:, 0:512], onc[:, 0:1], r_sb[:, par, 0:512], start=(oc == 0), stop=(oc == 31))
                    te.matmul(pS[:, 512:1024], onc[:, 0:1], r_sb[:, par, 512:1024], start=(oc == 0), stop=(oc == 31))
                    te.matmul(pQ[:, 0:512], onc[:, 0:1], sq0[:, 0:512], start=(oc == 0), stop=(oc == 31))
                    M[f"stat0_{oc}"] = C.bump("pe", te.matmul(pQ[:, 512:1024], onc[:, 0:1], sq0[:, 512:1024], start=(oc == 0), stop=(oc == 31)))
                @block.sync
                def _(sp, oc=oc, par=par):
                    C.wait(sp, "sp", M[f"stat0_{oc}"])
                    M[f"rst0_{oc}"] = C.dma(f"rsb{par}", sp.dma_start(out=r_scr[128 * oc:128 * (oc + 1), :], in_=r_sb[:, par]))

            @block.vector
            def _(dv):
                C.wait(dv, "dve", M["stat0_31"])
                C.self_edge("dve", dv, dv.tensor_scalar(rrow0[:, 0], pS[:], 1.0 / H, None, OP.mult))
                C.self_edge("dve", dv, dv.tensor_scalar(rrow0[:, 1], pQ[:], 1.0 / H, None, OP.mult))
                C.self_edge("dve", dv, dv.tensor_tensor(rrow0[:, 2], rrow0[:, 0], rrow0[:, 0], OP.mult))
                C.self_edge("dve", dv, dv.tensor_tensor(rrow0[:, 3], rrow0[:, 1], rrow0[:, 2], OP.subtract))
                M["var0"] = C.bump("dve", dv.tensor_scalar(rrow0[:, 3], rrow0[:, 3], 1.0, 1e-5, OP.mult, OP.add))
            @block.scalar
            def _(sc):
                C.wait(sc, "act", M["var0"])
                M["sqrt0"] = C.bump("act", sc.activation(rrow0[:, 4], rrow0[:, 3], AF.Sqrt))
            @block.vector
            def _(dv):
                C.wait(dv, "dve", M["sqrt0"])
                M["rstd0"] = C.self_edge("dve", dv, dv.reciprocal(rrow0[:, 5], rrow0[:, 4]))
            @block.tensor
            def _(te):
                C.wait(te, "pe", M["rstd0"])
                te.matmul(pB[:, 0:512], onr[:], rrow0[:, 5, 0:512], start=True, stop=True)
                te.matmul(pB[:, 512:1024], onr[:], rrow0[:, 5, 512:1024], start=True, stop=True)
                te.matmul(pA[:, 0:512], onr[:], rrow0[:, 0, 0:512], start=True, stop=True)
                M["bc0"] = C.bump("pe", te.matmul(pA[:, 512:1024], onr[:], rrow0[:, 0, 512:1024], start=True, stop=True))
            for oc in range(32):
                par = oc % 2
                @block.sync
                def _(sp, oc=oc, par=par):
                    C.wait(sp, "sp", C.dma_total("rsb0"))
                    C.wait(sp, "sp", C.dma_total("rsb1"))
                    if oc >= 2:
                        C.wait(sp, "sp", M[f"norm0_{oc-2}"])
                    M[f"rrd0_{oc}"] = C.dma(f"rrd{par}", sp.dma_start(out=r_sb[:, par], in_=r_scr[128 * oc:128 * (oc + 1), :]))
                @block.vector
                def _(dv, oc=oc, par=par):
                    C.wait(dv, "dve", M[f"rrd0_{oc}"])
                    C.wait(dv, "dve", M["bc0"])
                    if oc >= 2:
                        C.wait(dv, "dve", M[f"xst0_{oc-2}"])
                    dv.tensor_tensor(xn0[:, par], r_sb[:, par], pA[:], OP.subtract)
                    M[f"norm0_{oc}"] = C.bump("dve", dv.tensor_tensor(xn0[:, par], xn0[:, par], pB[:], OP.mult))
                @block.sync
                def _(sp, oc=oc, par=par):
                    C.wait(sp, "sp", M[f"norm0_{oc}"])
                    M[f"xst0_{oc}"] = C.dma(f"xsc{par}", sp.dma_start(out=x2_scr[128 * oc:128 * (oc + 1), :], in_=xn0[:, par]))
            C.join(block)

        with contextlib.ExitStack() as ps:
            x2bf = ps.enter_context(nc.sbuf_tensor("x2bf", [128, 32, TLOC], BF16))
            wqb = ps.enter_context(nc.sbuf_tensor("wqb", [128, 2, 32, 128], BF16))
            evq = ps.enter_context(nc.sbuf_tensor("evq", [128, 2, TLOC], BF16))
            pq = ps.enter_context(nc.psum_tensor("pq", [128, TLOC], F32))

            @block.gpsimd
            def _(gp):
                C.dma("gcq", gp.dma_start(out=x2bf[:], in_=x2_scr.rearrange("(e p) t -> p e t", p=128)))
            GCQ = C.dma_total("gcq")
            for fc in range(34):
                par = fc % 2
                @block.gpsimd
                def _(gp, fc=fc, par=par):
                    if fc >= 2:
                        C.wait(gp, "pool", M[f"qmm_{fc-2}"])
                    M[f"qwl_{fc}"] = C.dma(f"qwl{par}", gp.dma_start(
                        out=wqb[:, par],
                        in_=w_qkv[:, 128 * fc:128 * (fc + 1)].rearrange("(e p) j -> p e j", p=128)))
                @block.tensor
                def _(te, fc=fc, par=par):
                    C.wait(te, "pe", M[f"qwl_{fc}"])
                    if fc == 0:
                        C.wait(te, "pe", GCQ)
                    if fc >= 1:
                        C.wait(te, "pe", M[f"qev_{fc-1}"])
                    last = None
                    for ec in range(32):
                        te.ldweights(wqb[:, par, ec])
                        te.matmul(pq[:, 0:512], wqb[:, par, ec], x2bf[:, ec, 0:512], start=(ec == 0), stop=(ec == 31))
                        last = te.matmul(pq[:, 512:1024], wqb[:, par, ec], x2bf[:, ec, 512:1024], start=(ec == 0), stop=(ec == 31))
                    M[f"qmm_{fc}"] = C.bump("pe", last)
                @block.scalar
                def _(sc, fc=fc, par=par):
                    C.wait(sc, "act", M[f"qmm_{fc}"])
                    if fc >= 2:
                        C.wait(sc, "act", M[f"qst_{fc-2}"])
                    M[f"qev_{fc}"] = C.bump("act", sc.copy(evq[:, par], pq[:]))
                @block.sync
                def _(sp, fc=fc, par=par):
                    C.wait(sp, "sp", M[f"qev_{fc}"])
                    if fc < 32:
                        row = 512 * (fc % 8) + 128 * (fc // 8)
                        M[f"qst_{fc}"] = C.dma(f"qst{par}", sp.dma_start(out=q_in[row:row + 128, :], in_=evq[:, par]))
                    else:
                        M[f"qst_{fc}"] = C.dma(f"qst{par}", sp.dma_start(
                            out=kv_in[128 * (fc - 32):128 * (fc - 31), :], in_=evq[:, par]))

            @block.gpsimd
            def _(gp):
                C.wait(gp, "pool", C.dma_total("qst0"))
                C.wait(gp, "pool", C.dma_total("qst1"))
                M["cc_q"] = C.bump("cc", gp.collective_compute(
                    "AllToAll", OP.bypass, replica_groups=RG,
                    ins=[q_in.tensor.ap().opt()], outs=[q_out.tensor.ap().opt()]))
                M["cc_kv"] = C.bump("cc", gp.collective_compute(
                    "AllGather", OP.bypass, replica_groups=RG,
                    ins=[kv_in.tensor.ap().opt()], outs=[kv_out.tensor.ap().opt()]))
            C.join(block)

        with contextlib.ExitStack() as ps:
            NQT = 16
            SCL = 128 ** -0.5
            qsb = ps.enter_context(nc.sbuf_tensor("qsb", [128, 4, T], BF16))
            ksb = ps.enter_context(nc.sbuf_tensor("ksb", [128, 64, 128], BF16))
            vtb = ps.enter_context(nc.sbuf_tensor("vtb", [128, 64, 128], BF16))
            vsb = ps.enter_context(nc.sbuf_tensor("vsb", [128, 64, 128], BF16))
            kcb = ps.enter_context(nc.sbuf_tensor("kcb", [128, 64], F32))
            mkb = ps.enter_context(nc.sbuf_tensor("mkb", [128, 4, 512], F32))
            slps = ps.enter_context(nc.sbuf_tensor("slps", [128, 4], F32))
            qrc = ps.enter_context(nc.sbuf_tensor("qrc", [128, 4, 512], F32))
            o128 = ps.enter_context(nc.sbuf_tensor("o128", [128, 128], BF16))
            slpk2 = ps.enter_context(nc.sbuf_tensor("slpk2", [128, 2, 64], F32))
            qtm = ps.enter_context(nc.sbuf_tensor("qtm", [128, 1], F32))
            qbq = ps.enter_context(nc.sbuf_tensor("qbq", [128, 2, 512], F32))
            probs = ps.enter_context(nc.sbuf_tensor("probs", [128, 4, 512], BF16))
            asb = ps.enter_context(nc.sbuf_tensor("asb", [128, 2, 512], F32))
            ssb = ps.enter_context(nc.sbuf_tensor("ssb", [128, 2, 512], F32))
            rrec = ps.enter_context(nc.sbuf_tensor("rrec", [128, 2, 512], F32))
            ob = ps.enter_context(nc.sbuf_tensor("ob", [128, 2, 512], BF16))
            pqk = ps.enter_context(nc.psum_tensor("pqk", [128, 4, 512], F32))
            pattn = ps.enter_context(nc.psum_tensor("pattn", [128, 512], F32))
            pbcS = ps.enter_context(nc.psum_tensor("pbcS", [128, 512], F32))
            pv = ps.enter_context(nc.psum_tensor("pv", [128, 2, 1024], BF16))

            @block.sync
            def _(sp):
                for k in range(32):
                    ow, gg = owner(k), gidx(k)
                    for hh in range(4):
                        C.dma("aql", sp.dma_start(
                            out=qsb[:, hh, 256 * k:256 * (k + 1)],
                            in_=q_out[512 * ow + 128 * hh:512 * ow + 128 * (hh + 1), 256 * gg:256 * (gg + 1)]))
                    C.dma("aql", sp.dma_start(
                        out=ksb[:].rearrange("p kt j -> p (kt j)")[:, 256 * k:256 * (k + 1)],
                        in_=kv_out[256 * ow:256 * ow + 128, 256 * gg:256 * (gg + 1)]))
                    C.dma("aql", sp.dma_start(
                        out=vtb[:].rearrange("p kt j -> p (kt j)")[:, 256 * k:256 * (k + 1)],
                        in_=kv_out[256 * ow + 128:256 * ow + 256, 256 * gg:256 * (gg + 1)]))
                C.dma("cst3", sp.dma_start(out=kcb[:], in_=kcol[:]))
                C.dma("cst3", sp.dma_start(out=mkb[:], in_=maskc[:]))
                C.dma("cst3", sp.dma_start(out=slps[:], in_=slopes_c[:]))
                C.dma("cst3", sp.dma_start(out=qrc[:].rearrange("p a b -> p (a b)"), in_=qrefc[:].rearrange("p a b -> p (a b)")))
                C.dma("cst3", sp.dma_start(out=o128[:], in_=ones128b[:]))
            AQL = C.dma_total("aql")
            CST3 = C.dma_total("cst3")

            actV_base = C.ecnt.get("act", 0)
            @block.tensor
            def _(te):
                C.wait(te, "pe", AQL)
                C.wait(te, "pe", CST0)
                for kt in range(64):
                    par = kt % 2
                    if kt >= 2:
                        C.wait(te, "pe", ("E", "act", actV_base + kt - 1))
                    M[f"vtr_{kt}"] = C.bump("pe", te.transpose(pv[:, par, 0:128], vtb[:, kt], idb[:]))
            @block.scalar
            def _(sc):
                for kt in range(64):
                    par = kt % 2
                    C.wait(sc, "act", M[f"vtr_{kt}"])
                    M[f"vev_{kt}"] = C.bump("act", sc.copy(vsb[:, kt], pv[:, par, 0:128]))
            assert C.ecnt["act"] == actV_base + 64

            pqk_free = {}
            for hh in range(4):
                sph = hh % 2
                @block.vector
                def _(dv, hh=hh, sph=sph):
                    C.wait(dv, "dve", CST3)
                    C.self_edge("dve", dv, dv.tensor_scalar(
                        slpk2[:, sph], kcb[:], slps[:, hh:hh + 1], None, OP.mult))

                for qt in range(NQT):
                    q0 = 512 * qt
                    nk = q0 // 128 + 4
                    qp = (hh * NQT + qt) % 2
                    act_base = C.ecnt["act"]
                    pe_base = C.ecnt["pe"]
                    ops = []
                    for kti in range(nk + 4):
                        if kti < nk:
                            ops.append(("qk", kti))
                        if kti >= 4:
                            ops.append(("pv", kti - 4))
                    aqk_val = {kt: pe_base + ops.index(("qk", kt)) + 1 for kt in range(nk)}
                    apr_val = {kt: act_base + kt + 1 for kt in range(nk)}
                    aaev_val = act_base + nk + 1
                    stt_ms = {}
                    apv_ms = {}
                    prev_afin = M.get(f"afin_{qp}")

                    @block.vector
                    def _(dv, hh=hh, q0=q0, qp=qp, nk=nk, sph=sph, aqk_val=aqk_val,
                          aaev_val=aaev_val, stt_ms=stt_ms):
                        C.self_edge("dve", dv, dv.tensor_scalar(
                            qtm[:], slps[:, hh:hh + 1], float(1 - q0), None, OP.mult))
                        dv.tensor_scalar(qbq[:, qp], qrc[:, hh], qtm[:, 0:1], None, OP.add)
                        for kt in range(nk):
                            par = kt % 4
                            diag = kt - (nk - 4)
                            C.wait(dv, "dve", ("E", "pe", aqk_val[kt]))
                            last = dv.scalar_tensor_tensor(
                                pqk[:, par], pqk[:, par], slpk2[:, sph, kt:kt + 1], qbq[:, qp], OP.add, OP.add)
                            if diag >= 0:
                                last = dv.tensor_tensor(pqk[:, par], pqk[:, par], mkb[:, diag], OP.add)
                            stt_ms[kt] = C.bump("dve", last)
                        C.wait(dv, "dve", ("E", "pe", pe_base + 2 * nk))
                        M[f"sclamp_{qp}"] = C.bump("dve", dv.tensor_scalar(
                            ssb[:, qp], pbcS[:], 1e-35, None, OP.max))
                        C.self_edge("dve", dv, dv.reciprocal(rrec[:, qp], ssb[:, qp]))
                        C.wait(dv, "dve", ("E", "act", aaev_val))
                        if f"aost_{qp}" in M:
                            C.wait(dv, "dve", M[f"aost_{qp}"])
                        M[f"afin_{qp}"] = C.bump("dve", dv.tensor_tensor(
                            ob[:, qp], asb[:, qp], rrec[:, qp], OP.mult))

                    @block.tensor
                    def _(te, hh=hh, q0=q0, qp=qp, nk=nk, ops=ops, apr_val=apr_val, apv_ms=apv_ms):
                        for what, kt in ops:
                            if what == "qk":
                                par = kt % 4
                                if kt >= 4:
                                    C.wait(te, "pe", ("E", "act", apr_val[kt - 4]))
                                elif par in pqk_free:
                                    C.wait(te, "pe", pqk_free[par])
                                te.ldweights(ksb[:, kt])
                                C.bump("pe", te.matmul(
                                    pqk[:, par], ksb[:, kt], qsb[:, hh, q0:q0 + 512], start=True, stop=True))
                            else:
                                par = kt % 4
                                C.wait(te, "pe", ("E", "act", apr_val[kt]))
                                st = (kt == 0)
                                sto = (kt == nk - 1)
                                if st:
                                    if "aaev_last" in M:
                                        C.wait(te, "pe", M["aaev_last"])
                                    if "sclamp_last" in M:
                                        C.wait(te, "pe", M["sclamp_last"])
                                te.ldweights(vsb[:, kt])
                                te.matmul(pattn[:], vsb[:, kt], probs[:, par], start=st, stop=sto)
                                te.ldweights(o128[:])
                                apv_ms[kt] = C.bump("pe", te.matmul(
                                    pbcS[:], o128[:], probs[:, par], start=st, stop=sto))
                    assert C.ecnt["pe"] == pe_base + 2 * nk, (C.ecnt["pe"], pe_base, nk)
                    for i in range(4):
                        pqk_free[(nk - 1 - i) % 4] = ("E", "act", apr_val[nk - 1 - i])
                    M["sclamp_last"] = M[f"sclamp_{qp}"]

                    @block.scalar
                    def _(sc, qp=qp, nk=nk, stt_ms=stt_ms, apv_ms=apv_ms, prev_afin=prev_afin):
                        for kt in range(nk):
                            par = kt % 4
                            C.wait(sc, "act", stt_ms[kt])
                            if kt >= 4:
                                C.wait(sc, "act", apv_ms[kt - 4])
                            elif f"probs_free_{par}" in M:
                                C.wait(sc, "act", M[f"probs_free_{par}"])
                            C.bump("act", sc.activation(
                                probs[:, par], pqk[:, par], AF.Exp, scale=SCL))
                        C.wait(sc, "act", apv_ms[nk - 1])
                        if prev_afin is not None:
                            C.wait(sc, "act", prev_afin)
                        M["aaev_last"] = M[f"aaev_{qp}"] = C.bump("act", sc.copy(asb[:, qp], pattn[:]))
                    assert C.ecnt["act"] == aaev_val, (C.ecnt["act"], aaev_val)
                    for i in range(4):
                        M[f"probs_free_{(nk - 1 - i) % 4}"] = apv_ms[nk - 1 - i]

                    @block.sync
                    def _(sp, hh=hh, qt=qt, qp=qp):
                        C.wait(sp, "sp", M[f"afin_{qp}"])
                        for ci in range(2):
                            k = 2 * qt + ci
                            ow, gg = owner(k), gidx(k)
                            C.dma(f"aost{qp}", sp.dma_start(
                                out=at_in[512 * ow + 128 * hh:512 * ow + 128 * (hh + 1), 256 * gg:256 * (gg + 1)],
                                in_=ob[:, qp, 256 * ci:256 * (ci + 1)]))
                        M[f"aost_{qp}"] = C.dma_total(f"aost{qp}")

            @block.gpsimd
            def _(gp):
                C.wait(gp, "pool", C.dma_total("aost0"))
                C.wait(gp, "pool", C.dma_total("aost1"))
                M["cc_at"] = C.bump("cc", gp.collective_compute(
                    "AllToAll", OP.bypass, replica_groups=RG,
                    ins=[at_in.tensor.ap().opt()], outs=[at_out.tensor.ap().opt()]))
            C.join(block)

        with contextlib.ExitStack() as ps:
            abf = ps.enter_context(nc.sbuf_tensor("abf", [128, 32, TLOC], BF16))
            wbufW = ps.enter_context(nc.sbuf_tensor("wbufW", [128, 2, 32, 128], BF16))
            x2t = ps.enter_context(nc.sbuf_tensor("x2tW", [128, 2, TLOC], F32))
            r_sb = ps.enter_context(nc.sbuf_tensor("r_sbW", [128, 2, TLOC], F32))
            sqW = ps.enter_context(nc.sbuf_tensor("sqW", [128, TLOC], F32))
            rrowW = ps.enter_context(nc.sbuf_tensor("rrowW", [1, 6, TLOC], F32))
            xnW = ps.enter_context(nc.sbuf_tensor("xnW", [128, 2, TLOC], F32))
            pW = ps.enter_context(nc.psum_tensor("pWW", [128, TLOC], F32))
            psum_s = ps.enter_context(nc.psum_tensor("pSW", [1, TLOC], F32))
            psum_q = ps.enter_context(nc.psum_tensor("pQW", [1, TLOC], F32))
            pbcW = ps.enter_context(nc.psum_tensor("pbcW", [128, TLOC], F32))

            @block.sync
            def _(sp):
                for g in range(32):
                    C.dma("awl", sp.dma_start(
                        out=abf[:, g],
                        in_=at_out[512 * (g % 8) + 128 * (g // 8):512 * (g % 8) + 128 * (g // 8) + 128, :]))
            AWL = C.dma_total("awl")

            for hc in range(32):
                par = hc % 2
                @block.gpsimd
                def _(gp, hc=hc, par=par):
                    if hc >= 2:
                        C.wait(gp, "pool", M[f"wmm_{hc-2}"])
                    M[f"wwl_{hc}"] = C.dma(f"wwl{par}", gp.dma_start(
                        out=wbufW[:, par],
                        in_=w_c[:, 128 * hc:128 * (hc + 1)].rearrange("(e p) j -> p e j", p=128)))
                @block.sync
                def _(sp, hc=hc, par=par):
                    if hc >= 2:
                        C.wait(sp, "sp", M[f"residW_{hc-2}"])
                    M[f"x2l_{hc}"] = C.dma(f"x2l{par}", sp.dma_start(out=x2t[:, par], in_=x2_scr[128 * hc:128 * (hc + 1), :]))
                @block.tensor
                def _(te, hc=hc, par=par):
                    C.wait(te, "pe", M[f"wwl_{hc}"])
                    if hc == 0:
                        C.wait(te, "pe", AWL)
                    if hc >= 1:
                        C.wait(te, "pe", M[f"residW_{hc-1}"])
                    last = None
                    for ec in range(32):
                        te.ldweights(wbufW[:, par, ec])
                        te.matmul(pW[:, 0:512], wbufW[:, par, ec], abf[:, ec, 0:512], start=(ec == 0), stop=(ec == 31))
                        last = te.matmul(pW[:, 512:1024], wbufW[:, par, ec], abf[:, ec, 512:1024], start=(ec == 0), stop=(ec == 31))
                    M[f"wmm_{hc}"] = C.bump("pe", last)
                @block.vector
                def _(dv, hc=hc, par=par):
                    C.wait(dv, "dve", M[f"wmm_{hc}"])
                    C.wait(dv, "dve", M[f"x2l_{hc}"])
                    if hc >= 2:
                        C.wait(dv, "dve", M[f"rstW_{hc-2}"])
                    M[f"residW_{hc}"] = C.bump("dve", dv.tensor_tensor(r_sb[:, par], pW[:], x2t[:, par], OP.add))
                @block.scalar
                def _(sc, hc=hc, par=par):
                    C.wait(sc, "act", M[f"residW_{hc}"])
                    if hc >= 1:
                        C.wait(sc, "act", M[f"statW_{hc-1}"])
                    M[f"sqW_{hc}"] = C.bump("act", sc.activation(sqW[:], r_sb[:, par], AF.Square))
                @block.tensor
                def _(te, hc=hc, par=par):
                    C.wait(te, "pe", M[f"sqW_{hc}"])
                    te.matmul(psum_s[:, 0:512], onc[:, 0:1], r_sb[:, par, 0:512], start=(hc == 0), stop=(hc == 31))
                    te.matmul(psum_s[:, 512:1024], onc[:, 0:1], r_sb[:, par, 512:1024], start=(hc == 0), stop=(hc == 31))
                    te.matmul(psum_q[:, 0:512], onc[:, 0:1], sqW[:, 0:512], start=(hc == 0), stop=(hc == 31))
                    M[f"statW_{hc}"] = C.bump("pe", te.matmul(psum_q[:, 512:1024], onc[:, 0:1], sqW[:, 512:1024], start=(hc == 0), stop=(hc == 31)))
                @block.sync
                def _(sp, hc=hc, par=par):
                    C.wait(sp, "sp", M[f"statW_{hc}"])
                    M[f"rstW_{hc}"] = C.dma(f"rsw{par}", sp.dma_start(out=r_scr[128 * hc:128 * (hc + 1), :], in_=r_sb[:, par]))

            @block.vector
            def _(dv):
                C.wait(dv, "dve", M["statW_31"])
                C.self_edge("dve", dv, dv.tensor_scalar(rrowW[:, 0], psum_s[:], 1.0 / H, None, OP.mult))
                C.self_edge("dve", dv, dv.tensor_scalar(rrowW[:, 1], psum_q[:], 1.0 / H, None, OP.mult))
                C.self_edge("dve", dv, dv.tensor_tensor(rrowW[:, 2], rrowW[:, 0], rrowW[:, 0], OP.mult))
                C.self_edge("dve", dv, dv.tensor_tensor(rrowW[:, 3], rrowW[:, 1], rrowW[:, 2], OP.subtract))
                M["varW"] = C.bump("dve", dv.tensor_scalar(rrowW[:, 3], rrowW[:, 3], 1.0, 1e-5, OP.mult, OP.add))
            @block.scalar
            def _(sc):
                C.wait(sc, "act", M["varW"])
                M["sqrtW"] = C.bump("act", sc.activation(rrowW[:, 4], rrowW[:, 3], AF.Sqrt))
            @block.vector
            def _(dv):
                C.wait(dv, "dve", M["sqrtW"])
                M["rstdW"] = C.self_edge("dve", dv, dv.reciprocal(rrowW[:, 5], rrowW[:, 4]))
            @block.tensor
            def _(te):
                C.wait(te, "pe", M["rstdW"])
                te.matmul(pbcW[:, 0:512], onr[:], rrowW[:, 5, 0:512], start=True, stop=True)
                te.matmul(pbcW[:, 512:1024], onr[:], rrowW[:, 5, 512:1024], start=True, stop=True)
                te.matmul(pW[:, 0:512], onr[:], rrowW[:, 0, 0:512], start=True, stop=True)
                M["bcW"] = C.bump("pe", te.matmul(pW[:, 512:1024], onr[:], rrowW[:, 0, 512:1024], start=True, stop=True))
            for hc in range(32):
                par = hc % 2
                @block.sync
                def _(sp, hc=hc, par=par):
                    C.wait(sp, "sp", C.dma_total("rsw0"))
                    C.wait(sp, "sp", C.dma_total("rsw1"))
                    if hc >= 2:
                        C.wait(sp, "sp", M[f"normW_{hc-2}"])
                    M[f"rrdW_{hc}"] = C.dma(f"rdw{par}", sp.dma_start(out=r_sb[:, par], in_=r_scr[128 * hc:128 * (hc + 1), :]))
                @block.vector
                def _(dv, hc=hc, par=par):
                    C.wait(dv, "dve", M[f"rrdW_{hc}"])
                    C.wait(dv, "dve", M["bcW"])
                    if hc >= 2:
                        C.wait(dv, "dve", M[f"xstW_{hc-2}"])
                    dv.tensor_tensor(xnW[:, par], r_sb[:, par], pW[:], OP.subtract)
                    M[f"normW_{hc}"] = C.bump("dve", dv.tensor_tensor(xnW[:, par], xnW[:, par], pbcW[:], OP.mult))
                @block.sync
                def _(sp, hc=hc, par=par):
                    C.wait(sp, "sp", M[f"normW_{hc}"])
                    M[f"xstW_{hc}"] = C.dma(f"xsw{par}", sp.dma_start(out=x3_scr[128 * hc:128 * (hc + 1), :], in_=xnW[:, par]))
            C.join(block)

        with contextlib.ExitStack() as ps:
            x3bf = ps.enter_context(nc.sbuf_tensor("x3bf", [128, 32, TLOC], BF16))
            rwb1 = ps.enter_context(nc.sbuf_tensor("rwb1", [128, 32, 64], BF16))
            rbb = ps.enter_context(nc.sbuf_tensor("rbb", [64, 1], F32))
            ebb = ps.enter_context(nc.sbuf_tensor("ebb", [64, 32, 128], BF16))
            lgTF = ps.enter_context(nc.sbuf_tensor("lgTF", [64, TLOC], F32))
            lgF = ps.enter_context(nc.sbuf_tensor("lgF", [128, 8, 64], F32))
            esbF = ps.enter_context(nc.sbuf_tensor("esbF", [128, 8, 64], F32))
            wdd = ps.enter_context(nc.sbuf_tensor("wdd", [128, 8, 65], F32))
            top8F = ps.enter_context(nc.sbuf_tensor("top8F", [128, 8, 8], F32))
            mrowF = ps.enter_context(nc.sbuf_tensor("mrowF", [128, 8, 4], F32))
            wdTF = ps.enter_context(nc.sbuf_tensor("wdTF", [65, TLOC], BF16))
            wsrow = ps.enter_context(nc.sbuf_tensor("wsrow", [1, TLOC], BF16))
            msbF = ps.enter_context(nc.sbuf_tensor("msbF", [128, TLOC], F32))
            x3f = ps.enter_context(nc.sbuf_tensor("x3fF", [128, 2, TLOC], F32))
            r_sb = ps.enter_context(nc.sbuf_tensor("r_sbF", [128, 2, TLOC], F32))
            sqF = ps.enter_context(nc.sbuf_tensor("sqF", [128, TLOC], F32))
            rrowF = ps.enter_context(nc.sbuf_tensor("rrowF", [1, 6, TLOC], F32))
            xnF = ps.enter_context(nc.sbuf_tensor("xnF", [128, 2, TLOC], F32))
            pA = ps.enter_context(nc.psum_tensor("pAF", [128, TLOC], F32))
            pB = ps.enter_context(nc.psum_tensor("pBF", [128, TLOC], F32))
            pS = ps.enter_context(nc.psum_tensor("pSF", [1, TLOC], F32))
            pQ = ps.enter_context(nc.psum_tensor("pQF", [1, TLOC], F32))

            @block.gpsimd
            def _(gp):
                C.dma("gcf", gp.dma_start(out=x3bf[:], in_=x3_scr.rearrange("(e p) t -> p e t", p=128)))
                C.dma("gcf", gp.dma_start(out=rwb1[:], in_=rwT1.rearrange("(e p) j -> p e j", p=128)))
                C.dma("gcf", gp.dma_start(out=ebb[:], in_=ebias.rearrange("p (o j) -> p o j", j=128)))
            @block.sync
            def _(sp):
                C.dma("cst4", sp.dma_start(out=rbb[:], in_=rb1[:]))
            GCF = C.dma_total("gcf")
            CST4 = C.dma_total("cst4")

            @block.tensor
            def _(te):
                C.wait(te, "pe", GCF)
                last = None
                for ec in range(32):
                    te.ldweights(rwb1[:, ec])
                    te.matmul(pA[0:64, 0:512], rwb1[:, ec], x3bf[:, ec, 0:512], start=(ec == 0), stop=(ec == 31))
                    last = te.matmul(pA[0:64, 512:1024], rwb1[:, ec], x3bf[:, ec, 512:1024], start=(ec == 0), stop=(ec == 31))
                M["flg_mm"] = C.bump("pe", last)
            @block.vector
            def _(dv):
                C.wait(dv, "dve", M["flg_mm"])
                C.wait(dv, "dve", CST4)
                M["flg_ev"] = C.bump("dve", dv.tensor_scalar(lgTF[:], pA[0:64, :], rbb[:, 0:1], None, OP.add))
            @block.tensor
            def _(te):
                C.wait(te, "pe", M["flg_ev"])
                last = None
                for ti in range(8):
                    last = te.transpose(pB[0:128, 64 * ti:64 * (ti + 1)], lgTF[:, 128 * ti:128 * (ti + 1)], idt[0:64, 0:64])
                M["flg_tr"] = C.bump("pe", last)
            @block.scalar
            def _(sc):
                C.wait(sc, "act", M["flg_tr"])
                M["flg2"] = C.bump("act", sc.copy(lgF[:].rearrange("p a b -> p (a b)"), pB[0:128, 0:512]))

            for ti in range(8):
                @block.vector
                def _(dv, ti=ti):
                    C.wait(dv, "dve", M["flg2"])
                    C.self_edge("dve", dv, dv.reduce_max(mrowF[:, ti, 0:1], lgF[:, ti], axis=AX.X))
                    M[f"fmx_{ti}"] = C.bump("dve", dv.tensor_scalar(mrowF[:, ti, 1:2], mrowF[:, ti, 0:1], -1.0, None, OP.mult))
                @block.scalar
                def _(sc, ti=ti):
                    C.wait(sc, "act", M[f"fmx_{ti}"])
                    M[f"fe_{ti}"] = C.bump("act", sc.activation(esbF[:, ti], lgF[:, ti], AF.Exp, bias=mrowF[:, ti, 1:2]))
                @block.vector
                def _(dv, ti=ti):
                    C.wait(dv, "dve", M[f"fe_{ti}"])
                    C.self_edge("dve", dv, dv.reduce_sum(mrowF[:, ti, 2:3], esbF[:, ti], axis=AX.X))
                    C.self_edge("dve", dv, dv.reciprocal(mrowF[:, ti, 3:4], mrowF[:, ti, 2:3]))
                    C.self_edge("dve", dv, dv.tensor_scalar(esbF[:, ti], esbF[:, ti], mrowF[:, ti, 3:4], None, OP.mult))
                    C.self_edge("dve", dv, dv.max(top8F[:, ti], esbF[:, ti]))
                    C.self_edge("dve", dv, dv.reduce_sum(wdd[:, ti, 64:65], top8F[:, ti], axis=AX.X))
                    C.self_edge("dve", dv, dv.tensor_scalar(wdd[:, ti, 0:64], esbF[:, ti], top8F[:, ti, 7:8], None, OP.is_ge))
                    M[f"fwd_{ti}"] = C.bump("dve", dv.tensor_tensor(wdd[:, ti, 0:64], wdd[:, ti, 0:64], esbF[:, ti], OP.mult))

            @block.tensor
            def _(te):
                for ti in range(8):
                    C.wait(te, "pe", M[f"fwd_{ti}"])
                last = None
                for ti in range(8):
                    last = te.transpose(pA[0:65, 128 * ti:128 * (ti + 1)], wdd[:, ti], idt[:])
                M["fwd_tr"] = C.bump("pe", last)
            @block.scalar
            def _(sc):
                C.wait(sc, "act", M["fwd_tr"])
                sc.copy(wdTF[:], pA[0:65, :])
                M["fwdT_ev"] = C.bump("act", sc.copy(wsrow[:], pA[64:65, :]))

            for hc in range(32):
                par = hc % 2
                @block.sync
                def _(sp, hc=hc, par=par):
                    if hc >= 2:
                        C.wait(sp, "sp", M[f"residF_{hc-2}"])
                    M[f"x3l_{hc}"] = C.dma(f"x3l{par}", sp.dma_start(out=x3f[:, par], in_=x3_scr[128 * hc:128 * (hc + 1), :]))
                @block.tensor
                def _(te, hc=hc, par=par):
                    C.wait(te, "pe", M["fwdT_ev"])
                    if hc >= 1:
                        C.wait(te, "pe", M[f"fmev_{hc-1}"])
                        C.wait(te, "pe", M[f"residF_{hc-1}"])
                    te.ldweights(ebb[:, hc])
                    te.matmul(pB[:, 0:512], ebb[:, hc], wdTF[0:64, 0:512], start=True, stop=True)
                    M[f"fmix_{hc}"] = C.bump("pe", te.matmul(pB[:, 512:1024], ebb[:, hc], wdTF[0:64, 512:1024], start=True, stop=True))
                    te.ldweights(onrb[:])
                    te.matmul(pA[:, 0:512], onrb[:], wsrow[:, 0:512], start=True, stop=True)
                    M[f"fws_{hc}"] = C.bump("pe", te.matmul(pA[:, 512:1024], onrb[:], wsrow[:, 512:1024], start=True, stop=True))
                @block.scalar
                def _(sc, hc=hc):
                    C.wait(sc, "act", M[f"fmix_{hc}"])
                    if hc >= 1:
                        C.wait(sc, "act", M[f"residF_{hc-1}"])
                    M[f"fmev_{hc}"] = C.bump("act", sc.copy(msbF[:], pB[:]))
                @block.vector
                def _(dv, hc=hc, par=par):
                    C.wait(dv, "dve", M[f"fws_{hc}"])
                    C.wait(dv, "dve", M[f"fmev_{hc}"])
                    C.wait(dv, "dve", M[f"x3l_{hc}"])
                    if hc >= 2:
                        C.wait(dv, "dve", M[f"rstF_{hc-2}"])
                    dv.tensor_tensor(r_sb[:, par], x3f[:, par], pA[:], OP.mult)
                    dv.tensor_tensor(r_sb[:, par], r_sb[:, par], msbF[:], OP.add)
                    M[f"residF_{hc}"] = C.bump("dve", dv.tensor_tensor(r_sb[:, par], r_sb[:, par], x3f[:, par], OP.add))
                @block.scalar
                def _(sc, hc=hc, par=par):
                    C.wait(sc, "act", M[f"residF_{hc}"])
                    if hc >= 1:
                        C.wait(sc, "act", M[f"statF_{hc-1}"])
                    M[f"sqF_{hc}"] = C.bump("act", sc.activation(sqF[:], r_sb[:, par], AF.Square))
                @block.tensor
                def _(te, hc=hc, par=par):
                    C.wait(te, "pe", M[f"sqF_{hc}"])
                    te.matmul(pS[:, 0:512], onc[:, 0:1], r_sb[:, par, 0:512], start=(hc == 0), stop=(hc == 31))
                    te.matmul(pS[:, 512:1024], onc[:, 0:1], r_sb[:, par, 512:1024], start=(hc == 0), stop=(hc == 31))
                    te.matmul(pQ[:, 0:512], onc[:, 0:1], sqF[:, 0:512], start=(hc == 0), stop=(hc == 31))
                    M[f"statF_{hc}"] = C.bump("pe", te.matmul(pQ[:, 512:1024], onc[:, 0:1], sqF[:, 512:1024], start=(hc == 0), stop=(hc == 31)))
                @block.sync
                def _(sp, hc=hc, par=par):
                    C.wait(sp, "sp", M[f"statF_{hc}"])
                    M[f"rstF_{hc}"] = C.dma(f"rsf{par}", sp.dma_start(out=r_scr[128 * hc:128 * (hc + 1), :], in_=r_sb[:, par]))

            @block.vector
            def _(dv):
                C.wait(dv, "dve", M["statF_31"])
                C.self_edge("dve", dv, dv.tensor_scalar(rrowF[:, 0], pS[:], 1.0 / H, None, OP.mult))
                C.self_edge("dve", dv, dv.tensor_scalar(rrowF[:, 1], pQ[:], 1.0 / H, None, OP.mult))
                C.self_edge("dve", dv, dv.tensor_tensor(rrowF[:, 2], rrowF[:, 0], rrowF[:, 0], OP.mult))
                C.self_edge("dve", dv, dv.tensor_tensor(rrowF[:, 3], rrowF[:, 1], rrowF[:, 2], OP.subtract))
                M["varF"] = C.bump("dve", dv.tensor_scalar(rrowF[:, 3], rrowF[:, 3], 1.0, 1e-5, OP.mult, OP.add))
            @block.scalar
            def _(sc):
                C.wait(sc, "act", M["varF"])
                M["sqrtF"] = C.bump("act", sc.activation(rrowF[:, 4], rrowF[:, 3], AF.Sqrt))
            @block.vector
            def _(dv):
                C.wait(dv, "dve", M["sqrtF"])
                M["rstdF"] = C.self_edge("dve", dv, dv.reciprocal(rrowF[:, 5], rrowF[:, 4]))
            @block.tensor
            def _(te):
                C.wait(te, "pe", M["rstdF"])
                te.matmul(pB[:, 0:512], onr[:], rrowF[:, 5, 0:512], start=True, stop=True)
                te.matmul(pB[:, 512:1024], onr[:], rrowF[:, 5, 512:1024], start=True, stop=True)
                te.matmul(pA[:, 0:512], onr[:], rrowF[:, 0, 0:512], start=True, stop=True)
                M["bcF"] = C.bump("pe", te.matmul(pA[:, 512:1024], onr[:], rrowF[:, 0, 512:1024], start=True, stop=True))
            for hc in range(32):
                par = hc % 2
                @block.sync
                def _(sp, hc=hc, par=par):
                    C.wait(sp, "sp", C.dma_total("rsf0"))
                    C.wait(sp, "sp", C.dma_total("rsf1"))
                    if hc >= 2:
                        C.wait(sp, "sp", M[f"normF_{hc-2}"])
                    M[f"rrdF_{hc}"] = C.dma(f"rdf{par}", sp.dma_start(out=r_sb[:, par], in_=r_scr[128 * hc:128 * (hc + 1), :]))
                @block.vector
                def _(dv, hc=hc, par=par):
                    C.wait(dv, "dve", M[f"rrdF_{hc}"])
                    C.wait(dv, "dve", M["bcF"])
                    if hc >= 2:
                        C.wait(dv, "dve", M[f"xstF_{hc-2}"])
                    dv.tensor_tensor(xnF[:, par], r_sb[:, par], pA[:], OP.subtract)
                    M[f"normF_{hc}"] = C.bump("dve", dv.tensor_tensor(xnF[:, par], xnF[:, par], pB[:], OP.mult))
                @block.sync
                def _(sp, hc=hc, par=par):
                    C.wait(sp, "sp", M[f"normF_{hc}"])
                    M[f"xstF_{hc}"] = C.dma(f"xff{par}", sp.dma_start(out=x4_out[128 * hc:128 * (hc + 1), :], in_=xnF[:, par]))

            @block.gpsimd
            def _(gp):
                C.wait(gp, "pool", C.dma_total("xff0"))
                C.wait(gp, "pool", C.dma_total("xff1"))
    return nc


def stage_fused(d):
    x0 = np.ascontiguousarray(d["L0_attn_hidden_states"].reshape(T, H).T)
    esl = np.zeros((NST, NST * 128), np.float32)
    for n in range(NST):
        esl[n, 128 * n:128 * (n + 1)] = 1.0
    rep = np.zeros((8, 128), np.float32)
    for e in range(8):
        rep[e, 16 * e:16 * (e + 1)] = 2.0
    A2 = d["L0_ffn_lora_A"].transpose(1, 0, 2).reshape(H, 128)
    maskc = np.where((128 * np.arange(4)[None, :, None] + np.arange(128)[:, None, None]) > np.arange(512)[None, None, :],
                     np.float32(-1e30), np.float32(0.0)).astype(np.float32)
    kcolv = (128 * np.arange(64)[None, :] + np.arange(128)[:, None]).astype(np.float32)
    slopes = np.asarray(alibi_slopes_py(32), np.float32)
    WdtT = np.ascontiguousarray(d["L0_attn_Wdt"].T)
    A = d["L0_attn_A"]
    dtbv = d["L0_attn_dt_bias"]
    Dpv = d["L0_attn_D_param"]
    common = dict(
        W_inT=np.ascontiguousarray(d["L0_attn_W_in"].T),
        WxT=np.ascontiguousarray(d["L0_attn_Wx"].T),
        conv_w=np.ascontiguousarray(d["L0_attn_conv_w"]),
        conv_b2=np.ascontiguousarray(d["L0_attn_conv_b"].reshape(64, 128).T),
        dtlnw2=np.ascontiguousarray(d["L0_attn_dt_ln_w"].reshape(2, 128).T),
        b_ln_w=np.ascontiguousarray(d["L0_attn_b_ln_w"]),
        c_ln_w=np.ascontiguousarray(d["L0_attn_c_ln_w"]),
        esel=esl.astype(bf16),
        W_outT=np.ascontiguousarray(d["L0_attn_W_out"].T),
        base_WT=np.ascontiguousarray(d["L0_ffn_base_W"].T),
        router_WT=np.ascontiguousarray(d["L0_ffn_router_W"].T),
        loraA2=np.ascontiguousarray(A2),
        loraB2=np.ascontiguousarray(d["L0_ffn_lora_B"].reshape(128, H)),
        rep16=rep,
        w_qkv=np.ascontiguousarray(d["L1_attn_w_qkv"]),
        w_c=np.ascontiguousarray(d["L1_attn_w_c"]),
        rwT1=np.ascontiguousarray(d["L1_ffn_router_weight"].T),
        rb1=np.ascontiguousarray(d["L1_ffn_router_bias"].reshape(64, 1)),
        ebias=np.ascontiguousarray(d["L1_ffn_expert_bias"]),
        ident=np.eye(128, dtype=np.float32),
        identb=np.eye(128, dtype=np.float32).astype(bf16),
        ones_col=np.ones((128, 1), np.float32),
        ones_row=np.ones((1, 128), np.float32),
        onesb=np.ones((128, 1), np.float32).astype(bf16),
        onesrb=np.ones((1, 128), np.float32).astype(bf16),
        kcol=kcolv,
        maskc=maskc,
        ones128b=np.ones((128, 128), np.float32).astype(bf16),
    )
    WS_ATTN = [64, 256, 512, 512]
    SCL_ATTN = 128 ** -0.5
    maps = []
    for c in range(NCORE):
        cols, pres = [], []
        for k in groups_of(c):
            cols.append(x0[:, G * k:G * (k + 1)])
            pres.append(np.zeros((H, 3), np.float32) if k == 0 else x0[:, G * k - 3:G * k])
        rows = slice(1024 * c, 1024 * (c + 1))
        Ac = A[rows].reshape(8, 128, NST).transpose(1, 0, 2).reshape(128, 8 * NST)
        heads = [c, 8 + c, 16 + c, 24 + c]
        maps.append(dict(
            common,
            x0T=np.ascontiguousarray(np.concatenate(cols, axis=1)),
            x0Tpre=np.ascontiguousarray(np.concatenate(pres + [np.zeros((H, 4), np.float32)], axis=1)),
            wdtT_c=np.ascontiguousarray(WdtT[:, rows]),
            A_c=np.ascontiguousarray(Ac.astype(np.float32)),
            dtb_c=np.ascontiguousarray(dtbv[rows].reshape(8, 128).T.astype(np.float32)),
            Dp_c=np.ascontiguousarray(Dpv[rows].reshape(8, 128).T.astype(np.float32)),
            slopes_c=np.ascontiguousarray(np.broadcast_to(
                (slopes[heads] / SCL_ATTN)[None, :], (128, 4)).astype(np.float32)),
            qrefc=np.ascontiguousarray(np.broadcast_to(np.stack([
                -(slopes[heads[s]] / SCL_ATTN) * ((np.arange(512) // WS_ATTN[s] + 1) * WS_ATTN[s]).astype(np.float64)
                for s in range(4)], axis=0).astype(np.float32)[None, :, :], (128, 4, 512))),
        ))
    return maps


def gather_tokens(per_core):
    rows = per_core[0].shape[0]
    out = np.empty((rows, T), per_core[0].dtype)
    for c in range(NCORE):
        for g, k in enumerate(groups_of(c)):
            out[:, G * k:G * (k + 1)] = per_core[c][:, G * g:G * (g + 1)]
    return out
_PROGS = {}

def kernel(**inputs):
    d = {k: np.ascontiguousarray(np.asarray(v)) for k, v in inputs.items()}
    if "nc" not in _PROGS:
        with allow_ici_all_to_all():
            _PROGS["nc"] = build_fused()
    nc = _PROGS["nc"]
    maps = stage_fused(d)
    res = run_bass_kernel_spmd(nc, maps, core_ids=list(range(NCORE)))
    x4_full = gather_tokens([res.results[c]["x4_out"] for c in range(NCORE)])
    return np.ascontiguousarray(x4_full.T).reshape(1, T, H).astype(np.float32)
